# version 07
# baseline (speedup 1.0000x reference)
import numpy as np
import ml_dtypes
from contextlib import ExitStack

import concourse.bass as bass
import concourse.mybir as mybir
from concourse.bass_utils import run_bass_kernel_spmd

F32 = mybir.dt.float32
BF16 = mybir.dt.bfloat16
I32 = mybir.dt.int32
AF = mybir.ActivationFunctionType
ALU = mybir.AluOpType
AX = mybir.AxisListType

B, T, D = 4, 2048, 2048
B_DEBUG = False
B_DEBUG_QT = 7
C_DEBUG = False
C_DEBUG_LITE = False
C_SCRATCH_INTERNAL = True
NCORES = 8
RPC = (B * T) // NCORES
NT = RPC // 128
DK = D // 128
INNER = 8192
EPS = 1e-5
LN2 = float(np.log(2.0))

CONST_VALS = sorted(set([EPS, -0.5, 0.5, -1.0, 1.0, 2.0, LN2, -LN2, 1.0 / LN2, 1e-38,
                         0.5 / LN2, 127.0, 1.0 / 127.0, 1e-30, -1e30, 1.0 / 448.0,
                         20.0, 14.0, -16.0, 23.0 * LN2, -23.0 * LN2, 1.0 / 2048.0, 1.0 / 8192.0,
                         1.0 / (INNER * D), 1.0 / INNER]))

ENGINES = ("sync", "vector", "scalar", "tensor", "gpsimd")


class SE:

    def __init__(self, nc, stack):
        self.nc = nc
        self.stack = stack
        self.ops = {e: [] for e in ENGINES}
        self.count = {e: 0 for e in ENGINES}
        self.dmacount = {e: 0 for e in ENGINES}
        self.esem = {e: stack.enter_context(nc.semaphore(f"es_{e}"))
                     for e in ENGINES}
        self.dsem = {e: stack.enter_context(nc.semaphore(f"ds_{e}"))
                     for e in ENGINES}
        self.last = None
        self.extra_sem = {}

    def _waits(self, eng):
        w = []
        if self.last is None:
            return w
        kind = self.last[0]
        if kind == "dma":
            _, deng, dcount = self.last
            w.append((self.dsem[deng], 16 * dcount))
        else:
            _, peng, pcount = self.last
            w.append((self.esem[peng], pcount))
        return w

    def _dep_waits(self, eng, deps):
        w = []
        if self.count[eng] > 0:
            w.append((self.esem[eng], self.count[eng]))
        seen = {}
        for d in deps:
            if d is None:
                continue
            kind = d[0]
            if kind == "dma":
                _, deng, dcount = d
                key = ("d", deng)
                seen[key] = max(seen.get(key, 0), 16 * dcount)
            else:
                _, peng, pcount = d
                if peng == eng:
                    continue
                key = ("e", peng)
                seen[key] = max(seen.get(key, 0), pcount)
        for (k, e2), v in seen.items():
            w.append(((self.dsem if k == "d" else self.esem)[e2], v))
        return w

    def op(self, eng, fn, deps=None, close=False):
        w = self._dep_waits(eng, deps) if deps is not None else self._waits(eng)
        self.count[eng] += 1
        n = self.count[eng]
        sem = self.esem[eng]

        def run(handle, w=w, fn=fn, sem=sem, n=n):
            for s, v in w:
                handle.wait_ge(s, v)
            inst = fn(handle)
            inst.then_inc(sem, 1)
        self.ops[eng].append(run)
        tok = ("eng", eng, n)
        if deps is None or close:
            self.last = tok
        return tok

    def dma(self, eng, out, in_, deps=None, close=False, **kw):
        w = self._dep_waits(eng, deps) if deps is not None else self._waits(eng)
        self.dmacount[eng] += 1
        n = self.dmacount[eng]
        sem = self.dsem[eng]

        def run(handle, w=w, out=out, in_=in_, sem=sem, kw=kw):
            for s, v in w:
                handle.wait_ge(s, v)
            handle.dma_start(out=out, in_=in_, **kw).then_inc(sem, 16)
        self.ops[eng].append(run)
        tok = ("dma", eng, n)
        if deps is None or close:
            self.last = tok
        return tok

    def fence(self):
        toks = []
        for e in ENGINES:
            if self.count[e]:
                toks.append(("eng", e, self.count[e]))
            if self.dmacount[e]:
                toks.append(("dma", e, self.dmacount[e]))
        return toks

    def barrier_wait(self, eng):
        snap = dict(self.count), dict(self.dmacount)

        def run(handle, snap=snap):
            counts, dcounts = snap
            for e in ENGINES:
                if counts[e] and e != eng:
                    handle.wait_ge(self.esem[e], counts[e])
                if dcounts[e]:
                    handle.wait_ge(self.dsem[e], 16 * dcounts[e])
        self.ops[eng].append(run)

    def finalize(self):
        nc = self.nc
        with nc.Block() as block:
            for e in ENGINES:
                lst = self.ops[e]
                if not lst and e != "sync":
                    continue

                def body(handle, lst=lst, e=e):
                    for fn in lst:
                        fn(handle)
                    if e == "sync":
                        for e2 in ENGINES:
                            if self.dmacount[e2]:
                                handle.wait_ge(self.dsem[e2],
                                               16 * self.dmacount[e2])
                            if self.count[e2] and e2 != "sync":
                                handle.wait_ge(self.esem[e2], self.count[e2])
                getattr(block, e)(body)


class Ctx:
    def __init__(self, nc, stack):
        self.nc = nc
        self.stack = stack
        self.se = SE(nc, stack)
        self.cpool = self.sbuf("cpool", [128, len(CONST_VALS)], F32)
        self.cidx = {float(v): j for j, v in enumerate(CONST_VALS)}
        self.ones_row = self.sbuf("ones_row", [1, 128], F32)
        self.ones_row_b = self.sbuf("ones_row_b", [1, 128], BF16)
        self.ones_col = self.sbuf("ones_col", [128, 1], F32)
        self._scr = {}

    def sbuf(self, name, shape, dtype):
        return self.stack.enter_context(self.nc.sbuf_tensor(name, shape, dtype))

    def psum(self, name, shape, dtype=F32):
        return self.stack.enter_context(self.nc.psum_tensor(name, shape, dtype))

    def const(self, val):
        val = float(val)
        return self.cpool[:, self.cidx[val]:self.cidx[val] + 1]

    def emit_init(self):
        se = self.se
        for j, v in enumerate(CONST_VALS):
            se.op("vector", lambda h, j=j, v=v: h.memset(
                self.cpool[:, j:j + 1], float(v)))
            self.nc.const_aps.register(F32, float(v), self.cpool[:, j:j + 1])
        se.op("vector", lambda h: h.memset(self.ones_row[:], 1.0))
        se.op("vector", lambda h: h.memset(self.ones_row_b[:], 1.0))
        se.op("vector", lambda h: h.memset(self.ones_col[:], 1.0))

    def emit_bcast(self, row_ap, out_psum_ap):
        n = row_ap.shape[-1]
        self.se.op("tensor", lambda h: h.matmul(
            out=out_psum_ap, lhsT=self.ones_row[:, 0:128], rhs=row_ap,
            start=True, stop=True))

    def emit_round(self, t_ap, out_ap, scr_i32, scr_a, scr_b):
        se = self.se
        se.op("vector", lambda h: h.tensor_copy(out=scr_i32, in_=t_ap))
        se.op("vector", lambda h: h.tensor_tensor(
            out=scr_a, in0=t_ap, in1=scr_i32, op=ALU.subtract))
        se.op("vector", lambda h: h.tensor_scalar(
            out=scr_b, in0=scr_a, scalar1=-0.5, scalar2=None, op0=ALU.is_lt))
        se.op("vector", lambda h: h.scalar_tensor_tensor(
            out=scr_a, in0=scr_a, scalar=0.5, in1=scr_b,
            op0=ALU.is_gt, op1=ALU.subtract))
        se.op("vector", lambda h: h.tensor_tensor(
            out=out_ap, in0=scr_i32, in1=scr_a, op=ALU.add))

    def emit_lnstats(self, x_ap, w, stats, mv, nmean, rstd, eps=EPS):
        se = self.se
        ng = w // 512
        for g in range(ng):
            se.op("vector", lambda h, g=g: h.bn_stats(
                out=stats[:, 6 * g:6 * (g + 1)], in_=x_ap[:, 512 * g:512 * (g + 1)]))
        se.op("vector", lambda h: h.bn_aggr(out=mv[:], in_=stats[:, 0:6 * ng]))
        se.op("vector", lambda h: h.tensor_scalar(
            out=nmean[:], in0=mv[:, 0:1], scalar1=-1.0, scalar2=None,
            op0=ALU.mult))
        se.op("scalar", lambda h: h.activation(
            out=rstd[:], in_=mv[:, 1:2], func=AF.Ln, bias=self.const(eps)))
        se.op("scalar", lambda h: h.activation(
            out=rstd[:], in_=rstd[:], func=AF.Exp, scale=self.const(-0.5)))

    def emit_transpose(self, in_sb, out_sb, n, pt, ident_b, in_off=0, out_off=0):
        se = self.se
        for j in range(n):
            se.op("tensor", lambda h, j=j: h.transpose(
                out=pt[:, 0:128], in_=in_sb[:, in_off + 128 * j:in_off + 128 * (j + 1)],
                identity=ident_b[:]))
            se.op("vector", lambda h, j=j: h.tensor_copy(
                out=out_sb[:, out_off + 128 * j:out_off + 128 * (j + 1)],
                in_=pt[:, 0:128]))


def make_identity(ctx, idn_i, ident_b):
    se = ctx.se
    se.op("gpsimd", lambda h: h.iota(
        idn_i[:], pattern=[[1, 128]], base=0, channel_multiplier=-1,
        allow_small_or_imprecise_dtypes=True))
    se.op("vector", lambda h: h.tensor_scalar(
        out=ident_b[:], in0=idn_i[:], scalar1=0.0, scalar2=None,
        op0=ALU.is_equal))


def shard_rows(a, n=NCORES):
    a = np.ascontiguousarray(a)
    a2 = a.reshape(B * T, -1)
    r = a2.shape[0] // n
    return [np.ascontiguousarray(a2[c * r:(c + 1) * r]) for c in range(n)]



def build_A():
    nc = bass.Bass("TRN2", target_bir_lowering=False)
    x_in = nc.dram_tensor("x", [RPC, D], F32, kind="ExternalInput").ap()
    wq_in = nc.dram_tensor("Wq", [D, 512], F32, kind="ExternalInput").ap()
    wk_in = nc.dram_tensor("Wk", [3, D, 512], F32, kind="ExternalInput").ap()
    wv_in = nc.dram_tensor("Wv", [3, D, 512], F32, kind="ExternalInput").ap()
    wg_in = nc.dram_tensor("Wg", [D, 4], F32, kind="ExternalInput").ap()
    w1s_in = nc.dram_tensor("w1s", [INNER, D], F32, kind="ExternalInput").ap()
    w2s_in = nc.dram_tensor("w2s", [D, INNER], F32, kind="ExternalInput").ap()

    xn0_out = nc.dram_tensor("xn0", [RPC, D], F32, kind="ExternalOutput").ap()
    q_out = nc.dram_tensor("q0", [RPC, 512], BF16, kind="ExternalOutput").ap()
    k_out = nc.dram_tensor("k0", [3, RPC, 512], BF16, kind="ExternalOutput").ap()
    v_out = nc.dram_tensor("v0", [3, RPC, 512], BF16, kind="ExternalOutput").ap()
    g_out = nc.dram_tensor("g0", [RPC, 4], F32, kind="ExternalOutput").ap()
    ws_out = nc.dram_tensor("mcl", [1, 2], F32, kind="ExternalOutput").ap()

    with ExitStack() as stack:
        ctx = Ctx(nc, stack)
        se = ctx.se
        xt = ctx.sbuf("xt", [128, D], F32)
        wq = ctx.sbuf("wq", [128, DK * 512], BF16)
        wk = [ctx.sbuf(f"wk{i}", [128, DK * 512], BF16) for i in range(3)]
        wv = [ctx.sbuf(f"wv{i}", [128, DK * 512], BF16) for i in range(3)]
        wg = ctx.sbuf("wg", [128, DK * 4], BF16)
        wsl1 = ctx.sbuf("wsl1", [128, D], F32)
        wsl2 = ctx.sbuf("wsl2", [128, INNER], F32)
        stats = ctx.sbuf("stats", [128, 24], F32)
        mv = ctx.sbuf("mv", [128, 2], F32)
        nmean = ctx.sbuf("nmean", [128, 1], F32)
        rstd = ctx.sbuf("rstd", [128, 1], F32)
        xn_f = ctx.sbuf("xn_f", [128, D], F32)
        xn_b = ctx.sbuf("xn_b", [128, D], BF16)
        xn_fm = ctx.sbuf("xn_fm", [128, DK * 128], BF16)
        idn_i = ctx.sbuf("idn_i", [128, 128], F32)
        ident_b = ctx.sbuf("ident_b", [128, 128], BF16)
        ev_b = ctx.sbuf("ev_b", [128, 512], BF16)
        g_ev = ctx.sbuf("g_ev", [128, 4], F32)
        gmax = ctx.sbuf("gmax", [128, 1], F32)
        gsum = ctx.sbuf("gsum", [128, 1], F32)
        col = ctx.sbuf("col", [128, 2], F32)
        wsacc = ctx.sbuf("wsacc", [128, 2], F32)
        wsfin = ctx.sbuf("wsfin", [1, 2], F32)

        pt = ctx.psum("pt", [128, 128], BF16)
        pp = ctx.psum("pp", [128, 512], F32)
        pg = ctx.psum("pg", [128, 4], F32)
        pws = ctx.psum("pws", [1, 2], F32)

        ctx.emit_init()
        make_identity(ctx, idn_i, ident_b)

        se.dma("gpsimd", wq[:].rearrange("p (k f) -> p k f", f=512),
               wq_in.rearrange("(k p) f -> p k f", p=128))
        for i in range(3):
            se.dma("gpsimd", wk[i][:].rearrange("p (k f) -> p k f", f=512),
                   wk_in[i].rearrange("(k p) f -> p k f", p=128))
            se.dma("gpsimd", wv[i][:].rearrange("p (k f) -> p k f", f=512),
                   wv_in[i].rearrange("(k p) f -> p k f", p=128))
        se.dma("gpsimd", wg[:].rearrange("p (k f) -> p k f", f=4),
               wg_in.rearrange("(k p) f -> p k f", p=128))
        se.op("vector", lambda h: h.memset(g_ev[:], 0.0))
        se.op("vector", lambda h: h.memset(wsacc[:], 0.0))

        for it in range(NT):
            se.dma("sync", xt[:], x_in[it * 128:(it + 1) * 128, :])
            ctx.emit_lnstats(xt[:], D, stats, mv, nmean, rstd)
            se.op("vector", lambda h: h.tensor_scalar(
                out=xn_f[:], in0=xt[:], scalar1=nmean[:, 0:1],
                scalar2=rstd[:, 0:1], op0=ALU.add, op1=ALU.mult))
            se.dma("sync", xn0_out[it * 128:(it + 1) * 128, :], xn_f[:])
            se.op("vector", lambda h: h.tensor_copy(out=xn_b[:], in_=xn_f[:]))
            ctx.emit_transpose(xn_b, xn_fm, DK, pt, ident_b)
            mats = [(wq, q_out)] + [(wk[i], (k_out, i)) for i in range(3)] \
                + [(wv[i], (v_out, i)) for i in range(3)]
            for w_sb, dst in mats:
                for k in range(DK):
                    se.op("tensor", lambda h, k=k, w_sb=w_sb: h.matmul(
                        out=pp[:], lhsT=xn_fm[:, 128 * k:128 * (k + 1)],
                        rhs=w_sb[:, 512 * k:512 * (k + 1)],
                        start=(k == 0), stop=(k == DK - 1)))
                se.op("vector", lambda h: h.tensor_copy(out=ev_b[:], in_=pp[:]))
                if isinstance(dst, tuple):
                    d, i = dst
                    se.dma("sync", d[i, it * 128:(it + 1) * 128, :], ev_b[:])
                else:
                    se.dma("sync", dst[it * 128:(it + 1) * 128, :], ev_b[:])
            for k in range(DK):
                se.op("tensor", lambda h, k=k: h.matmul(
                    out=pg[:], lhsT=xn_fm[:, 128 * k:128 * (k + 1)],
                    rhs=wg[:, 4 * k:4 * (k + 1)],
                    start=(k == 0), stop=(k == DK - 1)))
            se.op("vector", lambda h: h.reduce_max(
                out=gmax[:], in_=pg[:, 0:3], axis=AX.X))
            se.op("vector", lambda h: h.tensor_scalar(
                out=gmax[:], in0=gmax[:], scalar1=-1.0, scalar2=None,
                op0=ALU.mult))
            se.op("scalar", lambda h: h.activation(
                out=g_ev[:, 0:3], in_=pg[:, 0:3], func=AF.Exp,
                bias=gmax[:, 0:1], accum_out=gsum[:]))
            se.op("vector", lambda h: h.reciprocal(out=gsum[:], in_=gsum[:]))
            se.op("vector", lambda h: h.tensor_scalar(
                out=g_ev[:, 0:3], in0=g_ev[:, 0:3], scalar1=gsum[:, 0:1],
                scalar2=None, op0=ALU.mult))
            se.dma("sync", g_out[it * 128:(it + 1) * 128, :], g_ev[:])

        for j in range(INNER // 128):
            se.dma("sync", wsl1[:], w1s_in[j * 128:(j + 1) * 128, :])
            se.op("vector", lambda h: h.reduce_sum(
                out=col[:, 0:1], in_=wsl1[:], axis=AX.X,
                apply_absolute_value=True))
            se.op("vector", lambda h: h.tensor_tensor(
                out=wsacc[:, 0:1], in0=wsacc[:, 0:1], in1=col[:, 0:1],
                op=ALU.add))
        for j in range(D // 128):
            se.dma("sync", wsl2[:], w2s_in[j * 128:(j + 1) * 128, :])
            se.op("vector", lambda h: h.reduce_sum(
                out=col[:, 1:2], in_=wsl2[:], axis=AX.X,
                apply_absolute_value=True))
            se.op("vector", lambda h: h.tensor_tensor(
                out=wsacc[:, 1:2], in0=wsacc[:, 1:2], in1=col[:, 1:2],
                op=ALU.add))
        se.op("tensor", lambda h: h.matmul(
            out=pws[:], lhsT=ctx.ones_col[:], rhs=wsacc[:],
            start=True, stop=True))
        se.op("vector", lambda h: h.tensor_scalar(
            out=wsfin[:], in0=pws[:], scalar1=1.0 / (INNER * D), scalar2=EPS,
            op0=ALU.mult, op1=ALU.max))
        se.dma("sync", ws_out[:], wsfin[:])

        se.finalize()
    return nc


def run_A(inputs):
    nc = build_A()
    xs = shard_rows(inputs["L0_attn_x"])
    wgp = np.zeros((D, 4), np.float32)
    wgp[:, 0:3] = inputs["L0_attn_Wg"]
    in_maps = []
    for c in range(NCORES):
        in_maps.append({
            "x": xs[c],
            "Wq": np.ascontiguousarray(inputs["L0_attn_Wq"]),
            "Wk": np.ascontiguousarray(inputs["L0_attn_Wk"]),
            "Wv": np.ascontiguousarray(inputs["L0_attn_Wv"]),
            "Wg": wgp,
            "w1s": np.ascontiguousarray(inputs["L0_ffn_w1"]),
            "w2s": np.ascontiguousarray(inputs["L0_ffn_w2"]),
        })
    res = run_bass_kernel_spmd(nc, in_maps, core_ids=list(range(NCORES)))
    r = res.results
    out = {
        "xn0": np.concatenate([r[c]["xn0"] for c in range(NCORES)], 0),
        "q0": np.concatenate([r[c]["q0"] for c in range(NCORES)], 0),
        "k0": np.concatenate([r[c]["k0"] for c in range(NCORES)], 1),
        "v0": np.concatenate([r[c]["v0"] for c in range(NCORES)], 1),
        "g0": np.concatenate([r[c]["g0"] for c in range(NCORES)], 0),
        "mcl": r[0]["mcl"],
    }
    return out



def build_P():
    nc = bass.Bass("TRN2", target_bir_lowering=False)
    w1_in = nc.dram_tensor("w1s", [INNER // 8, D], F32, kind="ExternalInput").ap()
    w2_in = nc.dram_tensor("w2s", [D // 8, INNER], F32, kind="ExternalInput").ap()
    gw_in = nc.dram_tensor("gws", [INNER // 8, D], F32, kind="ExternalInput").ap()
    uw_in = nc.dram_tensor("uws", [INNER // 8, D], F32, kind="ExternalInput").ap()
    dw_in = nc.dram_tensor("dws", [D // 8, INNER], F32, kind="ExternalInput").ap()
    mcl_in = nc.dram_tensor("mcl", [1, 2], F32, kind="ExternalInput").ap()

    w1t_o = nc.dram_tensor("w1t_s", [D, INNER // 8], BF16, kind="ExternalOutput").ap()
    w2t_o = nc.dram_tensor("w2t_s", [INNER, D // 8], BF16, kind="ExternalOutput").ap()
    gq_o = nc.dram_tensor("gq_s", [D, INNER // 8], BF16, kind="ExternalOutput").ap()
    uq_o = nc.dram_tensor("uq_s", [D, INNER // 8], BF16, kind="ExternalOutput").ap()
    dq_o = nc.dram_tensor("dq_s", [INNER, D // 8], BF16, kind="ExternalOutput").ap()

    with ExitStack() as stack:
        ctx = Ctx(nc, stack)
        se = ctx.se
        idn_i = ctx.sbuf("idn_i", [128, 128], F32)
        ident_b = ctx.sbuf("ident_b", [128, 128], BF16)
        wld = ctx.sbuf("wld", [128, INNER], F32)
        m1 = ctx.sbuf("m1", [128, D], F32)
        m2 = ctx.sbuf("m2", [128, D], F32)
        m3 = ctx.sbuf("m3", [128, D], F32)
        m4 = ctx.sbuf("m4", [128, D], F32)
        m5 = ctx.sbuf("m5", [128, D], F32)
        mi = ctx.sbuf("mi", [128, D], I32)
        bmax = ctx.sbuf("bmax", [128, 64], F32)
        sbs = ctx.sbuf("sbs", [128, 64], F32)
        rss = ctx.sbuf("rss", [128, 64], F32)
        qb = ctx.sbuf("qb", [128, INNER], BF16)
        stg = ctx.sbuf("stg", [128, INNER], BF16)
        mclsb = ctx.sbuf("mclsb", [1, 2], F32)
        swcol = ctx.sbuf("swcol", [128, 2], F32)
        pH = ctx.psum("pH", [128, 1024], BF16)
        pP = ctx.psum("pP", [128, 512], F32)

        ctx.emit_init()
        make_identity(ctx, idn_i, ident_b)
        se.dma("sync", mclsb[:], mcl_in[:])
        se.op("tensor", lambda h: h.matmul(
            out=pP[:, 0:2], lhsT=ctx.ones_row[:], rhs=mclsb[:],
            start=True, stop=True))
        se.op("vector", lambda h: h.tensor_copy(out=swcol[:], in_=pP[:, 0:2]))
        se.op("vector", lambda h: h.reciprocal(out=swcol[:], in_=swcol[:]))

        def tern_window(src_ap, dst_ap, sw_col):
            se.op("vector", lambda h: h.tensor_scalar(
                out=m2[:], in0=src_ap, scalar1=sw_col, scalar2=None,
                op0=ALU.mult))
            se.op("vector", lambda h: h.tensor_scalar(
                out=m3[:], in0=m2[:], scalar1=-0.5, scalar2=None,
                op0=ALU.is_le))
            se.op("vector", lambda h: h.tensor_scalar(
                out=m2[:], in0=m2[:], scalar1=0.5, scalar2=None,
                op0=ALU.is_ge))
            se.op("vector", lambda h: h.tensor_tensor(
                out=dst_ap, in0=m2[:], in1=m3[:], op=ALU.subtract))

        for cch in range(8):
            se.dma("sync", wld[:, 0:D], w1_in[cch * 128:(cch + 1) * 128, :])
            tern_window(wld[:, 0:D], qb[:, 0:D], swcol[:, 0:1])
            for k in range(DK):
                se.op("tensor", lambda h, k=k: h.transpose(
                    out=pH[:, 0:128], in_=qb[:, 128 * k:128 * (k + 1)],
                    identity=ident_b[:]))
                se.op("vector", lambda h, k=k: h.tensor_copy(
                    out=stg[:, k * 128:(k + 1) * 128], in_=pH[:, 0:128]))
            se.dma("sync",
                   w1t_o[:, cch * 128:(cch + 1) * 128].rearrange(
                       "(k p) c -> p k c", p=128),
                   stg[:, 0:D].rearrange("p (k c) -> p k c", c=128))
        for cch in range(2):
            se.dma("sync", wld[:], w2_in[cch * 128:(cch + 1) * 128, :])
            for w in range(4):
                sl = slice(w * D, (w + 1) * D)
                tern_window(wld[:, sl], qb[:, sl], swcol[:, 1:2])
            for k in range(INNER // 128):
                se.op("tensor", lambda h, k=k: h.transpose(
                    out=pH[:, 0:128], in_=qb[:, 128 * k:128 * (k + 1)],
                    identity=ident_b[:]))
                se.op("vector", lambda h, k=k: h.tensor_copy(
                    out=stg[:, k * 128:(k + 1) * 128], in_=pH[:, 0:128]))
            se.dma("sync",
                   w2t_o[:, cch * 128:(cch + 1) * 128].rearrange(
                       "(k p) c -> p k c", p=128),
                   stg[:].rearrange("p (k c) -> p k c", c=128))
        for w_in, w_o in ((gw_in, gq_o), (uw_in, uq_o)):
            for cch in range(8):
                se.dma("sync", m4[:], w_in[cch * 128:(cch + 1) * 128, :])
                emit_mx_quant(ctx, m4[:], qb[:, 0:D], D, m1[:], m2[:], m3[:],
                              m4[:], m5[:], mi[:], bmax[:], sbs[:], rss[:])
                for k in range(DK):
                    se.op("tensor", lambda h, k=k: h.transpose(
                        out=pH[:, 0:128], in_=qb[:, 128 * k:128 * (k + 1)],
                        identity=ident_b[:]))
                    se.op("vector", lambda h, k=k: h.tensor_copy(
                        out=stg[:, k * 128:(k + 1) * 128], in_=pH[:, 0:128]))
                se.dma("sync",
                       w_o[:, cch * 128:(cch + 1) * 128].rearrange(
                           "(k p) c -> p k c", p=128),
                       stg[:, 0:D].rearrange("p (k c) -> p k c", c=128))
        for cch in range(2):
            se.dma("sync", wld[:], dw_in[cch * 128:(cch + 1) * 128, :])
            for w in range(4):
                sl = slice(w * D, (w + 1) * D)
                se.op("vector", lambda h, sl=sl: h.tensor_copy(
                    out=m4[:], in_=wld[:, sl]))
                emit_mx_quant(ctx, m4[:], qb[:, sl], D, m1[:], m2[:], m3[:],
                              m4[:], m5[:], mi[:], bmax[:], sbs[:], rss[:])
            for k in range(INNER // 128):
                se.op("tensor", lambda h, k=k: h.transpose(
                    out=pH[:, 0:128], in_=qb[:, 128 * k:128 * (k + 1)],
                    identity=ident_b[:]))
                se.op("vector", lambda h, k=k: h.tensor_copy(
                    out=stg[:, k * 128:(k + 1) * 128], in_=pH[:, 0:128]))
            se.dma("sync",
                   dq_o[:, cch * 128:(cch + 1) * 128].rearrange(
                       "(k p) c -> p k c", p=128),
                   stg[:].rearrange("p (k c) -> p k c", c=128))
        se.finalize()
    return nc


def run_P(inputs, aux):
    nc = build_P()
    in_maps = []
    for c in range(NCORES):
        i8, d8 = INNER // 8, D // 8
        in_maps.append({
            "w1s": np.ascontiguousarray(inputs["L0_ffn_w1"][c * i8:(c + 1) * i8]),
            "w2s": np.ascontiguousarray(inputs["L0_ffn_w2"][c * d8:(c + 1) * d8]),
            "gws": np.ascontiguousarray(inputs["L1_ffn_gate_w"][c * i8:(c + 1) * i8]),
            "uws": np.ascontiguousarray(inputs["L1_ffn_up_w"][c * i8:(c + 1) * i8]),
            "dws": np.ascontiguousarray(inputs["L1_ffn_down_w"][c * d8:(c + 1) * d8]),
            "mcl": np.asarray(aux["mcl"]).reshape(1, 2),
        })
    res = run_bass_kernel_spmd(nc, in_maps, core_ids=list(range(NCORES)))
    r = res.results
    return {
        "w1t": np.concatenate([r[c]["w1t_s"] for c in range(NCORES)], 1),
        "w2t": np.concatenate([r[c]["w2t_s"] for c in range(NCORES)], 1),
        "gq": np.concatenate([r[c]["gq_s"] for c in range(NCORES)], 1),
        "uq": np.concatenate([r[c]["uq_s"] for c in range(NCORES)], 1),
        "dq": np.concatenate([r[c]["dq_s"] for c in range(NCORES)], 1),
    }



def build_B():
    nc = bass.Bass("TRN2", target_bir_lowering=False)
    q_in = nc.dram_tensor("q", [RPC, 512], BF16, kind="ExternalInput").ap()
    k_in = nc.dram_tensor("k", [3, T, 512], BF16, kind="ExternalInput").ap()
    v_in = nc.dram_tensor("v", [3, T, 512], BF16, kind="ExternalInput").ap()
    g_in = nc.dram_tensor("g", [RPC, 4], F32, kind="ExternalInput").ap()
    dl_in = nc.dram_tensor("dl", [1, 4], F32, kind="ExternalInput").ap()
    osc_in = nc.dram_tensor("osc", [1, 1], F32, kind="ExternalInput").ap()
    wo_in = nc.dram_tensor("Wo", [512, D], F32, kind="ExternalInput").ap()
    half_in = nc.dram_tensor("half", [1, 1], F32, kind="ExternalInput").ap()

    a_out = nc.dram_tensor("attn0", [RPC, D], F32, kind="ExternalOutput").ap()
    if B_DEBUG:
        bl_dbg = nc.dram_tensor("bl_dbg", [NT, 128, 512], F32,
                                kind="ExternalOutput").ap()
        mk_dbg = nc.dram_tensor("mk_dbg", [3, 4, 128, 512], F32,
                                kind="ExternalOutput").ap()
        sc_dbg = nc.dram_tensor("sc_dbg", [3, 4, 128, 512], F32,
                                kind="ExternalOutput").ap()
        wt_dbg = nc.dram_tensor("wt_dbg", [3, 4, 128, 512], F32,
                                kind="ExternalOutput").ap()

    with ExitStack() as stack:
        ctx = Ctx(nc, stack)
        se = ctx.se
        idn_i = ctx.sbuf("idn_i", [128, 128], F32)
        ident_b = ctx.sbuf("ident_b", [128, 128], BF16)
        q_fm = ctx.sbuf("q_fm", [128, 4 * RPC], BF16)
        k_fm = [ctx.sbuf(f"k_fm{i}", [128, 4 * T], BF16) for i in range(3)]
        v_tm = [ctx.sbuf(f"v_tm{i}", [128, 16 * 512], BF16) for i in range(3)]
        gt = ctx.sbuf("gt", [128, NT * 4], F32)
        ld_tm = ctx.sbuf("ld_tm", [128, 512], BF16)
        t1 = ctx.sbuf("t1", [128, 512], F32)
        wtile = ctx.sbuf("wtile", [128, 512], F32)
        getile = ctx.sbuf("getile", [128, 512], F32)
        mk_b = ctx.sbuf("mk_b", [128, 512], BF16)
        mk_f32 = ctx.sbuf("mk_f32", [128, 512], F32)
        mkT = ctx.sbuf("mkT", [128, 512], BF16)
        blend = ctx.sbuf("blend", [128, 512], F32)
        blend_b = ctx.sbuf("blend_b", [128, 512], BF16)
        blendT = ctx.sbuf("blendT", [128, 512], BF16)
        att_f = ctx.sbuf("att_f", [128, 512], F32)
        wo = ctx.sbuf("wo", [128, 4 * D], BF16)
        sig = ctx.sbuf("sig", [1, 4], F32)
        lnd_row = ctx.sbuf("lnd_row", [1, 4], F32)
        lnd_col = ctx.sbuf("lnd_col", [128, 4], F32)
        bias_col = ctx.sbuf("bias_col", [128, 1], F32)
        osc_sb = ctx.sbuf("osc_sb", [1, 1], F32)
        osc_col = ctx.sbuf("osc_col", [128, 1], F32)

        pt = ctx.psum("pt", [128, 128], BF16)
        psc = ctx.psum("psc", [128, 512], F32)
        pav = ctx.psum("pav", [128, 512], F32)
        po = ctx.psum("po", [128, 512], F32)
        pb = ctx.psum("pb", [128, 4], F32)

        ctx.emit_init()
        make_identity(ctx, idn_i, ident_b)

        HALF = None
        ctx.emit_init2 = None

        se.dma("sync", osc_sb[:], osc_in[:])
        se.dma("sync", sig[:], dl_in[:])
        se.op("scalar", lambda h: h.activation(out=sig[:], in_=sig[:],
                                               func=AF.Sigmoid))
        se.op("scalar", lambda h: h.activation(out=lnd_row[:], in_=sig[:],
                                               func=AF.Ln))
        se.op("tensor", lambda h: h.matmul(out=pb[:], lhsT=ctx.ones_row[:],
                                           rhs=lnd_row[:], start=True, stop=True))
        se.op("vector", lambda h: h.tensor_copy(out=lnd_col[:], in_=pb[:]))
        se.op("tensor", lambda h: h.matmul(out=pb[:, 0:1], lhsT=ctx.ones_row[:],
                                           rhs=osc_sb[:], start=True, stop=True))
        se.op("vector", lambda h: h.tensor_copy(out=osc_col[:], in_=pb[:, 0:1]))

        se.dma("gpsimd", wo[:].rearrange("p (k j) -> p k j", j=D),
               wo_in.rearrange("(k p) j -> p k j", p=128))
        for it in range(NT):
            se.dma("sync", gt[:, 4 * it:4 * (it + 1)],
                   g_in[128 * it:128 * (it + 1), :])
        for it in range(NT):
            se.dma("sync", ld_tm[:], q_in[128 * it:128 * (it + 1), :])
            for j in range(4):
                se.op("tensor", lambda h, j=j: h.transpose(
                    out=pt[:], in_=ld_tm[:, 128 * j:128 * (j + 1)],
                    identity=ident_b[:]))
                se.op("vector", lambda h, j=j, it=it: h.tensor_copy(
                    out=q_fm[:, j * RPC + it * 128: j * RPC + (it + 1) * 128],
                    in_=pt[:]))
        for i in range(3):
            for st_ in range(16):
                se.dma("sync", v_tm[i][:, 512 * st_:512 * (st_ + 1)],
                       k_in[i, 128 * st_:128 * (st_ + 1), :])
                for j in range(4):
                    se.op("tensor", lambda h, j=j, i=i, st_=st_: h.transpose(
                        out=pt[:], in_=v_tm[i][:, 512 * st_ + 128 * j:
                                               512 * st_ + 128 * (j + 1)],
                        identity=ident_b[:]))
                    se.op("vector", lambda h, j=j, i=i, st_=st_: h.tensor_copy(
                        out=k_fm[i][:, j * T + st_ * 128:j * T + (st_ + 1) * 128],
                        in_=pt[:]))
            for st_ in range(16):
                se.dma("sync", v_tm[i][:, 512 * st_:512 * (st_ + 1)],
                       v_in[i, 128 * st_:128 * (st_ + 1), :])

        se.op("gpsimd", lambda h: h.iota(
            t1[:], pattern=[[1, 512]], base=-1, channel_multiplier=-1,
            allow_small_or_imprecise_dtypes=True))

        for qt in range(NT):
            t0 = qt * 128
            ch_lo = t0 // 512
            se.op("vector", lambda h: h.memset(blend[:], 0.0))
            for i in range(3):
                first_av = True
                for ch in range(ch_lo, 4):
                    s0 = ch * 512
                    for j in range(4):
                        se.op("tensor", lambda h, j=j, i=i, qt=qt, s0=s0: h.matmul(
                            out=psc[:],
                            lhsT=q_fm[:, j * RPC + qt * 128:j * RPC + (qt + 1) * 128],
                            rhs=k_fm[i][:, j * T + s0:j * T + s0 + 512],
                            start=(j == 0), stop=(j == 3)))
                    d0 = float(s0 - t0)
                    se.op("vector", lambda h, i=i, d0=d0: h.tensor_scalar(
                        out=bias_col[:], in0=lnd_col[:, i:i + 1], scalar1=d0,
                        scalar2=None, op0=ALU.mult))
                    se.op("vector", lambda h, i=i: h.tensor_scalar(
                        out=wtile[:], in0=t1[:], scalar1=lnd_col[:, i:i + 1],
                        scalar2=bias_col[:, 0:1], op0=ALU.mult, op1=ALU.add))
                    se.op("vector", lambda h: h.tensor_scalar_min(
                        out=wtile[:], in0=wtile[:], scalar1=0.0))
                    se.op("scalar", lambda h: h.activation(
                        out=wtile[:], in_=wtile[:], func=AF.Exp))
                    if s0 <= t0:
                        se.op("vector", lambda h, d0=d0: h.tensor_scalar(
                            out=getile[:], in0=t1[:], scalar1=d0, scalar2=0.0,
                            op0=ALU.add, op1=ALU.is_ge))
                        se.op("vector", lambda h: h.tensor_tensor(
                            out=wtile[:], in0=wtile[:], in1=getile[:],
                            op=ALU.mult))
                    se.op("vector", lambda h: h.tensor_tensor(
                        out=mk_b[:], in0=psc[:], in1=wtile[:], op=ALU.mult))
                    if B_DEBUG and qt == B_DEBUG_QT:
                        se.op("vector", lambda h: h.tensor_copy(
                            out=mk_f32[:], in_=mk_b[:]))
                        se.dma("sync", mk_dbg[i, ch], mk_f32[:])
                        se.op("vector", lambda h: h.tensor_copy(
                            out=mk_f32[:], in_=psc[:]))
                        se.dma("sync", sc_dbg[i, ch], mk_f32[:])
                        se.dma("sync", wt_dbg[i, ch], wtile[:])
                    for j in range(4):
                        se.op("tensor", lambda h, j=j: h.transpose(
                            out=pt[:], in_=mk_b[:, 128 * j:128 * (j + 1)],
                            identity=ident_b[:]))
                        se.op("vector", lambda h, j=j: h.tensor_copy(
                            out=mkT[:, 128 * j:128 * (j + 1)], in_=pt[:]))
                    for j in range(4):
                        stile = ch * 4 + j
                        se.op("tensor", lambda h, j=j, i=i, stile=stile: h.matmul(
                            out=pav[:], lhsT=mkT[:, 128 * j:128 * (j + 1)],
                            rhs=v_tm[i][:, 512 * stile:512 * (stile + 1)],
                            start=(j == 0), stop=(j == 3)))
                    first_av = False
                    se.op("vector", lambda h, i=i, qt=qt: h.scalar_tensor_tensor(
                        out=blend[:], in0=pav[:],
                        scalar=gt[:, 4 * qt + i:4 * qt + i + 1],
                        in1=blend[:], op0=ALU.mult, op1=ALU.add))
            if B_DEBUG:
                se.dma("sync", bl_dbg[qt], blend[:])
            se.op("vector", lambda h: h.tensor_copy(out=blend_b[:], in_=blend[:]))
            for j in range(4):
                se.op("tensor", lambda h, j=j: h.transpose(
                    out=pt[:], in_=blend_b[:, 128 * j:128 * (j + 1)],
                    identity=ident_b[:]))
                se.op("vector", lambda h, j=j: h.tensor_copy(
                    out=blendT[:, 128 * j:128 * (j + 1)], in_=pt[:]))
            for jc in range(4):
                for kk in range(4):
                    se.op("tensor", lambda h, jc=jc, kk=kk: h.matmul(
                        out=po[:], lhsT=blendT[:, 128 * kk:128 * (kk + 1)],
                        rhs=wo[:, kk * D + jc * 512:kk * D + jc * 512 + 512],
                        start=(kk == 0), stop=(kk == 3)))
                se.op("vector", lambda h: h.tensor_scalar(
                    out=att_f[:], in0=po[:], scalar1=osc_col[:, 0:1],
                    scalar2=None, op0=ALU.mult))
                se.dma("sync", a_out[128 * qt:128 * (qt + 1),
                                     512 * jc:512 * (jc + 1)], att_f[:])

        se.finalize()
    return nc


def run_B(inputs, aux):
    nc = build_B()
    qs = shard_rows(np.asarray(aux["q0"]))
    gs = shard_rows(np.asarray(aux["g0"]))
    k0 = np.asarray(aux["k0"]).reshape(3, B, T, 512)
    v0 = np.asarray(aux["v0"]).reshape(3, B, T, 512)
    dl = np.zeros((1, 4), np.float32)
    dl[0, 0:3] = inputs["L0_attn_decay_logits"]
    osc = np.asarray(inputs["L0_attn_out_scale"], np.float32).reshape(1, 1)
    in_maps = []
    for c in range(NCORES):
        b, half = c // 2, c % 2
        kk = k0[:, b]
        vv = v0[:, b]
        if half == 1:
            kk = np.concatenate([kk[:, 1024:2048], np.zeros_like(kk[:, 0:1024])], 1)
            vv = np.concatenate([vv[:, 1024:2048], np.zeros_like(vv[:, 0:1024])], 1)
        in_maps.append({
            "q": qs[c], "g": gs[c],
            "k": np.ascontiguousarray(kk), "v": np.ascontiguousarray(vv),
            "dl": dl, "osc": osc, "half": np.zeros((1, 1), np.float32),
            "Wo": np.ascontiguousarray(inputs["L0_attn_Wo"]),
        })
    res = run_bass_kernel_spmd(nc, in_maps, core_ids=list(range(NCORES)))
    return np.concatenate([res.results[c]["attn0"] for c in range(NCORES)], 0)



def build_C():
    nc = bass.Bass("TRN2", target_bir_lowering=False)
    xn0_in = nc.dram_tensor("xn0", [RPC, D], F32, kind="ExternalInput").ap()
    x_in = nc.dram_tensor("x", [RPC, D], F32, kind="ExternalInput").ap()
    at_in = nc.dram_tensor("attn0", [RPC, D], F32, kind="ExternalInput").ap()
    w1t_in = nc.dram_tensor("w1t", [D, INNER], BF16, kind="ExternalInput").ap()
    w2t_in = nc.dram_tensor("w2t", [INNER, D], BF16, kind="ExternalInput").ap()
    b1_in = nc.dram_tensor("b1", [1, INNER], F32, kind="ExternalInput").ap()
    lng_in = nc.dram_tensor("lng", [1, INNER], F32, kind="ExternalInput").ap()
    lnb_in = nc.dram_tensor("lnb", [1, INNER], F32, kind="ExternalInput").ap()
    b2_in = nc.dram_tensor("b2", [1, D], F32, kind="ExternalInput").ap()
    ws_in = nc.dram_tensor("mcl2", [1, 2], F32, kind="ExternalInput").ap()

    x1_out = nc.dram_tensor("x1", [RPC, D], F32, kind="ExternalOutput").ap()
    xn1_out = nc.dram_tensor("xn1", [RPC, D], F32, kind="ExternalOutput").ap()
    if C_DEBUG or C_DEBUG_LITE:
        f0dbg_out = nc.dram_tensor("f0dbg", [RPC, D], F32, kind="ExternalOutput").ap()
    if C_DEBUG:
        dbg_out = nc.dram_tensor("dbg", [128, 64], F32, kind="ExternalOutput").ap()
        hdbg_out = nc.dram_tensor("hdbg", [RPC, INNER], F32, kind="ExternalOutput").ap()
        hqdbg_out = nc.dram_tensor("hqdbg", [INNER, RPC], BF16, kind="ExternalOutput").ap()

    w1t_d = w1t_in
    w2t_d = w2t_in
    h_d = nc.dram_tensor("h_d", [RPC, INNER], F32, kind="Internal").ap()
    hq_d = nc.dram_tensor("hq_d", [INNER, RPC], BF16, kind="Internal").ap()

    with ExitStack() as stack:
        ctx = Ctx(nc, stack)
        se = ctx.se
        idn_i = ctx.sbuf("idn_i", [128, 128], F32)
        ident_b = ctx.sbuf("ident_b", [128, 128], BF16)
        xq_fm = ctx.sbuf("xq_fm", [128, DK * RPC], BF16)
        big_f1 = ctx.sbuf("big_f1", [128, INNER], F32)
        big_f2 = ctx.sbuf("big_f2", [128, D], F32)
        big_f3 = ctx.sbuf("big_f3", [128, D], F32)
        big_i = ctx.sbuf("big_i", [128, D], I32)
        big_b = ctx.sbuf("big_b", [128, INNER], BF16)
        hqT = ctx.sbuf("hqT", [128, D], BF16)
        gw = ctx.sbuf("gw", [128, D], F32)
        bw = ctx.sbuf("bw", [128, D], F32)
        b2r = ctx.sbuf("b2r", [128, D], F32)
        b1w = ctx.sbuf("b1w", [128, 512], F32)
        rowb1 = ctx.sbuf("rowb1", [1, 512], F32)
        rowbuf2 = ctx.sbuf("rowbuf2", [1, D], F32)
        b2row = ctx.sbuf("b2row", [1, D], F32)
        ws16 = ctx.sbuf("ws16", [1, 16], F32)
        wsred = ctx.sbuf("wsred", [1, 2], F32)
        swcol = ctx.sbuf("swcol", [128, 2], F32)
        mclcol = ctx.sbuf("mclcol", [128, 2], F32)
        cols = ctx.sbuf("cols", [128, 16], F32)
        runsum = ctx.sbuf("runsum", [128, NT], F32)
        runsq = ctx.sbuf("runsq", [128, NT], F32)
        amax2 = ctx.sbuf("amax2", [128, NT], F32)
        asq = ctx.sbuf("asq", [128, NT], F32)
        comb2 = ctx.sbuf("comb2", [128, NT], F32)
        mucol = ctx.sbuf("mucol", [128, NT], F32)
        rstd2 = ctx.sbuf("rstd2", [128, NT], F32)
        s2c = ctx.sbuf("s2c", [128, NT], F32)
        inv2c = ctx.sbuf("inv2c", [128, NT], F32)
        invs = ctx.sbuf("invs", [128, NT], F32)
        c1 = ctx.sbuf("c1", [128, 1], F32)
        c2 = ctx.sbuf("c2", [128, 1], F32)
        c3 = ctx.sbuf("c3", [128, 1], F32)

        pA = ctx.psum("pA", [128, 512], F32)
        pB = ctx.psum("pB", [128, 512], F32)
        pC = ctx.psum("pC", [128, 512], F32)
        pD = ctx.psum("pD", [128, 512], F32)
        pH = ctx.psum("pH", [128, 1024], BF16)
        pds = [pA, pB, pC, pD]

        ctx.emit_init()
        make_identity(ctx, idn_i, ident_b)

        se.dma("sync", wsred[:], ws_in[:])
        se.op("tensor", lambda h: h.matmul(
            out=pA[:, 0:2], lhsT=ctx.ones_row[:], rhs=wsred[:],
            start=True, stop=True))
        se.op("vector", lambda h: h.tensor_copy(out=mclcol[:], in_=pA[:, 0:2]))
        se.dma("sync", b2row[:], b2_in[:])
        for j in range(4):
            se.op("tensor", lambda h, j=j: h.matmul(
                out=pB[:], lhsT=ctx.ones_row[:],
                rhs=b2row[:, 512 * j:512 * (j + 1)], start=True, stop=True))
            se.op("vector", lambda h, j=j: h.tensor_copy(
                out=b2r[:, 512 * j:512 * (j + 1)], in_=pB[:]))
        se.op("vector", lambda h: h.memset(runsum[:], 0.0))
        se.op("vector", lambda h: h.memset(runsq[:], 0.0))
        se.op("vector", lambda h: h.memset(amax2[:], 0.0))
        se.op("vector", lambda h: h.memset(asq[:], 0.0))

        for it in range(NT):
            xt = big_f1[:, 0:D]
            se.dma("sync", xt, xn0_in[it * 128:(it + 1) * 128, :])
            se.op("scalar", lambda h, xt=xt: h.activation(
                out=big_f2[:, 0:D], in_=xt, func=AF.Square, accum_out=c1[:]))
            se.op("vector", lambda h: h.tensor_scalar(
                out=c1[:], in0=c1[:], scalar1=1.0 / D, scalar2=EPS,
                op0=ALU.mult, op1=ALU.add))
            se.op("scalar", lambda h: h.activation(
                out=c1[:], in_=c1[:], func=AF.Ln))
            se.op("scalar", lambda h: h.activation(
                out=c1[:], in_=c1[:], func=AF.Exp, scale=ctx.const(-0.5)))
            se.op("vector", lambda h, xt=xt: h.reduce_max(
                out=c2[:], in_=xt, axis=AX.X, apply_absolute_value=True))
            se.op("vector", lambda h: h.tensor_tensor(
                out=c2[:], in0=c2[:], in1=c1[:], op=ALU.mult))
            se.op("vector", lambda h: h.tensor_scalar_max(
                out=c2[:], in0=c2[:], scalar1=EPS))
            se.op("vector", lambda h, it=it: h.tensor_scalar(
                out=invs[:, it:it + 1], in0=c2[:], scalar1=1.0 / 127.0,
                scalar2=None, op0=ALU.mult))
            se.op("vector", lambda h: h.reciprocal(out=c2[:], in_=c2[:]))
            se.op("vector", lambda h: h.tensor_scalar(
                out=c2[:], in0=c2[:], scalar1=127.0, scalar2=None,
                op0=ALU.mult))
            se.op("vector", lambda h: h.tensor_tensor(
                out=c3[:], in0=c1[:], in1=c2[:], op=ALU.mult))
            se.op("vector", lambda h, xt=xt: h.tensor_scalar(
                out=big_f2[:, 0:D], in0=xt, scalar1=c3[:, 0:1], scalar2=None,
                op0=ALU.mult))
            ctx.emit_round(big_f2[:, 0:D], big_b[:, 0:D], big_i[:],
                           big_f3[:, 0:D], big_f1[:, D:2 * D])
            for k in range(DK):
                se.op("tensor", lambda h, k=k: h.transpose(
                    out=pH[:, 0:128], in_=big_b[:, 128 * k:128 * (k + 1)],
                    identity=ident_b[:]))
                se.op("vector", lambda h, k=k, it=it: h.tensor_copy(
                    out=xq_fm[:, k * RPC + it * 128:k * RPC + (it + 1) * 128],
                    in_=pH[:, 0:128]))

        free_tok = None
        sq_tok = None
        dma_tok = None
        mm_last = None
        for wg in range(16):
            rdeps = [se.last] if wg == 0 else [mm_last]
            for k in range(DK):
                rhs_tok = se.dma("sync", big_b[:, k * 512:(k + 1) * 512],
                                 w1t_d[k * 128:(k + 1) * 128,
                                       wg * 512:(wg + 1) * 512], deps=rdeps)
            bt = se.dma("sync", rowb1[:], b1_in[0:1, wg * 512:(wg + 1) * 512],
                        deps=rdeps)
            bt = se.op("tensor", lambda h: h.matmul(
                out=pC[:], lhsT=ctx.ones_row[:], rhs=rowb1[:],
                start=True, stop=True), deps=[bt, free_tok] if free_tok
                else [bt])
            b1_tok = se.op("vector", lambda h: h.tensor_copy(
                out=b1w[:], in_=pC[:]), deps=[bt])
            for it in range(NT):
                pm = pds[it % 2]
                mdeps = [rhs_tok, b1_tok] + ([free_tok] if free_tok else [])
                for k in range(DK):
                    mm_last = se.op("tensor", lambda h, k=k, it=it, pm=pm: h.matmul(
                        out=pm[:], lhsT=xq_fm[:, k * RPC + it * 128:
                                              k * RPC + (it + 1) * 128],
                        rhs=big_b[:, k * 512:(k + 1) * 512],
                        start=(k == 0), stop=(k == DK - 1)), deps=mdeps)
                hb = big_f1[:, 0:512]
                cdeps = [mm_last] + ([sq_tok] if sq_tok else []) + \
                    ([dma_tok] if dma_tok else [])
                free_tok = t = se.op("vector", lambda h, pm=pm, it=it, hb=hb:
                                     h.tensor_scalar(
                    out=hb, in0=pm[:], scalar1=invs[:, it:it + 1],
                    scalar2=mclcol[:, 0:1], op0=ALU.mult, op1=ALU.mult),
                    deps=cdeps)
                t = se.op("vector", lambda h, hb=hb: h.tensor_tensor(
                    out=hb, in0=hb, in1=b1w[:], op=ALU.add), deps=[t])
                t = se.op("scalar", lambda h, hb=hb: h.activation(
                    out=big_f3[:, 0:512], in_=hb, func=AF.Tanh,
                    scale=ctx.const(0.5)), deps=[t])
                t = se.op("vector", lambda h, hb=hb: h.scalar_tensor_tensor(
                    out=hb, in0=big_f3[:, 0:512], scalar=1.0, in1=hb,
                    op0=ALU.add, op1=ALU.mult), deps=[t])
                t = se.op("vector", lambda h, hb=hb: h.tensor_scalar(
                    out=hb, in0=hb, scalar1=0.5, scalar2=None, op0=ALU.mult),
                    deps=[t])
                t = se.op("scalar", lambda h, hb=hb: h.activation(
                    out=hb, in_=hb, func=AF.Gelu, accum_out=c1[:]), deps=[t])
                t = se.op("vector", lambda h, it=it: h.tensor_tensor(
                    out=runsum[:, it:it + 1], in0=runsum[:, it:it + 1],
                    in1=c1[:], op=ALU.add), deps=[t])
                sq_tok = se.op("scalar", lambda h, hb=hb: h.activation(
                    out=big_f3[:, 0:512], in_=hb, func=AF.Square,
                    accum_out=c2[:]), deps=[t])
                t = se.op("vector", lambda h, it=it: h.tensor_tensor(
                    out=runsq[:, it:it + 1], in0=runsq[:, it:it + 1],
                    in1=c2[:], op=ALU.add), deps=[sq_tok])
                dma_tok = se.dma("sync", h_d[it * 128:(it + 1) * 128,
                                             wg * 512:(wg + 1) * 512], hb,
                                 deps=[sq_tok])
        se.op("vector", lambda h: h.engine_nop(), deps=se.fence(), close=True)

        for it in range(NT):
            se.op("vector", lambda h, it=it: h.tensor_scalar(
                out=mucol[:, it:it + 1], in0=runsum[:, it:it + 1],
                scalar1=1.0 / INNER, scalar2=None, op0=ALU.mult))
            se.op("scalar", lambda h, it=it: h.activation(
                out=c1[:], in_=mucol[:, it:it + 1], func=AF.Square))
            se.op("vector", lambda h, it=it: h.tensor_scalar(
                out=c2[:], in0=runsq[:, it:it + 1], scalar1=1.0 / INNER,
                scalar2=None, op0=ALU.mult))
            se.op("vector", lambda h: h.tensor_tensor(
                out=c2[:], in0=c2[:], in1=c1[:], op=ALU.subtract))
            se.op("scalar", lambda h: h.activation(
                out=c2[:], in_=c2[:], func=AF.Ln, bias=ctx.const(EPS)))
            se.op("scalar", lambda h, it=it: h.activation(
                out=rstd2[:, it:it + 1], in_=c2[:], func=AF.Exp,
                scale=ctx.const(-0.5)))

        for win in range(4):
            se.dma("sync", rowbuf2[:], lng_in[0:1, win * D:(win + 1) * D])
            for j in range(4):
                se.op("tensor", lambda h, j=j: h.matmul(
                    out=pC[:], lhsT=ctx.ones_row[:],
                    rhs=rowbuf2[:, 512 * j:512 * (j + 1)], start=True, stop=True))
                se.op("vector", lambda h, j=j: h.tensor_copy(
                    out=gw[:, 512 * j:512 * (j + 1)], in_=pC[:]))
            se.dma("sync", rowbuf2[:], lnb_in[0:1, win * D:(win + 1) * D])
            for j in range(4):
                se.op("tensor", lambda h, j=j: h.matmul(
                    out=pC[:], lhsT=ctx.ones_row[:],
                    rhs=rowbuf2[:, 512 * j:512 * (j + 1)], start=True, stop=True))
                se.op("vector", lambda h, j=j: h.tensor_copy(
                    out=bw[:, 512 * j:512 * (j + 1)], in_=pC[:]))
            for it in range(NT):
                se.dma("sync", big_f1[:, 0:D],
                       h_d[it * 128:(it + 1) * 128, win * D:(win + 1) * D])
                se.op("vector", lambda h, it=it: h.tensor_scalar(
                    out=big_f2[:, 0:D], in0=big_f1[:, 0:D],
                    scalar1=mucol[:, it:it + 1], scalar2=rstd2[:, it:it + 1],
                    op0=ALU.subtract, op1=ALU.mult))
                se.op("vector", lambda h: h.tensor_tensor(
                    out=big_f2[:, 0:D], in0=big_f2[:, 0:D], in1=gw[:, 0:D],
                    op=ALU.mult))
                se.op("vector", lambda h: h.tensor_tensor(
                    out=big_f2[:, 0:D], in0=big_f2[:, 0:D], in1=bw[:, 0:D],
                    op=ALU.add))
                se.op("vector", lambda h: h.reduce_max(
                    out=c1[:], in_=big_f2[:, 0:D], axis=AX.X,
                    apply_absolute_value=True))
                se.op("vector", lambda h, it=it: h.tensor_tensor(
                    out=amax2[:, it:it + 1], in0=amax2[:, it:it + 1],
                    in1=c1[:], op=ALU.max))
                se.op("scalar", lambda h: h.activation(
                    out=big_f3[:, 0:D], in_=big_f2[:, 0:D], func=AF.Square,
                    accum_out=c2[:]))
                se.op("vector", lambda h, it=it: h.tensor_tensor(
                    out=asq[:, it:it + 1], in0=asq[:, it:it + 1],
                    in1=c2[:], op=ALU.add))

        for it in range(NT):
            se.op("vector", lambda h, it=it: h.tensor_scalar(
                out=c3[:], in0=asq[:, it:it + 1], scalar1=1.0 / INNER,
                scalar2=EPS, op0=ALU.mult, op1=ALU.add))
            se.op("scalar", lambda h: h.activation(out=c3[:], in_=c3[:],
                                                   func=AF.Ln))
            se.op("scalar", lambda h: h.activation(
                out=c3[:], in_=c3[:], func=AF.Exp, scale=ctx.const(-0.5)))
            se.op("vector", lambda h, it=it: h.tensor_tensor(
                out=c1[:], in0=amax2[:, it:it + 1], in1=c3[:], op=ALU.mult))
            se.op("vector", lambda h: h.tensor_scalar_max(
                out=c1[:], in0=c1[:], scalar1=EPS))
            se.op("vector", lambda h, it=it: h.tensor_scalar(
                out=inv2c[:, it:it + 1], in0=c1[:], scalar1=1.0 / 127.0,
                scalar2=None, op0=ALU.mult))
            se.op("vector", lambda h: h.reciprocal(out=c1[:], in_=c1[:]))
            se.op("vector", lambda h, it=it: h.tensor_scalar(
                out=s2c[:, it:it + 1], in0=c1[:], scalar1=127.0,
                scalar2=None, op0=ALU.mult))
            se.op("vector", lambda h, it=it: h.tensor_tensor(
                out=comb2[:, it:it + 1], in0=s2c[:, it:it + 1], in1=c3[:],
                op=ALU.mult))

        for win in range(4):
            se.dma("sync", rowbuf2[:], lng_in[0:1, win * D:(win + 1) * D])
            for j in range(4):
                se.op("tensor", lambda h, j=j: h.matmul(
                    out=pC[:], lhsT=ctx.ones_row[:],
                    rhs=rowbuf2[:, 512 * j:512 * (j + 1)], start=True, stop=True))
                se.op("vector", lambda h, j=j: h.tensor_copy(
                    out=gw[:, 512 * j:512 * (j + 1)], in_=pC[:]))
            se.dma("sync", rowbuf2[:], lnb_in[0:1, win * D:(win + 1) * D])
            for j in range(4):
                se.op("tensor", lambda h, j=j: h.matmul(
                    out=pC[:], lhsT=ctx.ones_row[:],
                    rhs=rowbuf2[:, 512 * j:512 * (j + 1)], start=True, stop=True))
                se.op("vector", lambda h, j=j: h.tensor_copy(
                    out=bw[:, 512 * j:512 * (j + 1)], in_=pC[:]))
            for it in range(NT):
                se.dma("sync", big_f1[:, 0:D],
                       h_d[it * 128:(it + 1) * 128, win * D:(win + 1) * D])
                se.op("vector", lambda h, it=it: h.tensor_scalar(
                    out=big_f2[:, 0:D], in0=big_f1[:, 0:D],
                    scalar1=mucol[:, it:it + 1], scalar2=rstd2[:, it:it + 1],
                    op0=ALU.subtract, op1=ALU.mult))
                se.op("vector", lambda h: h.tensor_tensor(
                    out=big_f2[:, 0:D], in0=big_f2[:, 0:D], in1=gw[:, 0:D],
                    op=ALU.mult))
                se.op("vector", lambda h: h.tensor_tensor(
                    out=big_f2[:, 0:D], in0=big_f2[:, 0:D], in1=bw[:, 0:D],
                    op=ALU.add))
                se.op("vector", lambda h, it=it: h.tensor_scalar(
                    out=big_f2[:, 0:D], in0=big_f2[:, 0:D],
                    scalar1=comb2[:, it:it + 1], scalar2=None, op0=ALU.mult))
                ctx.emit_round(big_f2[:, 0:D], big_b[:, 0:D], big_i[:],
                               big_f3[:, 0:D], big_f1[:, D:2 * D])
                for k in range(DK):
                    se.op("tensor", lambda h, k=k: h.transpose(
                        out=pH[:, 0:128], in_=big_b[:, 128 * k:128 * (k + 1)],
                        identity=ident_b[:]))
                    se.op("vector", lambda h, k=k: h.tensor_copy(
                        out=hqT[:, k * 128:(k + 1) * 128], in_=pH[:, 0:128]))
                se.dma("sync",
                       hq_d[win * D:(win + 1) * D,
                            it * 128:(it + 1) * 128].rearrange(
                                "(k p) c -> p k c", p=128),
                       hqT[:].rearrange("p (k c) -> p k c", c=128))

        for dch in range(4):
            for tp in range(2):
                nk = INNER // 128
                dtoks = {}
                mtoks = {}
                base = se.last
                for k in range(nk + 1):
                    if k < nk:
                        par = k % 2
                        deps = [base] if k < 2 else [mtoks[k - 2]]
                        d1 = se.dma("sync", big_b[:, par * 512:par * 512 + 512],
                                    w2t_d[k * 128:(k + 1) * 128,
                                          dch * 512:(dch + 1) * 512],
                                    deps=deps)
                        d2 = se.dma("sync",
                                    hqT[:, par * 512:par * 512 + 512],
                                    hq_d[k * 128:(k + 1) * 128,
                                         tp * 512:(tp + 1) * 512],
                                    deps=deps)
                        dtoks[k] = d2
                    if k >= 1:
                        kk = k - 1
                        par = kk % 2
                        for tt in range(4):
                            mtoks[kk] = se.op(
                                "tensor", lambda h, kk=kk, tt=tt, par=par: h.matmul(
                                    out=pds[tt][:],
                                    lhsT=hqT[:, par * 512 + tt * 128:
                                             par * 512 + (tt + 1) * 128],
                                    rhs=big_b[:, par * 512:par * 512 + 512],
                                    start=(kk == 0), stop=(kk == nk - 1)),
                                deps=[dtoks[kk]])
                se.op("vector", lambda h: h.engine_nop(), deps=[mtoks[nk - 1]],
                      close=True)
                for tt in range(4):
                    it = tp * 4 + tt
                    ob = big_f1[:, 0:512]
                    se.op("vector", lambda h, tt=tt, it=it, ob=ob: h.tensor_scalar(
                        out=ob, in0=pds[tt][:], scalar1=inv2c[:, it:it + 1],
                        scalar2=mclcol[:, 1:2], op0=ALU.mult, op1=ALU.mult))
                    se.op("vector", lambda h, dch=dch, ob=ob: h.tensor_tensor(
                        out=ob, in0=ob,
                        in1=b2r[:, dch * 512:(dch + 1) * 512], op=ALU.add))
                    if C_DEBUG or C_DEBUG_LITE:
                        se.dma("sync", f0dbg_out[it * 128:(it + 1) * 128,
                                                 dch * 512:(dch + 1) * 512], ob)
                    se.dma("sync", big_f2[:, 0:512],
                           x_in[it * 128:(it + 1) * 128,
                                dch * 512:(dch + 1) * 512])
                    se.op("vector", lambda h, ob=ob: h.tensor_tensor(
                        out=ob, in0=ob, in1=big_f2[:, 0:512], op=ALU.add))
                    se.dma("sync", big_f2[:, 0:512],
                           at_in[it * 128:(it + 1) * 128,
                                 dch * 512:(dch + 1) * 512])
                    se.op("vector", lambda h, ob=ob: h.tensor_tensor(
                        out=ob, in0=ob, in1=big_f2[:, 0:512], op=ALU.add))
                    se.dma("sync", x1_out[it * 128:(it + 1) * 128,
                                          dch * 512:(dch + 1) * 512], ob)

        if C_DEBUG:
            dbg = ctx.sbuf("dbg_sb", [128, 64], F32)
            se.op("vector", lambda h: h.tensor_copy(out=dbg[:, 0:2], in_=swcol[:]))
            se.op("vector", lambda h: h.tensor_copy(out=dbg[:, 2:4], in_=mclcol[:]))
            se.op("vector", lambda h: h.tensor_copy(out=dbg[:, 4:12], in_=mucol[:]))
            se.op("vector", lambda h: h.tensor_copy(out=dbg[:, 12:20], in_=rstd2[:]))
            se.op("vector", lambda h: h.tensor_copy(out=dbg[:, 20:28], in_=amax2[:]))
            se.op("vector", lambda h: h.tensor_copy(out=dbg[:, 28:36], in_=s2c[:]))
            se.op("vector", lambda h: h.tensor_copy(out=dbg[:, 36:44], in_=inv2c[:]))
            se.op("vector", lambda h: h.tensor_copy(out=dbg[:, 44:52], in_=invs[:]))
            se.op("vector", lambda h: h.tensor_copy(out=dbg[:, 52:60], in_=runsum[:]))
            se.dma("sync", dbg_out[:], dbg[:])
            for it in range(NT):
                se.dma("sync", hdbg_out[it * 128:(it + 1) * 128, :],
                       h_d[it * 128:(it + 1) * 128, :])
            for kk in range(INNER // 128):
                se.dma("sync", hqdbg_out[kk * 128:(kk + 1) * 128, :],
                       hq_d[kk * 128:(kk + 1) * 128, :])

        stats = big_f3[:, 0:24]
        mv = big_f3[:, 24:26]
        nmean = c1
        rstd = c2
        for it in range(NT):
            se.dma("sync", big_f1[:, 0:D], x1_out[it * 128:(it + 1) * 128, :])
            ctx.emit_lnstats(big_f1[:, 0:D], D, stats, mv, nmean, rstd)
            se.op("vector", lambda h: h.tensor_scalar(
                out=big_f2[:, 0:D], in0=big_f1[:, 0:D], scalar1=nmean[:, 0:1],
                scalar2=rstd[:, 0:1], op0=ALU.add, op1=ALU.mult))
            se.dma("sync", xn1_out[it * 128:(it + 1) * 128, :], big_f2[:, 0:D])

        se.finalize()
    return nc


def run_C(inputs, aux, prep):
    nc = build_C()
    xn0s = shard_rows(np.asarray(aux["xn0"]))
    xs = shard_rows(inputs["L0_attn_x"])
    ats = shard_rows(np.asarray(aux["attn0"]))
    in_maps = []
    for c in range(NCORES):
        in_maps.append({
            "xn0": xn0s[c], "x": xs[c], "attn0": ats[c],
            "w1t": prep["w1t"], "w2t": prep["w2t"],
            "b1": inputs["L0_ffn_b1"].reshape(1, -1),
            "lng": inputs["L0_ffn_ln_g"].reshape(1, -1),
            "lnb": inputs["L0_ffn_ln_b"].reshape(1, -1),
            "b2": inputs["L0_ffn_b2"].reshape(1, -1),
            "mcl2": np.asarray(aux["mcl"]).reshape(1, 2),
        })
    res = run_bass_kernel_spmd(nc, in_maps, core_ids=list(range(NCORES)))
    return {
        "x1": np.concatenate([res.results[c]["x1"] for c in range(NCORES)], 0),
        "xn1": np.concatenate([res.results[c]["xn1"] for c in range(NCORES)], 0),
    }



def build_D():
    nc = bass.Bass("TRN2", target_bir_lowering=False)
    xkv_in = nc.dram_tensor("xkv", [T, D], F32, kind="ExternalInput").ap()
    wq_in = nc.dram_tensor("Wq", [D, 256], F32, kind="ExternalInput").ap()
    wk_in = nc.dram_tensor("Wk", [D, 256], F32, kind="ExternalInput").ap()
    wv_in = nc.dram_tensor("Wv", [D, 256], F32, kind="ExternalInput").ap()
    wo_in = nc.dram_tensor("Wo", [256, D], F32, kind="ExternalInput").ap()
    wg_in = nc.dram_tensor("Wg", [D, D], F32, kind="ExternalInput").ap()
    ve0_in = nc.dram_tensor("ve0", [1, 256], F32, kind="ExternalInput").ap()
    ve1_in = nc.dram_tensor("ve1", [1, 256], F32, kind="ExternalInput").ap()
    km_in = nc.dram_tensor("kmask", [1, T], F32, kind="ExternalInput").ap()

    a_out = nc.dram_tensor("attn1", [RPC, D], F32, kind="ExternalOutput").ap()
    qp_d = nc.dram_tensor("qp_d", [16, 32 * RPC], BF16, kind="Internal").ap()
    kp_d = nc.dram_tensor("kp_d", [16, 32 * T], BF16, kind="Internal").ap()
    gate_d = nc.dram_tensor("gate_d", [RPC, D], F32, kind="Internal").ap()

    with ExitStack() as stack:
        ctx = Ctx(nc, stack)
        se = ctx.se
        idn_i = ctx.sbuf("idn_i", [128, 128], F32)
        ident_b = ctx.sbuf("ident_b", [128, 128], BF16)
        wq = ctx.sbuf("wq", [128, DK * 256], BF16)
        wk = ctx.sbuf("wk", [128, DK * 256], BF16)
        wv = ctx.sbuf("wv", [128, DK * 256], BF16)
        wo = ctx.sbuf("wo", [128, 2 * D], BF16)
        wgj = ctx.sbuf("wgj", [128, DK * 512], BF16)
        xt = ctx.sbuf("xt", [128, D], F32)
        xb = ctx.sbuf("xb", [128, D], BF16)
        fmt = ctx.sbuf("fmt", [128, DK * 128], BF16)
        fm_own = ctx.sbuf("fm_own", [128, NT * D], BF16)
        kq = ctx.sbuf("kq", [128, 256], F32)
        sa = ctx.sbuf("sa", [128, 256], BF16)
        sb_ = ctx.sbuf("sb_", [128, 256], BF16)
        trp = ctx.sbuf("trp", [128, 256], BF16)
        pv_tm = ctx.sbuf("pv_tm", [128, 16 * 256], BF16)
        kmr = ctx.sbuf("kmr", [128, T], F32)
        kmrow = ctx.sbuf("kmrow", [1, T], F32)
        verow = ctx.sbuf("verow", [1, 256], F32)
        ve0r = ctx.sbuf("ve0r", [128, 256], F32)
        dver = ctx.sbuf("dver", [128, 256], F32)
        qp_h = ctx.sbuf("qp_h", [16, RPC], BF16)
        kp_h = ctx.sbuf("kp_h", [16, T], BF16)
        t1b = ctx.sbuf("t1b", [128, 512], F32)
        getile = ctx.sbuf("getile", [128, 512], F32)
        srow = ctx.sbuf("srow", [128, T], F32)
        att_b = ctx.sbuf("att_b", [128, T], BF16)
        attT = ctx.sbuf("attT", [128, 1024], BF16)
        out1 = ctx.sbuf("out1", [128, NT * 256], F32)
        rs_r = ctx.sbuf("rs_r", [128, 1], F32)
        c1 = ctx.sbuf("c1", [128, 1], F32)
        ob = ctx.sbuf("ob", [128, 256], F32)
        obb = ctx.sbuf("obb", [128, 256], BF16)
        o_fm = ctx.sbuf("o_fm", [128, 256], BF16)
        gbuf = ctx.sbuf("gbuf", [128, 512], F32)
        att_f = ctx.sbuf("att_f", [128, 512], F32)

        pH = ctx.psum("pH", [128, 1024], BF16)
        pHd = ctx.psum("pHd", [128, 1024], BF16)
        pH2d = ctx.psum("pH2d", [128, 1024], BF16)
        pP = ctx.psum("pP", [128, 512], F32)
        pQ = ctx.psum("pQ", [128, 512], F32)
        pR = ctx.psum("pR", [128, 512], F32)

        ctx.emit_init()
        make_identity(ctx, idn_i, ident_b)

        se.dma("gpsimd", wq[:].rearrange("p (k f) -> p k f", f=256),
               wq_in.rearrange("(k p) f -> p k f", p=128))
        se.dma("gpsimd", wk[:].rearrange("p (k f) -> p k f", f=256),
               wk_in.rearrange("(k p) f -> p k f", p=128))
        se.dma("gpsimd", wv[:].rearrange("p (k f) -> p k f", f=256),
               wv_in.rearrange("(k p) f -> p k f", p=128))
        se.dma("gpsimd", wo[:].rearrange("p (k f) -> p k f", f=D),
               wo_in.rearrange("(k p) f -> p k f", p=128))
        se.dma("sync", kmrow[:], km_in[:])
        for j in range(4):
            se.op("tensor", lambda h, j=j: h.matmul(
                out=pP[:], lhsT=ctx.ones_row[:],
                rhs=kmrow[:, 512 * j:512 * (j + 1)], start=True, stop=True))
            se.op("vector", lambda h, j=j: h.tensor_copy(
                out=kmr[:, 512 * j:512 * (j + 1)], in_=pP[:]))
        se.dma("sync", verow[:], ve0_in[:])
        se.op("tensor", lambda h: h.matmul(
            out=pP[:, 0:256], lhsT=ctx.ones_row[:], rhs=verow[:],
            start=True, stop=True))
        se.op("vector", lambda h: h.tensor_copy(out=ve0r[:], in_=pP[:, 0:256]))
        se.dma("sync", verow[:], ve1_in[:])
        se.op("tensor", lambda h: h.matmul(
            out=pP[:, 0:256], lhsT=ctx.ones_row[:], rhs=verow[:],
            start=True, stop=True))
        se.op("vector", lambda h: h.tensor_copy(out=dver[:], in_=pP[:, 0:256]))
        se.op("vector", lambda h: h.tensor_tensor(
            out=dver[:], in0=dver[:], in1=ve0r[:], op=ALU.subtract))

        for st_ in range(16):
            se.dma("sync", xt[:], xkv_in[st_ * 128:(st_ + 1) * 128, :])
            se.op("vector", lambda h: h.tensor_copy(out=xb[:], in_=xt[:]))
            for k in range(DK):
                se.op("tensor", lambda h, k=k: h.transpose(
                    out=pH[:, 0:128], in_=xb[:, 128 * k:128 * (k + 1)],
                    identity=ident_b[:]))
                se.op("vector", lambda h, k=k: h.tensor_copy(
                    out=fmt[:, 128 * k:128 * (k + 1)], in_=pH[:, 0:128]))
            if st_ >= 8:
                se.op("vector", lambda h, st_=st_: h.tensor_copy(
                    out=fm_own[:, (st_ - 8) * D:(st_ - 7) * D], in_=fmt[:]))
            projs = [("k", wk, kp_d, T)]
            if st_ >= 8:
                projs.append(("q", wq, qp_d, RPC))
            for nm, wsb, dst, width in projs:
                for k in range(DK):
                    se.op("tensor", lambda h, k=k, wsb=wsb: h.matmul(
                        out=pP[:, 0:256], lhsT=fmt[:, 128 * k:128 * (k + 1)],
                        rhs=wsb[:, 256 * k:256 * (k + 1)],
                        start=(k == 0), stop=(k == DK - 1)))
                se.op("vector", lambda h: h.tensor_copy(out=kq[:], in_=pP[:, 0:256]))
                se.op("scalar", lambda h: h.activation(
                    out=sa[:], in_=kq[:], func=AF.Sigmoid))
                se.op("scalar", lambda h: h.activation(
                    out=sb_[:], in_=kq[:], func=AF.Sigmoid,
                    scale=ctx.const(-1.0)))
                soff = st_ * 128 if nm == "k" else (st_ - 8) * 128
                for sgn, src in ((0, sa), (8, sb_)):
                    for j in range(2):
                        se.op("tensor", lambda h, j=j, src=src: h.transpose(
                            out=pH[:, 0:128], in_=src[:, 128 * j:128 * (j + 1)],
                            identity=ident_b[:]))
                        se.op("vector", lambda h, j=j: h.tensor_copy(
                            out=trp[:, 128 * j:128 * (j + 1)], in_=pH[:, 0:128]))
                    for j in range(2):
                        dview = dst.rearrange("r (j2 g s) -> j2 g r s",
                                              j2=2, g=16)[
                            j, :, sgn:sgn + 8, soff:soff + 128]
                        se.dma("sync", dview,
                               trp[:, 128 * j:128 * (j + 1)])
            for k in range(DK):
                se.op("tensor", lambda h, k=k: h.matmul(
                    out=pP[:, 0:256], lhsT=fmt[:, 128 * k:128 * (k + 1)],
                    rhs=wv[:, 256 * k:256 * (k + 1)],
                    start=(k == 0), stop=(k == DK - 1)))
            se.op("scalar", lambda h, st_=st_: h.activation(
                out=pv_tm[:, 256 * st_:256 * (st_ + 1)], in_=pP[:, 0:256],
                func=AF.Sigmoid))

        for jc in range(4):
            se.dma("gpsimd", wgj[:].rearrange("p (k f) -> p k f", f=512),
                   wg_in[:, jc * 512:(jc + 1) * 512].rearrange(
                       "(k p) f -> p k f", p=128))
            for it in range(NT):
                for k in range(DK):
                    se.op("tensor", lambda h, k=k, it=it: h.matmul(
                        out=pQ[:], lhsT=fm_own[:, it * D + 128 * k:
                                               it * D + 128 * (k + 1)],
                        rhs=wgj[:, 512 * k:512 * (k + 1)],
                        start=(k == 0), stop=(k == DK - 1)))
                se.op("scalar", lambda h: h.activation(
                    out=gbuf[:], in_=pQ[:], func=AF.Sigmoid))
                se.dma("sync", gate_d[it * 128:(it + 1) * 128,
                                      jc * 512:(jc + 1) * 512], gbuf[:])

        se.op("gpsimd", lambda h: h.iota(
            t1b[:], pattern=[[-1, 512]], base=0, channel_multiplier=1,
            allow_small_or_imprecise_dtypes=True))

        for hh in range(32):
            se.dma("sync", qp_h[:], qp_d[:, hh * RPC:(hh + 1) * RPC])
            se.dma("sync", kp_h[:], kp_d[:, hh * T:(hh + 1) * T])
            amask_tok = None
            for qt in range(NT):
                t0g = 1024 + qt * 128
                chd = t0g // 512
                base0 = se.last
                d0g = float(t0g - chd * 512)
                a1 = se.op("vector", lambda h, d0g=d0g: h.tensor_scalar(
                    out=getile[:], in0=t1b[:], scalar1=d0g,
                    scalar2=0.0, op0=ALU.add, op1=ALU.is_ge), deps=[base0])
                a2 = se.op("vector", lambda h: h.tensor_scalar(
                    out=getile[:], in0=getile[:], scalar1=1e30,
                    scalar2=-1e30, op0=ALU.mult, op1=ALU.add), deps=[a1])
                amask_tok = se.op("vector", lambda h, s0=chd * 512:
                                  h.tensor_tensor(
                    out=getile[:], in0=getile[:],
                    in1=kmr[:, s0:s0 + 512], op=ALU.add), deps=[a2])
                ev_toks = []
                mm_tok = {}
                for ch in range(chd + 1):
                    s0 = ch * 512
                    pp = pP if ch % 2 == 0 else pQ
                    mdeps = [base0]
                    if ch >= 2:
                        mdeps = [ev_toks[ch - 2]]
                    mm_tok[ch] = se.op("tensor", lambda h, qt=qt, ch=ch,
                                       pp=pp: h.matmul(
                        out=pp[:], lhsT=qp_h[:, qt * 128:(qt + 1) * 128],
                        rhs=kp_h[:, ch * 512:(ch + 1) * 512],
                        start=True, stop=True), deps=mdeps)
                    if ch == chd:
                        ev = se.op("vector", lambda h, s0=s0, pp=pp:
                                   h.tensor_tensor(
                            out=srow[:, s0:s0 + 512], in0=pp[:],
                            in1=getile[:], op=ALU.add),
                            deps=[mm_tok[ch], amask_tok])
                    else:
                        ev = se.op("vector", lambda h, s0=s0, pp=pp:
                                   h.tensor_tensor(
                            out=srow[:, s0:s0 + 512], in0=pp[:],
                            in1=kmr[:, s0:s0 + 512], op=ALU.add),
                            deps=[mm_tok[ch]])
                    ev_toks.append(ev)
                se.op("vector", lambda h: h.engine_nop(), deps=[ev_toks[-1]],
                      close=True)
                used = (chd + 1) * 512
                se.op("scalar", lambda h, used=used: h.activation(
                    out=att_b[:, 0:used], in_=srow[:, 0:used], func=AF.Exp,
                    bias=ctx.const(-16.0), accum_out=rs_r[:]))
                se.op("vector", lambda h: h.reciprocal(out=rs_r[:], in_=rs_r[:]))
                nst = used // 128
                base = se.last
                ttoks = []
                etoks = {}
                mtoks = []
                for j in range(nst):
                    g = (j // 4) % 2
                    ph = pHd if g == 0 else pH2d
                    jj = j % 4
                    deps = [base]
                    if (j // 4) >= 2:
                        deps.append(etoks[(j // 4) - 2])
                    lastj = (jj == 3 or j == nst - 1)
                    t_tok = se.op("tensor", lambda h, j=j, jj=jj, ph=ph,
                                  lastj=lastj: h.matmul(
                        out=ph[:, jj * 128:(jj + 1) * 128],
                        lhsT=att_b[:, 128 * j:128 * (j + 1)],
                        rhs=ident_b[:], is_transpose=True,
                        start=(jj == 0), stop=lastj), deps=deps)
                    ttoks.append(t_tok)
                    if jj == 3 or j == nst - 1:
                        grp = j // 4
                        e_tok = se.op("vector", lambda h, g=g, ph=ph: h.tensor_copy(
                            out=attT[:, g * 512:(g + 1) * 512],
                            in_=ph[:, 0:512]),
                            deps=[t_tok])
                        etoks[grp] = e_tok
                        for j2 in range(grp * 4, min(grp * 4 + 4, nst)):
                            m_tok = se.op("tensor", lambda h, j2=j2, g=g,
                                          hh=hh, nst=nst: h.matmul(
                                out=pR[:, 0:8],
                                lhsT=attT[:, g * 512 + (j2 % 4) * 128:
                                          g * 512 + (j2 % 4 + 1) * 128],
                                rhs=pv_tm[:, j2 * 256 + hh * 8:
                                          j2 * 256 + hh * 8 + 8],
                                start=(j2 == 0), stop=(j2 == nst - 1)),
                                deps=[e_tok])
                            mtoks.append(m_tok)
                se.op("vector", lambda h, qt=qt, hh=hh: h.tensor_scalar(
                    out=out1[:, qt * 256 + hh * 8:qt * 256 + hh * 8 + 8],
                    in0=pR[:, 0:8], scalar1=rs_r[:, 0:1], scalar2=None,
                    op0=ALU.mult), deps=[mtoks[-1]], close=True)

        for qt in range(NT):
            se.op("vector", lambda h, qt=qt: h.tensor_tensor(
                out=ob[:], in0=out1[:, qt * 256:(qt + 1) * 256], in1=dver[:],
                op=ALU.mult))
            se.op("vector", lambda h: h.tensor_tensor(
                out=ob[:], in0=ob[:], in1=ve0r[:], op=ALU.add))
            se.op("vector", lambda h: h.tensor_copy(out=obb[:], in_=ob[:]))
            for j in range(2):
                se.op("tensor", lambda h, j=j: h.transpose(
                    out=pH[:, 0:128], in_=obb[:, 128 * j:128 * (j + 1)],
                    identity=ident_b[:]))
                se.op("vector", lambda h, j=j: h.tensor_copy(
                    out=o_fm[:, 128 * j:128 * (j + 1)], in_=pH[:, 0:128]))
            for jc in range(4):
                for k in range(2):
                    se.op("tensor", lambda h, jc=jc, k=k: h.matmul(
                        out=pQ[:], lhsT=o_fm[:, 128 * k:128 * (k + 1)],
                        rhs=wo[:, k * D + jc * 512:k * D + jc * 512 + 512],
                        start=(k == 0), stop=(k == 1)))
                se.dma("sync", gbuf[:], gate_d[qt * 128:(qt + 1) * 128,
                                               jc * 512:(jc + 1) * 512])
                se.op("vector", lambda h: h.tensor_tensor(
                    out=att_f[:], in0=pQ[:], in1=gbuf[:], op=ALU.mult))
                se.dma("sync", a_out[qt * 128:(qt + 1) * 128,
                                     jc * 512:(jc + 1) * 512], att_f[:])

        se.finalize()
    return nc


def run_D(inputs, aux):
    nc = build_D()
    xn1 = np.asarray(aux["xn1"]).reshape(B, T, D)
    in_maps = []
    for c in range(NCORES):
        b, half = c // 2, c % 2
        if half == 1:
            xkv = xn1[b]
            km = np.zeros((1, T), np.float32)
        else:
            xkv = np.concatenate([np.zeros_like(xn1[b][0:1024]),
                                  xn1[b][0:1024]], 0)
            km = np.zeros((1, T), np.float32)
            km[0, 0:1024] = -1e30
        in_maps.append({
            "xkv": np.ascontiguousarray(xkv), "kmask": km,
            "Wq": np.ascontiguousarray(inputs["L1_attn_Wq"]),
            "Wk": np.ascontiguousarray(inputs["L1_attn_Wk"]),
            "Wv": np.ascontiguousarray(inputs["L1_attn_Wv"]),
            "Wo": np.ascontiguousarray(inputs["L1_attn_Wo"]),
            "Wg": np.ascontiguousarray(inputs["L1_attn_Wg"]),
            "ve0": inputs["L1_attn_v_emb0"].reshape(1, -1),
            "ve1": inputs["L1_attn_v_emb1"].reshape(1, -1),
        })
    res = run_bass_kernel_spmd(nc, in_maps, core_ids=list(range(NCORES)))
    return np.concatenate([res.results[c]["attn1"] for c in range(NCORES)], 0)



def emit_mx_quant(ctx, t_ap, out_ap, W, m1, m2, m3, m4, m5, mi, bmax, sb, rs):
    se = ctx.se
    nb = W // 32
    se.op("vector", lambda h: h.reduce_max(
        out=bmax, in_=t_ap.rearrange("p (w k) -> p w k", k=32), axis=AX.X,
        apply_absolute_value=True))
    se.op("vector", lambda h: h.tensor_scalar(
        out=sb, in0=bmax, scalar1=1.0 / 448.0, scalar2=1e-30,
        op0=ALU.mult, op1=ALU.max))
    se.op("vector", lambda h: h.reciprocal(out=rs, in_=sb))
    t3 = t_ap.rearrange("p (w k) -> p w k", k=32)
    rs3 = rs.rearrange("p (w k) -> p w k", k=1)
    m13 = m1.rearrange("p (w k) -> p w k", k=32)
    a3b, b3b = bass.broadcast_tensor_aps(t3, rs3)
    se.op("vector", lambda h: h.tensor_tensor(out=m13, in0=a3b, in1=b3b,
                                              op=ALU.mult))
    se.op("scalar", lambda h: h.activation(out=m2, in_=m1, func=AF.Square))
    se.op("vector", lambda h: h.tensor_scalar_max(out=m2, in0=m2,
                                                  scalar1=1e-38))
    se.op("scalar", lambda h: h.activation(out=m2, in_=m2, func=AF.Ln))
    se.op("vector", lambda h: h.tensor_scalar(
        out=m2, in0=m2, scalar1=0.5 / LN2, scalar2=20.0,
        op0=ALU.mult, op1=ALU.add))
    se.op("vector", lambda h: h.tensor_copy(out=mi, in_=m2))
    se.op("vector", lambda h: h.tensor_tensor(out=m3, in0=m2, in1=mi,
                                              op=ALU.subtract))
    se.op("vector", lambda h: h.tensor_scalar(
        out=m3, in0=m3, scalar1=0.0, scalar2=None, op0=ALU.is_lt))
    se.op("vector", lambda h: h.scalar_tensor_tensor(
        out=m2, in0=m3, scalar=-1.0, in1=mi, op0=ALU.mult, op1=ALU.add))
    se.op("vector", lambda h: h.tensor_scalar_max(out=m2, in0=m2,
                                                  scalar1=14.0))
    se.op("scalar", lambda h: h.activation(
        out=m3, in_=m2, func=AF.Exp, scale=ctx.const(-LN2),
        bias=ctx.const(23.0 * LN2)))
    se.op("vector", lambda h: h.tensor_tensor(out=m1, in0=m1, in1=m3,
                                              op=ALU.mult))
    ctx.emit_round(m1, m5, mi, m4, m3)
    se.op("scalar", lambda h: h.activation(
        out=m3, in_=m2, func=AF.Exp, scale=ctx.const(LN2),
        bias=ctx.const(-23.0 * LN2)))
    se.op("vector", lambda h: h.tensor_tensor(out=m1, in0=m5, in1=m3,
                                              op=ALU.mult))
    m13b = m1.rearrange("p (w k) -> p w k", k=32)
    sb3 = sb.rearrange("p (w k) -> p w k", k=1)
    o3 = out_ap.rearrange("p (w k) -> p w k", k=32)
    a3c, b3c = bass.broadcast_tensor_aps(m13b, sb3)
    se.op("vector", lambda h: h.tensor_tensor(out=o3, in0=a3c, in1=b3c,
                                              op=ALU.mult))


def build_E():
    nc = bass.Bass("TRN2", target_bir_lowering=False)
    xn1_in = nc.dram_tensor("xn1", [RPC, D], F32, kind="ExternalInput").ap()
    x1_in = nc.dram_tensor("x1", [RPC, D], F32, kind="ExternalInput").ap()
    a1_in = nc.dram_tensor("attn1", [RPC, D], F32, kind="ExternalInput").ap()
    gq_t = nc.dram_tensor("gq_t", [D, INNER], BF16, kind="ExternalInput").ap()
    uq_t = nc.dram_tensor("uq_t", [D, INNER], BF16, kind="ExternalInput").ap()
    dq_t = nc.dram_tensor("dq_t", [INNER, D], BF16, kind="ExternalInput").ap()

    out_o = nc.dram_tensor("out", [RPC, D], F32, kind="ExternalOutput").ap()
    hq_d2 = nc.dram_tensor("hq_d2", [INNER, RPC], BF16, kind="Internal").ap()

    with ExitStack() as stack:
        ctx = Ctx(nc, stack)
        se = ctx.se
        idn_i = ctx.sbuf("idn_i", [128, 128], F32)
        ident_b = ctx.sbuf("ident_b", [128, 128], BF16)
        xq_fm = ctx.sbuf("xq_fm", [128, DK * RPC], BF16)
        wld = ctx.sbuf("wld", [128, INNER], F32)
        m1 = ctx.sbuf("m1", [128, D], F32)
        m2 = ctx.sbuf("m2", [128, D], F32)
        m3 = ctx.sbuf("m3", [128, D], F32)
        m4 = ctx.sbuf("m4", [128, D], F32)
        m5 = ctx.sbuf("m5", [128, D], F32)
        mi = ctx.sbuf("mi", [128, D], I32)
        bmax = ctx.sbuf("bmax", [128, 64], F32)
        sbs = ctx.sbuf("sbs", [128, 64], F32)
        rss = ctx.sbuf("rss", [128, 64], F32)
        qb = ctx.sbuf("qb", [128, D], BF16)
        grhs = ctx.sbuf("grhs", [128, INNER], BF16)
        urhs = ctx.sbuf("urhs", [128, INNER], BF16)
        hbuf = ctx.sbuf("hbuf", [128, 512], F32)
        hbuf4 = ctx.sbuf("hbuf4", [128, 2048], F32)
        qb4 = ctx.sbuf("qb4", [128, 2048], BF16)
        stg16 = ctx.sbuf("stg16", [128, 2048], BF16)
        hqv = ctx.sbuf("hqv", [128, 512], BF16)
        stg4 = ctx.sbuf("stg4", [128, 512], BF16)
        hql2 = ctx.sbuf("hql2", [128, 1024], BF16)
        resb = ctx.sbuf("resb", [128, 512], F32)
        ob = ctx.sbuf("ob", [128, 512], F32)

        pA = ctx.psum("pA", [128, 512], F32)
        pB = ctx.psum("pB", [128, 512], F32)
        pC = ctx.psum("pC", [128, 512], F32)
        pD = ctx.psum("pD", [128, 512], F32)
        pH = ctx.psum("pH", [128, 1024], BF16)
        pH2 = ctx.psum("pH2", [128, 1024], BF16)
        pds = [pA, pB, pC, pD]

        ctx.emit_init()
        make_identity(ctx, idn_i, ident_b)

        for it in range(NT):
            se.dma("sync", m1[:], xn1_in[it * 128:(it + 1) * 128, :])
            se.op("vector", lambda h: h.tensor_copy(out=m4[:], in_=m1[:]))
            emit_mx_quant(ctx, m4[:], qb[:], D, m1[:], m2[:], m3[:], m4[:],
                          m5[:], mi[:], bmax[:], sbs[:], rss[:])
            for k in range(DK):
                se.op("tensor", lambda h, k=k: h.transpose(
                    out=pH[:, 0:128], in_=qb[:, 128 * k:128 * (k + 1)],
                    identity=ident_b[:]))
                se.op("vector", lambda h, k=k, it=it: h.tensor_copy(
                    out=xq_fm[:, k * RPC + it * 128:k * RPC + (it + 1) * 128],
                    in_=pH[:, 0:128]))

        free_tok = None
        tail_tok = None
        store_tok = None
        mm_last = None
        W = 4 * 512
        nb = W // 32
        for wg in range(16):
            rdeps = [se.last] if wg == 0 else [mm_last]
            se.dma("sync", grhs[:].rearrange("p (k c) -> p k c", c=512),
                   gq_t.rearrange("(k p) c -> p k c", p=128)[
                       :, :, wg * 512:(wg + 1) * 512], deps=rdeps)
            rhs_tok = se.dma("sync", urhs[:].rearrange("p (k c) -> p k c", c=512),
                             uq_t.rearrange("(k p) c -> p k c", p=128)[
                                 :, :, wg * 512:(wg + 1) * 512], deps=rdeps)
            for g4 in range(2):
                hm_tok = None
                for i4 in range(4):
                    it = g4 * 4 + i4
                    mdeps = [rhs_tok] + ([free_tok] if free_tok else [])
                    for k in range(DK):
                        se.op("tensor", lambda h, k=k, it=it: h.matmul(
                            out=pA[:], lhsT=xq_fm[:, k * RPC + it * 128:
                                                  k * RPC + (it + 1) * 128],
                            rhs=grhs[:, 512 * k:512 * (k + 1)],
                            start=(k == 0), stop=(k == DK - 1)), deps=mdeps)
                    for k in range(DK):
                        mm_last = se.op("tensor", lambda h, k=k, it=it: h.matmul(
                            out=pB[:], lhsT=xq_fm[:, k * RPC + it * 128:
                                                  k * RPC + (it + 1) * 128],
                            rhs=urhs[:, 512 * k:512 * (k + 1)],
                            start=(k == 0), stop=(k == DK - 1)), deps=mdeps)
                    sl = slice(i4 * 512, (i4 + 1) * 512)
                    cdeps = [mm_last] + ([tail_tok] if tail_tok else [])
                    t = se.op("scalar", lambda h, sl=sl: h.activation(
                        out=hbuf4[:, sl], in_=pA[:], func=AF.Exp,
                        scale=ctx.const(-1.0)), deps=cdeps)
                    t = se.op("vector", lambda h, sl=sl: h.tensor_scalar(
                        out=hbuf4[:, sl], in0=hbuf4[:, sl], scalar1=1.0,
                        scalar2=None, op0=ALU.add), deps=[t])
                    t = se.op("vector", lambda h, sl=sl: h.reciprocal(
                        out=hbuf4[:, sl], in_=hbuf4[:, sl]), deps=[t])
                    t = se.op("vector", lambda h, sl=sl: h.tensor_tensor(
                        out=hbuf4[:, sl], in0=hbuf4[:, sl], in1=pA[:],
                        op=ALU.mult), deps=[t])
                    free_tok = t = se.op("vector", lambda h, sl=sl: h.tensor_tensor(
                        out=hbuf4[:, sl], in0=hbuf4[:, sl], in1=pB[:],
                        op=ALU.mult), deps=[t])
                    hm_tok = t
                t = hm_tok
                t = se.op("vector", lambda h: h.reduce_max(
                    out=bmax[:, 0:nb],
                    in_=hbuf4[:].rearrange("p (w k) -> p w k", k=32),
                    axis=AX.X, apply_absolute_value=True), deps=[t])
                t = se.op("vector", lambda h: h.tensor_scalar(
                    out=sbs[:, 0:nb], in0=bmax[:, 0:nb], scalar1=1.0 / 448.0,
                    scalar2=1e-30, op0=ALU.mult, op1=ALU.max), deps=[t])
                t = se.op("vector", lambda h: h.reciprocal(
                    out=rss[:, 0:nb], in_=sbs[:, 0:nb]), deps=[t])
                t3 = hbuf4[:].rearrange("p (w k) -> p w k", k=32)
                rs3 = rss[:, 0:nb].rearrange("p (w k) -> p w k", k=1)
                m13 = m1[:].rearrange("p (w k) -> p w k", k=32)
                a3b, b3b = bass.broadcast_tensor_aps(t3, rs3)
                t = se.op("vector", lambda h: h.tensor_tensor(
                    out=m13, in0=a3b, in1=b3b, op=ALU.mult), deps=[t])
                t = se.op("scalar", lambda h: h.activation(
                    out=m2[:], in_=m1[:], func=AF.Square), deps=[t])
                t = se.op("vector", lambda h: h.tensor_scalar_max(
                    out=m2[:], in0=m2[:], scalar1=1e-38), deps=[t])
                t = se.op("scalar", lambda h: h.activation(
                    out=m2[:], in_=m2[:], func=AF.Ln), deps=[t])
                t = se.op("vector", lambda h: h.tensor_scalar(
                    out=m2[:], in0=m2[:], scalar1=0.5 / LN2, scalar2=20.0,
                    op0=ALU.mult, op1=ALU.add), deps=[t])
                t = se.op("vector", lambda h: h.tensor_copy(
                    out=mi[:], in_=m2[:]), deps=[t])
                t = se.op("vector", lambda h: h.tensor_tensor(
                    out=m3[:], in0=m2[:], in1=mi[:], op=ALU.subtract), deps=[t])
                t = se.op("vector", lambda h: h.tensor_scalar(
                    out=m3[:], in0=m3[:], scalar1=0.0, scalar2=None,
                    op0=ALU.is_lt), deps=[t])
                t = se.op("vector", lambda h: h.scalar_tensor_tensor(
                    out=m2[:], in0=m3[:], scalar=-1.0, in1=mi[:],
                    op0=ALU.mult, op1=ALU.add), deps=[t])
                t = se.op("vector", lambda h: h.tensor_scalar_max(
                    out=m2[:], in0=m2[:], scalar1=14.0), deps=[t])
                t = se.op("scalar", lambda h: h.activation(
                    out=m3[:], in_=m2[:], func=AF.Exp, scale=ctx.const(-LN2),
                    bias=ctx.const(23.0 * LN2)), deps=[t])
                t = se.op("vector", lambda h: h.tensor_tensor(
                    out=m1[:], in0=m1[:], in1=m3[:], op=ALU.mult), deps=[t])
                t = se.op("vector", lambda h: h.tensor_copy(
                    out=mi[:], in_=m1[:]), deps=[t])
                t = se.op("vector", lambda h: h.tensor_tensor(
                    out=m4[:], in0=m1[:], in1=mi[:], op=ALU.subtract), deps=[t])
                t = se.op("vector", lambda h: h.tensor_scalar(
                    out=m3[:], in0=m4[:], scalar1=-0.5, scalar2=None,
                    op0=ALU.is_lt), deps=[t])
                t = se.op("vector", lambda h: h.scalar_tensor_tensor(
                    out=m4[:], in0=m4[:], scalar=0.5, in1=m3[:],
                    op0=ALU.is_gt, op1=ALU.subtract), deps=[t])
                t = se.op("vector", lambda h: h.tensor_tensor(
                    out=m5[:], in0=mi[:], in1=m4[:], op=ALU.add), deps=[t])
                t = se.op("scalar", lambda h: h.activation(
                    out=m3[:], in_=m2[:], func=AF.Exp, scale=ctx.const(LN2),
                    bias=ctx.const(-23.0 * LN2)), deps=[t])
                t = se.op("vector", lambda h: h.tensor_tensor(
                    out=m1[:], in0=m5[:], in1=m3[:], op=ALU.mult), deps=[t])
                m13b = m1[:].rearrange("p (w k) -> p w k", k=32)
                sb3 = sbs[:, 0:nb].rearrange("p (w k) -> p w k", k=1)
                o3 = qb4[:].rearrange("p (w k) -> p w k", k=32)
                a3c, b3c = bass.broadcast_tensor_aps(m13b, sb3)
                t = se.op("vector", lambda h: h.tensor_tensor(
                    out=o3, in0=a3c, in1=b3c, op=ALU.mult), deps=[t])
                tail_tok = t
                ttoks = []
                for half in range(2):
                    ph = pH if half == 0 else pH2
                    tt0 = None
                    for j in range(8):
                        idx = half * 8 + j
                        it4, kk4 = divmod(idx, 4)
                        src = qb4[:, it4 * 512 + kk4 * 128:
                                  it4 * 512 + (kk4 + 1) * 128]
                        prevev = [ttoks[-2]] if len(ttoks) >= 2 else []
                        tt0 = se.op("tensor", lambda h, j=j, src=src, ph=ph:
                                    h.matmul(
                            out=ph[:, j * 128:(j + 1) * 128], lhsT=src,
                            rhs=ident_b[:], is_transpose=True,
                            start=(j == 0), stop=(j == 7)),
                            deps=[t] + prevev)
                    evd = [tt0] + ([store_tok] if store_tok else [])
                    ev = se.op("vector", lambda h, half=half, ph=ph:
                               h.tensor_copy(
                        out=stg16[:, half * 1024:(half + 1) * 1024],
                        in_=ph[:, 0:1024]), deps=evd)
                    ttoks.append(ev)
                for i4 in range(4):
                    itg = g4 * 4 + i4
                    store_tok = se.dma(
                        "sync",
                        hq_d2[wg * 512:(wg + 1) * 512,
                              itg * 128:(itg + 1) * 128].rearrange(
                            "(k p) c -> p k c", p=128),
                        stg16[:, i4 * 512:(i4 + 1) * 512].rearrange(
                            "p (k c) -> p k c", c=128),
                        deps=ttoks)
        se.op("vector", lambda h: h.engine_nop(), deps=se.fence(), close=True)

        for dch in range(4):
            for tp in range(2):
                nk = INNER // 128
                dtoks = {}
                mtoks = {}
                base = se.last
                for k in range(nk + 1):
                    if k < nk:
                        par = k % 2
                        deps = [base] if k < 2 else [mtoks[k - 2]]
                        se.dma("sync", qb[:, par * 512:par * 512 + 512],
                               dq_t[k * 128:(k + 1) * 128,
                                    dch * 512:(dch + 1) * 512], deps=deps)
                        dtoks[k] = se.dma(
                            "sync", hql2[:, par * 512:par * 512 + 512],
                            hq_d2[k * 128:(k + 1) * 128,
                                  tp * 512:(tp + 1) * 512], deps=deps)
                    if k >= 1:
                        kk = k - 1
                        par = kk % 2
                        for tt in range(4):
                            mtoks[kk] = se.op(
                                "tensor", lambda h, kk=kk, tt=tt, par=par: h.matmul(
                                    out=pds[tt][:],
                                    lhsT=hql2[:, par * 512 + tt * 128:
                                              par * 512 + (tt + 1) * 128],
                                    rhs=qb[:, par * 512:par * 512 + 512],
                                    start=(kk == 0), stop=(kk == nk - 1)),
                                deps=[dtoks[kk]])
                se.op("vector", lambda h: h.engine_nop(), deps=[mtoks[nk - 1]],
                      close=True)
                for tt in range(4):
                    it = tp * 4 + tt
                    se.dma("sync", resb[:],
                           x1_in[it * 128:(it + 1) * 128,
                                 dch * 512:(dch + 1) * 512])
                    se.op("vector", lambda h, tt=tt: h.tensor_tensor(
                        out=ob[:], in0=pds[tt][:], in1=resb[:], op=ALU.add))
                    se.dma("sync", resb[:],
                           a1_in[it * 128:(it + 1) * 128,
                                 dch * 512:(dch + 1) * 512])
                    se.op("vector", lambda h: h.tensor_tensor(
                        out=ob[:], in0=ob[:], in1=resb[:], op=ALU.add))
                    se.dma("sync", out_o[it * 128:(it + 1) * 128,
                                         dch * 512:(dch + 1) * 512], ob[:])

        se.finalize()
    return nc


def run_E(inputs, aux, prep):
    nc = build_E()
    xn1s = shard_rows(np.asarray(aux["xn1"]))
    x1s = shard_rows(np.asarray(aux["x1"]))
    a1s = shard_rows(np.asarray(aux["attn1"]))
    in_maps = []
    for c in range(NCORES):
        in_maps.append({
            "xn1": xn1s[c], "x1": x1s[c], "attn1": a1s[c],
            "gq_t": prep["gq"], "uq_t": prep["uq"], "dq_t": prep["dq"],
        })
    res = run_bass_kernel_spmd(nc, in_maps, core_ids=list(range(NCORES)))
    return np.concatenate([res.results[c]["out"] for c in range(NCORES)], 0)


def kernel(**inputs):
    aux = run_A(inputs)
    prep = run_P(inputs, aux)
    aux["attn0"] = run_B(inputs, aux)
    auxC = run_C(inputs, aux, prep)
    aux.update(auxC)
    aux["attn1"] = run_D(inputs, aux)
    out = run_E(inputs, aux, prep)
    return out.reshape(B, T, D)


# version 08
# speedup vs baseline: 1.1814x; 1.1814x over previous
import numpy as np
import ml_dtypes
from contextlib import ExitStack

import concourse.bass as bass
import concourse.mybir as mybir
from concourse.bass_utils import run_bass_kernel_spmd

F32 = mybir.dt.float32
BF16 = mybir.dt.bfloat16
I32 = mybir.dt.int32
AF = mybir.ActivationFunctionType
ALU = mybir.AluOpType
AX = mybir.AxisListType

B, T, D = 4, 2048, 2048
B_DEBUG = False
B_DEBUG_QT = 7
C_DEBUG = False
C_DEBUG_LITE = False
C_SCRATCH_INTERNAL = True
NCORES = 8
RPC = (B * T) // NCORES
NT = RPC // 128
DK = D // 128
INNER = 8192
EPS = 1e-5
LN2 = float(np.log(2.0))

CONST_VALS = sorted(set([EPS, -0.5, 0.5, -1.0, 1.0, 2.0, LN2, -LN2, 1.0 / LN2, 1e-38,
                         0.5 / LN2, 127.0, 1.0 / 127.0, 1e-30, -1e30, 1.0 / 448.0,
                         20.0, 14.0, -16.0, 23.0 * LN2, -23.0 * LN2, 1.0 / 2048.0, 1.0 / 8192.0,
                         1.0 / (INNER * D), 1.0 / INNER]))

ENGINES = ("sync", "vector", "scalar", "tensor", "gpsimd")


class SE:

    def __init__(self, nc, stack):
        self.nc = nc
        self.stack = stack
        self.ops = {e: [] for e in ENGINES}
        self.count = {e: 0 for e in ENGINES}
        self.dmacount = {e: 0 for e in ENGINES}
        self.esem = {e: stack.enter_context(nc.semaphore(f"es_{e}"))
                     for e in ENGINES}
        self.dsem = {e: stack.enter_context(nc.semaphore(f"ds_{e}"))
                     for e in ENGINES}
        self.last = None
        self.extra_sem = {}

    def _waits(self, eng):
        w = []
        if self.last is None:
            return w
        kind = self.last[0]
        if kind == "dma":
            _, deng, dcount = self.last
            w.append((self.dsem[deng], 16 * dcount))
        else:
            _, peng, pcount = self.last
            w.append((self.esem[peng], pcount))
        return w

    def _dep_waits(self, eng, deps):
        w = []
        if self.count[eng] > 0:
            w.append((self.esem[eng], self.count[eng]))
        seen = {}
        for d in deps:
            if d is None:
                continue
            kind = d[0]
            if kind == "dma":
                _, deng, dcount = d
                key = ("d", deng)
                seen[key] = max(seen.get(key, 0), 16 * dcount)
            else:
                _, peng, pcount = d
                if peng == eng:
                    continue
                key = ("e", peng)
                seen[key] = max(seen.get(key, 0), pcount)
        for (k, e2), v in seen.items():
            w.append(((self.dsem if k == "d" else self.esem)[e2], v))
        return w

    def op(self, eng, fn, deps=None, close=False):
        w = self._dep_waits(eng, deps) if deps is not None else self._waits(eng)
        self.count[eng] += 1
        n = self.count[eng]
        sem = self.esem[eng]

        def run(handle, w=w, fn=fn, sem=sem, n=n):
            for s, v in w:
                handle.wait_ge(s, v)
            inst = fn(handle)
            inst.then_inc(sem, 1)
        self.ops[eng].append(run)
        tok = ("eng", eng, n)
        if deps is None or close:
            self.last = tok
        return tok

    def dma(self, eng, out, in_, deps=None, close=False, **kw):
        w = self._dep_waits(eng, deps) if deps is not None else self._waits(eng)
        self.dmacount[eng] += 1
        n = self.dmacount[eng]
        sem = self.dsem[eng]

        def run(handle, w=w, out=out, in_=in_, sem=sem, kw=kw):
            for s, v in w:
                handle.wait_ge(s, v)
            handle.dma_start(out=out, in_=in_, **kw).then_inc(sem, 16)
        self.ops[eng].append(run)
        tok = ("dma", eng, n)
        if deps is None or close:
            self.last = tok
        return tok

    def fence(self):
        toks = []
        for e in ENGINES:
            if self.count[e]:
                toks.append(("eng", e, self.count[e]))
            if self.dmacount[e]:
                toks.append(("dma", e, self.dmacount[e]))
        return toks

    def barrier_wait(self, eng):
        snap = dict(self.count), dict(self.dmacount)

        def run(handle, snap=snap):
            counts, dcounts = snap
            for e in ENGINES:
                if counts[e] and e != eng:
                    handle.wait_ge(self.esem[e], counts[e])
                if dcounts[e]:
                    handle.wait_ge(self.dsem[e], 16 * dcounts[e])
        self.ops[eng].append(run)

    def finalize(self):
        nc = self.nc
        with nc.Block() as block:
            for e in ENGINES:
                lst = self.ops[e]
                if not lst and e != "sync":
                    continue

                def body(handle, lst=lst, e=e):
                    for fn in lst:
                        fn(handle)
                    if e == "sync":
                        for e2 in ENGINES:
                            if self.dmacount[e2]:
                                handle.wait_ge(self.dsem[e2],
                                               16 * self.dmacount[e2])
                            if self.count[e2] and e2 != "sync":
                                handle.wait_ge(self.esem[e2], self.count[e2])
                getattr(block, e)(body)


class Ctx:
    def __init__(self, nc, stack):
        self.nc = nc
        self.stack = stack
        self.se = SE(nc, stack)
        self.cpool = self.sbuf("cpool", [128, len(CONST_VALS)], F32)
        self.cidx = {float(v): j for j, v in enumerate(CONST_VALS)}
        self.ones_row = self.sbuf("ones_row", [1, 128], F32)
        self.ones_row_b = self.sbuf("ones_row_b", [1, 128], BF16)
        self.ones_col = self.sbuf("ones_col", [128, 1], F32)
        self._scr = {}

    def sbuf(self, name, shape, dtype):
        return self.stack.enter_context(self.nc.sbuf_tensor(name, shape, dtype))

    def psum(self, name, shape, dtype=F32):
        return self.stack.enter_context(self.nc.psum_tensor(name, shape, dtype))

    def const(self, val):
        val = float(val)
        return self.cpool[:, self.cidx[val]:self.cidx[val] + 1]

    def emit_init(self):
        se = self.se
        for j, v in enumerate(CONST_VALS):
            se.op("vector", lambda h, j=j, v=v: h.memset(
                self.cpool[:, j:j + 1], float(v)))
            self.nc.const_aps.register(F32, float(v), self.cpool[:, j:j + 1])
        se.op("vector", lambda h: h.memset(self.ones_row[:], 1.0))
        se.op("vector", lambda h: h.memset(self.ones_row_b[:], 1.0))
        se.op("vector", lambda h: h.memset(self.ones_col[:], 1.0))

    def emit_bcast(self, row_ap, out_psum_ap):
        n = row_ap.shape[-1]
        self.se.op("tensor", lambda h: h.matmul(
            out=out_psum_ap, lhsT=self.ones_row[:, 0:128], rhs=row_ap,
            start=True, stop=True))

    def emit_round(self, t_ap, out_ap, scr_i32, scr_a, scr_b):
        se = self.se
        se.op("vector", lambda h: h.tensor_copy(out=scr_i32, in_=t_ap))
        se.op("vector", lambda h: h.tensor_tensor(
            out=scr_a, in0=t_ap, in1=scr_i32, op=ALU.subtract))
        se.op("vector", lambda h: h.tensor_scalar(
            out=scr_b, in0=scr_a, scalar1=-0.5, scalar2=None, op0=ALU.is_lt))
        se.op("vector", lambda h: h.scalar_tensor_tensor(
            out=scr_a, in0=scr_a, scalar=0.5, in1=scr_b,
            op0=ALU.is_gt, op1=ALU.subtract))
        se.op("vector", lambda h: h.tensor_tensor(
            out=out_ap, in0=scr_i32, in1=scr_a, op=ALU.add))

    def emit_lnstats(self, x_ap, w, stats, mv, nmean, rstd, eps=EPS):
        se = self.se
        ng = w // 512
        for g in range(ng):
            se.op("vector", lambda h, g=g: h.bn_stats(
                out=stats[:, 6 * g:6 * (g + 1)], in_=x_ap[:, 512 * g:512 * (g + 1)]))
        se.op("vector", lambda h: h.bn_aggr(out=mv[:], in_=stats[:, 0:6 * ng]))
        se.op("vector", lambda h: h.tensor_scalar(
            out=nmean[:], in0=mv[:, 0:1], scalar1=-1.0, scalar2=None,
            op0=ALU.mult))
        se.op("scalar", lambda h: h.activation(
            out=rstd[:], in_=mv[:, 1:2], func=AF.Ln, bias=self.const(eps)))
        se.op("scalar", lambda h: h.activation(
            out=rstd[:], in_=rstd[:], func=AF.Exp, scale=self.const(-0.5)))

    def emit_transpose(self, in_sb, out_sb, n, pt, ident_b, in_off=0, out_off=0):
        se = self.se
        for j in range(n):
            se.op("tensor", lambda h, j=j: h.transpose(
                out=pt[:, 0:128], in_=in_sb[:, in_off + 128 * j:in_off + 128 * (j + 1)],
                identity=ident_b[:]))
            se.op("vector", lambda h, j=j: h.tensor_copy(
                out=out_sb[:, out_off + 128 * j:out_off + 128 * (j + 1)],
                in_=pt[:, 0:128]))


def make_identity(ctx, idn_i, ident_b):
    se = ctx.se
    se.op("gpsimd", lambda h: h.iota(
        idn_i[:], pattern=[[1, 128]], base=0, channel_multiplier=-1,
        allow_small_or_imprecise_dtypes=True))
    se.op("vector", lambda h: h.tensor_scalar(
        out=ident_b[:], in0=idn_i[:], scalar1=0.0, scalar2=None,
        op0=ALU.is_equal))


def shard_rows(a, n=NCORES):
    a = np.ascontiguousarray(a)
    a2 = a.reshape(B * T, -1)
    r = a2.shape[0] // n
    return [np.ascontiguousarray(a2[c * r:(c + 1) * r]) for c in range(n)]



def build_A():
    nc = bass.Bass("TRN2", target_bir_lowering=False)
    x_in = nc.dram_tensor("x", [RPC, D], F32, kind="ExternalInput").ap()
    wq_in = nc.dram_tensor("Wq", [D, 512], F32, kind="ExternalInput").ap()
    wk_in = nc.dram_tensor("Wk", [3, D, 512], F32, kind="ExternalInput").ap()
    wv_in = nc.dram_tensor("Wv", [3, D, 512], F32, kind="ExternalInput").ap()
    wg_in = nc.dram_tensor("Wg", [D, 4], F32, kind="ExternalInput").ap()
    w1s_in = nc.dram_tensor("w1s", [INNER, D], F32, kind="ExternalInput").ap()
    w2s_in = nc.dram_tensor("w2s", [D, INNER], F32, kind="ExternalInput").ap()

    xn0_out = nc.dram_tensor("xn0", [RPC, D], F32, kind="ExternalOutput").ap()
    q_out = nc.dram_tensor("q0", [RPC, 512], BF16, kind="ExternalOutput").ap()
    k_out = nc.dram_tensor("k0", [3, RPC, 512], BF16, kind="ExternalOutput").ap()
    v_out = nc.dram_tensor("v0", [3, RPC, 512], BF16, kind="ExternalOutput").ap()
    g_out = nc.dram_tensor("g0", [RPC, 4], F32, kind="ExternalOutput").ap()
    ws_out = nc.dram_tensor("mcl", [1, 2], F32, kind="ExternalOutput").ap()

    with ExitStack() as stack:
        ctx = Ctx(nc, stack)
        se = ctx.se
        xt = ctx.sbuf("xt", [128, D], F32)
        wq = ctx.sbuf("wq", [128, DK * 512], BF16)
        wk = [ctx.sbuf(f"wk{i}", [128, DK * 512], BF16) for i in range(3)]
        wv = [ctx.sbuf(f"wv{i}", [128, DK * 512], BF16) for i in range(3)]
        wg = ctx.sbuf("wg", [128, DK * 4], BF16)
        wsl1 = ctx.sbuf("wsl1", [128, D], F32)
        wsl2 = ctx.sbuf("wsl2", [128, INNER], F32)
        stats = ctx.sbuf("stats", [128, 24], F32)
        mv = ctx.sbuf("mv", [128, 2], F32)
        nmean = ctx.sbuf("nmean", [128, 1], F32)
        rstd = ctx.sbuf("rstd", [128, 1], F32)
        xn_f = ctx.sbuf("xn_f", [128, D], F32)
        xn_b = ctx.sbuf("xn_b", [128, D], BF16)
        xn_fm = ctx.sbuf("xn_fm", [128, DK * 128], BF16)
        idn_i = ctx.sbuf("idn_i", [128, 128], F32)
        ident_b = ctx.sbuf("ident_b", [128, 128], BF16)
        ev_b = ctx.sbuf("ev_b", [128, 512], BF16)
        g_ev = ctx.sbuf("g_ev", [128, 4], F32)
        gmax = ctx.sbuf("gmax", [128, 1], F32)
        gsum = ctx.sbuf("gsum", [128, 1], F32)
        col = ctx.sbuf("col", [128, 2], F32)
        wsacc = ctx.sbuf("wsacc", [128, 2], F32)
        wsfin = ctx.sbuf("wsfin", [1, 2], F32)

        pt = ctx.psum("pt", [128, 128], BF16)
        pp = ctx.psum("pp", [128, 512], F32)
        pg = ctx.psum("pg", [128, 4], F32)
        pws = ctx.psum("pws", [1, 2], F32)

        ctx.emit_init()
        make_identity(ctx, idn_i, ident_b)

        se.dma("gpsimd", wq[:].rearrange("p (k f) -> p k f", f=512),
               wq_in.rearrange("(k p) f -> p k f", p=128))
        for i in range(3):
            se.dma("gpsimd", wk[i][:].rearrange("p (k f) -> p k f", f=512),
                   wk_in[i].rearrange("(k p) f -> p k f", p=128))
            se.dma("gpsimd", wv[i][:].rearrange("p (k f) -> p k f", f=512),
                   wv_in[i].rearrange("(k p) f -> p k f", p=128))
        se.dma("gpsimd", wg[:].rearrange("p (k f) -> p k f", f=4),
               wg_in.rearrange("(k p) f -> p k f", p=128))
        se.op("vector", lambda h: h.memset(g_ev[:], 0.0))
        se.op("vector", lambda h: h.memset(wsacc[:], 0.0))

        for it in range(NT):
            se.dma("sync", xt[:], x_in[it * 128:(it + 1) * 128, :])
            ctx.emit_lnstats(xt[:], D, stats, mv, nmean, rstd)
            se.op("vector", lambda h: h.tensor_scalar(
                out=xn_f[:], in0=xt[:], scalar1=nmean[:, 0:1],
                scalar2=rstd[:, 0:1], op0=ALU.add, op1=ALU.mult))
            se.dma("sync", xn0_out[it * 128:(it + 1) * 128, :], xn_f[:])
            se.op("vector", lambda h: h.tensor_copy(out=xn_b[:], in_=xn_f[:]))
            ctx.emit_transpose(xn_b, xn_fm, DK, pt, ident_b)
            mats = [(wq, q_out)] + [(wk[i], (k_out, i)) for i in range(3)] \
                + [(wv[i], (v_out, i)) for i in range(3)]
            for w_sb, dst in mats:
                for k in range(DK):
                    se.op("tensor", lambda h, k=k, w_sb=w_sb: h.matmul(
                        out=pp[:], lhsT=xn_fm[:, 128 * k:128 * (k + 1)],
                        rhs=w_sb[:, 512 * k:512 * (k + 1)],
                        start=(k == 0), stop=(k == DK - 1)))
                se.op("vector", lambda h: h.tensor_copy(out=ev_b[:], in_=pp[:]))
                if isinstance(dst, tuple):
                    d, i = dst
                    se.dma("sync", d[i, it * 128:(it + 1) * 128, :], ev_b[:])
                else:
                    se.dma("sync", dst[it * 128:(it + 1) * 128, :], ev_b[:])
            for k in range(DK):
                se.op("tensor", lambda h, k=k: h.matmul(
                    out=pg[:], lhsT=xn_fm[:, 128 * k:128 * (k + 1)],
                    rhs=wg[:, 4 * k:4 * (k + 1)],
                    start=(k == 0), stop=(k == DK - 1)))
            se.op("vector", lambda h: h.reduce_max(
                out=gmax[:], in_=pg[:, 0:3], axis=AX.X))
            se.op("vector", lambda h: h.tensor_scalar(
                out=gmax[:], in0=gmax[:], scalar1=-1.0, scalar2=None,
                op0=ALU.mult))
            se.op("scalar", lambda h: h.activation(
                out=g_ev[:, 0:3], in_=pg[:, 0:3], func=AF.Exp,
                bias=gmax[:, 0:1], accum_out=gsum[:]))
            se.op("vector", lambda h: h.reciprocal(out=gsum[:], in_=gsum[:]))
            se.op("vector", lambda h: h.tensor_scalar(
                out=g_ev[:, 0:3], in0=g_ev[:, 0:3], scalar1=gsum[:, 0:1],
                scalar2=None, op0=ALU.mult))
            se.dma("sync", g_out[it * 128:(it + 1) * 128, :], g_ev[:])

        red_tok = {}
        dma_tok = {}
        base = se.last
        for j in range(INNER // 128 + 1):
            par = j % 2
            if j < INNER // 128:
                deps = [base] if j < 2 else [red_tok[j - 2]]
                dma_tok[j] = se.dma("sync", wsl2[:, par * D:(par + 1) * D],
                                    w1s_in[j * 128:(j + 1) * 128, :],
                                    deps=deps)
            if j >= 1:
                jj = j - 1
                ppar = jj % 2
                t = se.op("vector", lambda h, ppar=ppar: h.reduce_sum(
                    out=col[:, 0:1], in_=wsl2[:, ppar * D:(ppar + 1) * D],
                    axis=AX.X, apply_absolute_value=True), deps=[dma_tok[jj]])
                red_tok[jj] = se.op("vector", lambda h: h.tensor_tensor(
                    out=wsacc[:, 0:1], in0=wsacc[:, 0:1], in1=col[:, 0:1],
                    op=ALU.add), deps=[t])
        se.op("vector", lambda h: h.engine_nop(),
              deps=[red_tok[INNER // 128 - 1]], close=True)
        for j in range(D // 128):
            d1 = se.dma("sync", wsl2[:, 0:4096],
                        w2s_in[j * 128:(j + 1) * 128, 0:4096],
                        deps=[se.last])
            d2 = se.dma("sync", wsl2[:, 4096:8192],
                        w2s_in[j * 128:(j + 1) * 128, 4096:8192],
                        deps=[d1])
            t = se.op("vector", lambda h: h.reduce_sum(
                out=col[:, 1:2], in_=wsl2[:], axis=AX.X,
                apply_absolute_value=True), deps=[d2])
            se.op("vector", lambda h: h.tensor_tensor(
                out=wsacc[:, 1:2], in0=wsacc[:, 1:2], in1=col[:, 1:2],
                op=ALU.add), deps=[t], close=True)
        se.op("tensor", lambda h: h.matmul(
            out=pws[:], lhsT=ctx.ones_col[:], rhs=wsacc[:],
            start=True, stop=True))
        se.op("vector", lambda h: h.tensor_scalar(
            out=wsfin[:], in0=pws[:], scalar1=1.0 / (INNER * D), scalar2=EPS,
            op0=ALU.mult, op1=ALU.max))
        se.dma("sync", ws_out[:], wsfin[:])

        se.finalize()
    return nc


def run_A(inputs):
    nc = build_A()
    xs = shard_rows(inputs["L0_attn_x"])
    wgp = np.zeros((D, 4), np.float32)
    wgp[:, 0:3] = inputs["L0_attn_Wg"]
    in_maps = []
    for c in range(NCORES):
        in_maps.append({
            "x": xs[c],
            "Wq": np.ascontiguousarray(inputs["L0_attn_Wq"]),
            "Wk": np.ascontiguousarray(inputs["L0_attn_Wk"]),
            "Wv": np.ascontiguousarray(inputs["L0_attn_Wv"]),
            "Wg": wgp,
            "w1s": np.ascontiguousarray(inputs["L0_ffn_w1"]),
            "w2s": np.ascontiguousarray(inputs["L0_ffn_w2"]),
        })
    res = run_bass_kernel_spmd(nc, in_maps, core_ids=list(range(NCORES)))
    r = res.results
    out = {
        "xn0": np.concatenate([r[c]["xn0"] for c in range(NCORES)], 0),
        "q0": np.concatenate([r[c]["q0"] for c in range(NCORES)], 0),
        "k0": np.concatenate([r[c]["k0"] for c in range(NCORES)], 1),
        "v0": np.concatenate([r[c]["v0"] for c in range(NCORES)], 1),
        "g0": np.concatenate([r[c]["g0"] for c in range(NCORES)], 0),
        "mcl": r[0]["mcl"],
    }
    return out



def build_P():
    nc = bass.Bass("TRN2", target_bir_lowering=False)
    w1_in = nc.dram_tensor("w1s", [INNER // 8, D], F32, kind="ExternalInput").ap()
    w2_in = nc.dram_tensor("w2s", [D // 8, INNER], F32, kind="ExternalInput").ap()
    gw_in = nc.dram_tensor("gws", [INNER // 8, D], F32, kind="ExternalInput").ap()
    uw_in = nc.dram_tensor("uws", [INNER // 8, D], F32, kind="ExternalInput").ap()
    dw_in = nc.dram_tensor("dws", [D // 8, INNER], F32, kind="ExternalInput").ap()
    mcl_in = nc.dram_tensor("mcl", [1, 2], F32, kind="ExternalInput").ap()

    w1t_o = nc.dram_tensor("w1t_s", [D, INNER // 8], BF16, kind="ExternalOutput").ap()
    w2t_o = nc.dram_tensor("w2t_s", [INNER, D // 8], BF16, kind="ExternalOutput").ap()
    gq_o = nc.dram_tensor("gq_s", [D, INNER // 8], BF16, kind="ExternalOutput").ap()
    uq_o = nc.dram_tensor("uq_s", [D, INNER // 8], BF16, kind="ExternalOutput").ap()
    dq_o = nc.dram_tensor("dq_s", [INNER, D // 8], BF16, kind="ExternalOutput").ap()

    with ExitStack() as stack:
        ctx = Ctx(nc, stack)
        se = ctx.se
        idn_i = ctx.sbuf("idn_i", [128, 128], F32)
        ident_b = ctx.sbuf("ident_b", [128, 128], BF16)
        wld = ctx.sbuf("wld", [128, INNER], F32)
        m1 = ctx.sbuf("m1", [128, D], F32)
        m2 = ctx.sbuf("m2", [128, D], F32)
        m3 = ctx.sbuf("m3", [128, D], F32)
        m4 = ctx.sbuf("m4", [128, D], F32)
        m5 = ctx.sbuf("m5", [128, D], F32)
        mi = ctx.sbuf("mi", [128, D], I32)
        bmax = ctx.sbuf("bmax", [128, 64], F32)
        sbs = ctx.sbuf("sbs", [128, 64], F32)
        rss = ctx.sbuf("rss", [128, 64], F32)
        qb = ctx.sbuf("qb", [128, INNER], BF16)
        stg = ctx.sbuf("stg", [128, INNER], BF16)
        mclsb = ctx.sbuf("mclsb", [1, 2], F32)
        swcol = ctx.sbuf("swcol", [128, 2], F32)
        pH = ctx.psum("pH", [128, 1024], BF16)
        pP = ctx.psum("pP", [128, 512], F32)

        ctx.emit_init()
        make_identity(ctx, idn_i, ident_b)
        se.dma("sync", mclsb[:], mcl_in[:])
        se.op("tensor", lambda h: h.matmul(
            out=pP[:, 0:2], lhsT=ctx.ones_row[:], rhs=mclsb[:],
            start=True, stop=True))
        se.op("vector", lambda h: h.tensor_copy(out=swcol[:], in_=pP[:, 0:2]))
        se.op("vector", lambda h: h.reciprocal(out=swcol[:], in_=swcol[:]))

        def tern_window(src_ap, dst_ap, sw_col):
            se.op("vector", lambda h: h.tensor_scalar(
                out=m2[:], in0=src_ap, scalar1=sw_col, scalar2=None,
                op0=ALU.mult))
            se.op("vector", lambda h: h.tensor_scalar(
                out=m3[:], in0=m2[:], scalar1=-0.5, scalar2=None,
                op0=ALU.is_le))
            se.op("vector", lambda h: h.tensor_scalar(
                out=m2[:], in0=m2[:], scalar1=0.5, scalar2=None,
                op0=ALU.is_ge))
            se.op("vector", lambda h: h.tensor_tensor(
                out=dst_ap, in0=m2[:], in1=m3[:], op=ALU.subtract))

        for cch in range(8):
            se.dma("sync", wld[:, 0:D], w1_in[cch * 128:(cch + 1) * 128, :])
            tern_window(wld[:, 0:D], qb[:, 0:D], swcol[:, 0:1])
            for k in range(DK):
                se.op("tensor", lambda h, k=k: h.transpose(
                    out=pH[:, 0:128], in_=qb[:, 128 * k:128 * (k + 1)],
                    identity=ident_b[:]))
                se.op("vector", lambda h, k=k: h.tensor_copy(
                    out=stg[:, k * 128:(k + 1) * 128], in_=pH[:, 0:128]))
            se.dma("sync",
                   w1t_o[:, cch * 128:(cch + 1) * 128].rearrange(
                       "(k p) c -> p k c", p=128),
                   stg[:, 0:D].rearrange("p (k c) -> p k c", c=128))
        for cch in range(2):
            se.dma("sync", wld[:], w2_in[cch * 128:(cch + 1) * 128, :])
            for w in range(4):
                sl = slice(w * D, (w + 1) * D)
                tern_window(wld[:, sl], qb[:, sl], swcol[:, 1:2])
            for k in range(INNER // 128):
                se.op("tensor", lambda h, k=k: h.transpose(
                    out=pH[:, 0:128], in_=qb[:, 128 * k:128 * (k + 1)],
                    identity=ident_b[:]))
                se.op("vector", lambda h, k=k: h.tensor_copy(
                    out=stg[:, k * 128:(k + 1) * 128], in_=pH[:, 0:128]))
            se.dma("sync",
                   w2t_o[:, cch * 128:(cch + 1) * 128].rearrange(
                       "(k p) c -> p k c", p=128),
                   stg[:].rearrange("p (k c) -> p k c", c=128))
        for w_in, w_o in ((gw_in, gq_o), (uw_in, uq_o)):
            for cch in range(8):
                se.dma("sync", m4[:], w_in[cch * 128:(cch + 1) * 128, :])
                emit_mx_quant(ctx, m4[:], qb[:, 0:D], D, m1[:], m2[:], m3[:],
                              m4[:], m5[:], mi[:], bmax[:], sbs[:], rss[:])
                for k in range(DK):
                    se.op("tensor", lambda h, k=k: h.transpose(
                        out=pH[:, 0:128], in_=qb[:, 128 * k:128 * (k + 1)],
                        identity=ident_b[:]))
                    se.op("vector", lambda h, k=k: h.tensor_copy(
                        out=stg[:, k * 128:(k + 1) * 128], in_=pH[:, 0:128]))
                se.dma("sync",
                       w_o[:, cch * 128:(cch + 1) * 128].rearrange(
                           "(k p) c -> p k c", p=128),
                       stg[:, 0:D].rearrange("p (k c) -> p k c", c=128))
        for cch in range(2):
            se.dma("sync", wld[:], dw_in[cch * 128:(cch + 1) * 128, :])
            for w in range(4):
                sl = slice(w * D, (w + 1) * D)
                se.op("vector", lambda h, sl=sl: h.tensor_copy(
                    out=m4[:], in_=wld[:, sl]))
                emit_mx_quant(ctx, m4[:], qb[:, sl], D, m1[:], m2[:], m3[:],
                              m4[:], m5[:], mi[:], bmax[:], sbs[:], rss[:])
            for k in range(INNER // 128):
                se.op("tensor", lambda h, k=k: h.transpose(
                    out=pH[:, 0:128], in_=qb[:, 128 * k:128 * (k + 1)],
                    identity=ident_b[:]))
                se.op("vector", lambda h, k=k: h.tensor_copy(
                    out=stg[:, k * 128:(k + 1) * 128], in_=pH[:, 0:128]))
            se.dma("sync",
                   dq_o[:, cch * 128:(cch + 1) * 128].rearrange(
                       "(k p) c -> p k c", p=128),
                   stg[:].rearrange("p (k c) -> p k c", c=128))
        se.finalize()
    return nc


def run_P(inputs, aux):
    nc = build_P()
    in_maps = []
    for c in range(NCORES):
        i8, d8 = INNER // 8, D // 8
        in_maps.append({
            "w1s": np.ascontiguousarray(inputs["L0_ffn_w1"][c * i8:(c + 1) * i8]),
            "w2s": np.ascontiguousarray(inputs["L0_ffn_w2"][c * d8:(c + 1) * d8]),
            "gws": np.ascontiguousarray(inputs["L1_ffn_gate_w"][c * i8:(c + 1) * i8]),
            "uws": np.ascontiguousarray(inputs["L1_ffn_up_w"][c * i8:(c + 1) * i8]),
            "dws": np.ascontiguousarray(inputs["L1_ffn_down_w"][c * d8:(c + 1) * d8]),
            "mcl": np.asarray(aux["mcl"]).reshape(1, 2),
        })
    res = run_bass_kernel_spmd(nc, in_maps, core_ids=list(range(NCORES)))
    r = res.results
    return {
        "w1t": np.concatenate([r[c]["w1t_s"] for c in range(NCORES)], 1),
        "w2t": np.concatenate([r[c]["w2t_s"] for c in range(NCORES)], 1),
        "gq": np.concatenate([r[c]["gq_s"] for c in range(NCORES)], 1),
        "uq": np.concatenate([r[c]["uq_s"] for c in range(NCORES)], 1),
        "dq": np.concatenate([r[c]["dq_s"] for c in range(NCORES)], 1),
    }



def build_B():
    nc = bass.Bass("TRN2", target_bir_lowering=False)
    q_in = nc.dram_tensor("q", [RPC, 512], BF16, kind="ExternalInput").ap()
    k_in = nc.dram_tensor("k", [3, T, 512], BF16, kind="ExternalInput").ap()
    v_in = nc.dram_tensor("v", [3, T, 512], BF16, kind="ExternalInput").ap()
    g_in = nc.dram_tensor("g", [RPC, 4], F32, kind="ExternalInput").ap()
    dl_in = nc.dram_tensor("dl", [1, 4], F32, kind="ExternalInput").ap()
    osc_in = nc.dram_tensor("osc", [1, 1], F32, kind="ExternalInput").ap()
    wo_in = nc.dram_tensor("Wo", [512, D], F32, kind="ExternalInput").ap()
    half_in = nc.dram_tensor("half", [1, 1], F32, kind="ExternalInput").ap()

    a_out = nc.dram_tensor("attn0", [RPC, D], F32, kind="ExternalOutput").ap()
    if B_DEBUG:
        bl_dbg = nc.dram_tensor("bl_dbg", [NT, 128, 512], F32,
                                kind="ExternalOutput").ap()
        mk_dbg = nc.dram_tensor("mk_dbg", [3, 4, 128, 512], F32,
                                kind="ExternalOutput").ap()
        sc_dbg = nc.dram_tensor("sc_dbg", [3, 4, 128, 512], F32,
                                kind="ExternalOutput").ap()
        wt_dbg = nc.dram_tensor("wt_dbg", [3, 4, 128, 512], F32,
                                kind="ExternalOutput").ap()

    with ExitStack() as stack:
        ctx = Ctx(nc, stack)
        se = ctx.se
        idn_i = ctx.sbuf("idn_i", [128, 128], F32)
        ident_b = ctx.sbuf("ident_b", [128, 128], BF16)
        q_fm = ctx.sbuf("q_fm", [128, 4 * RPC], BF16)
        k_fm = [ctx.sbuf(f"k_fm{i}", [128, 4 * T], BF16) for i in range(3)]
        v_tm = [ctx.sbuf(f"v_tm{i}", [128, 16 * 512], BF16) for i in range(3)]
        gt = ctx.sbuf("gt", [128, NT * 4], F32)
        ld_tm = ctx.sbuf("ld_tm", [128, 512], BF16)
        t1 = ctx.sbuf("t1", [128, 512], F32)
        wtile = ctx.sbuf("wtile", [128, 512], F32)
        getile = ctx.sbuf("getile", [128, 512], F32)
        mk_b = ctx.sbuf("mk_b", [128, 512], BF16)
        mk_f32 = ctx.sbuf("mk_f32", [128, 512], F32)
        mkT = ctx.sbuf("mkT", [128, 512], BF16)
        blend = ctx.sbuf("blend", [128, 512], F32)
        blend_b = ctx.sbuf("blend_b", [128, 512], BF16)
        blendT = ctx.sbuf("blendT", [128, 512], BF16)
        att_f = ctx.sbuf("att_f", [128, 512], F32)
        wo = ctx.sbuf("wo", [128, 4 * D], BF16)
        sig = ctx.sbuf("sig", [1, 4], F32)
        lnd_row = ctx.sbuf("lnd_row", [1, 4], F32)
        lnd_col = ctx.sbuf("lnd_col", [128, 4], F32)
        bias_col = ctx.sbuf("bias_col", [128, 1], F32)
        osc_sb = ctx.sbuf("osc_sb", [1, 1], F32)
        osc_col = ctx.sbuf("osc_col", [128, 1], F32)

        pt = ctx.psum("pt", [128, 128], BF16)
        psc = ctx.psum("psc", [128, 512], F32)
        pav = ctx.psum("pav", [128, 512], F32)
        po = ctx.psum("po", [128, 512], F32)
        pb = ctx.psum("pb", [128, 4], F32)

        ctx.emit_init()
        make_identity(ctx, idn_i, ident_b)

        HALF = None
        ctx.emit_init2 = None

        se.dma("sync", osc_sb[:], osc_in[:])
        se.dma("sync", sig[:], dl_in[:])
        se.op("scalar", lambda h: h.activation(out=sig[:], in_=sig[:],
                                               func=AF.Sigmoid))
        se.op("scalar", lambda h: h.activation(out=lnd_row[:], in_=sig[:],
                                               func=AF.Ln))
        se.op("tensor", lambda h: h.matmul(out=pb[:], lhsT=ctx.ones_row[:],
                                           rhs=lnd_row[:], start=True, stop=True))
        se.op("vector", lambda h: h.tensor_copy(out=lnd_col[:], in_=pb[:]))
        se.op("tensor", lambda h: h.matmul(out=pb[:, 0:1], lhsT=ctx.ones_row[:],
                                           rhs=osc_sb[:], start=True, stop=True))
        se.op("vector", lambda h: h.tensor_copy(out=osc_col[:], in_=pb[:, 0:1]))

        se.dma("gpsimd", wo[:].rearrange("p (k j) -> p k j", j=D),
               wo_in.rearrange("(k p) j -> p k j", p=128))
        for it in range(NT):
            se.dma("sync", gt[:, 4 * it:4 * (it + 1)],
                   g_in[128 * it:128 * (it + 1), :])
        for it in range(NT):
            se.dma("sync", ld_tm[:], q_in[128 * it:128 * (it + 1), :])
            for j in range(4):
                se.op("tensor", lambda h, j=j: h.transpose(
                    out=pt[:], in_=ld_tm[:, 128 * j:128 * (j + 1)],
                    identity=ident_b[:]))
                se.op("vector", lambda h, j=j, it=it: h.tensor_copy(
                    out=q_fm[:, j * RPC + it * 128: j * RPC + (it + 1) * 128],
                    in_=pt[:]))
        for i in range(3):
            for st_ in range(16):
                se.dma("sync", v_tm[i][:, 512 * st_:512 * (st_ + 1)],
                       k_in[i, 128 * st_:128 * (st_ + 1), :])
                for j in range(4):
                    se.op("tensor", lambda h, j=j, i=i, st_=st_: h.transpose(
                        out=pt[:], in_=v_tm[i][:, 512 * st_ + 128 * j:
                                               512 * st_ + 128 * (j + 1)],
                        identity=ident_b[:]))
                    se.op("vector", lambda h, j=j, i=i, st_=st_: h.tensor_copy(
                        out=k_fm[i][:, j * T + st_ * 128:j * T + (st_ + 1) * 128],
                        in_=pt[:]))
            for st_ in range(16):
                se.dma("sync", v_tm[i][:, 512 * st_:512 * (st_ + 1)],
                       v_in[i, 128 * st_:128 * (st_ + 1), :])

        se.op("gpsimd", lambda h: h.iota(
            t1[:], pattern=[[1, 512]], base=-1, channel_multiplier=-1,
            allow_small_or_imprecise_dtypes=True))

        for qt in range(NT):
            t0 = qt * 128
            ch_lo = t0 // 512
            se.op("vector", lambda h: h.memset(blend[:], 0.0))
            for i in range(3):
                first_av = True
                for ch in range(ch_lo, 4):
                    s0 = ch * 512
                    for j in range(4):
                        se.op("tensor", lambda h, j=j, i=i, qt=qt, s0=s0: h.matmul(
                            out=psc[:],
                            lhsT=q_fm[:, j * RPC + qt * 128:j * RPC + (qt + 1) * 128],
                            rhs=k_fm[i][:, j * T + s0:j * T + s0 + 512],
                            start=(j == 0), stop=(j == 3)))
                    d0 = float(s0 - t0)
                    se.op("vector", lambda h, i=i, d0=d0: h.tensor_scalar(
                        out=bias_col[:], in0=lnd_col[:, i:i + 1], scalar1=d0,
                        scalar2=None, op0=ALU.mult))
                    se.op("vector", lambda h, i=i: h.tensor_scalar(
                        out=wtile[:], in0=t1[:], scalar1=lnd_col[:, i:i + 1],
                        scalar2=bias_col[:, 0:1], op0=ALU.mult, op1=ALU.add))
                    se.op("vector", lambda h: h.tensor_scalar_min(
                        out=wtile[:], in0=wtile[:], scalar1=0.0))
                    se.op("scalar", lambda h: h.activation(
                        out=wtile[:], in_=wtile[:], func=AF.Exp))
                    if s0 <= t0:
                        se.op("vector", lambda h, d0=d0: h.tensor_scalar(
                            out=getile[:], in0=t1[:], scalar1=d0, scalar2=0.0,
                            op0=ALU.add, op1=ALU.is_ge))
                        se.op("vector", lambda h: h.tensor_tensor(
                            out=wtile[:], in0=wtile[:], in1=getile[:],
                            op=ALU.mult))
                    se.op("vector", lambda h: h.tensor_tensor(
                        out=mk_b[:], in0=psc[:], in1=wtile[:], op=ALU.mult))
                    if B_DEBUG and qt == B_DEBUG_QT:
                        se.op("vector", lambda h: h.tensor_copy(
                            out=mk_f32[:], in_=mk_b[:]))
                        se.dma("sync", mk_dbg[i, ch], mk_f32[:])
                        se.op("vector", lambda h: h.tensor_copy(
                            out=mk_f32[:], in_=psc[:]))
                        se.dma("sync", sc_dbg[i, ch], mk_f32[:])
                        se.dma("sync", wt_dbg[i, ch], wtile[:])
                    for j in range(4):
                        se.op("tensor", lambda h, j=j: h.transpose(
                            out=pt[:], in_=mk_b[:, 128 * j:128 * (j + 1)],
                            identity=ident_b[:]))
                        se.op("vector", lambda h, j=j: h.tensor_copy(
                            out=mkT[:, 128 * j:128 * (j + 1)], in_=pt[:]))
                    for j in range(4):
                        stile = ch * 4 + j
                        se.op("tensor", lambda h, j=j, i=i, stile=stile: h.matmul(
                            out=pav[:], lhsT=mkT[:, 128 * j:128 * (j + 1)],
                            rhs=v_tm[i][:, 512 * stile:512 * (stile + 1)],
                            start=(j == 0), stop=(j == 3)))
                    first_av = False
                    se.op("vector", lambda h, i=i, qt=qt: h.scalar_tensor_tensor(
                        out=blend[:], in0=pav[:],
                        scalar=gt[:, 4 * qt + i:4 * qt + i + 1],
                        in1=blend[:], op0=ALU.mult, op1=ALU.add))
            if B_DEBUG:
                se.dma("sync", bl_dbg[qt], blend[:])
            se.op("vector", lambda h: h.tensor_copy(out=blend_b[:], in_=blend[:]))
            for j in range(4):
                se.op("tensor", lambda h, j=j: h.transpose(
                    out=pt[:], in_=blend_b[:, 128 * j:128 * (j + 1)],
                    identity=ident_b[:]))
                se.op("vector", lambda h, j=j: h.tensor_copy(
                    out=blendT[:, 128 * j:128 * (j + 1)], in_=pt[:]))
            for jc in range(4):
                for kk in range(4):
                    se.op("tensor", lambda h, jc=jc, kk=kk: h.matmul(
                        out=po[:], lhsT=blendT[:, 128 * kk:128 * (kk + 1)],
                        rhs=wo[:, kk * D + jc * 512:kk * D + jc * 512 + 512],
                        start=(kk == 0), stop=(kk == 3)))
                se.op("vector", lambda h: h.tensor_scalar(
                    out=att_f[:], in0=po[:], scalar1=osc_col[:, 0:1],
                    scalar2=None, op0=ALU.mult))
                se.dma("sync", a_out[128 * qt:128 * (qt + 1),
                                     512 * jc:512 * (jc + 1)], att_f[:])

        se.finalize()
    return nc


def run_B(inputs, aux):
    nc = build_B()
    qs = shard_rows(np.asarray(aux["q0"]))
    gs = shard_rows(np.asarray(aux["g0"]))
    k0 = np.asarray(aux["k0"]).reshape(3, B, T, 512)
    v0 = np.asarray(aux["v0"]).reshape(3, B, T, 512)
    dl = np.zeros((1, 4), np.float32)
    dl[0, 0:3] = inputs["L0_attn_decay_logits"]
    osc = np.asarray(inputs["L0_attn_out_scale"], np.float32).reshape(1, 1)
    in_maps = []
    for c in range(NCORES):
        b, half = c // 2, c % 2
        kk = k0[:, b]
        vv = v0[:, b]
        if half == 1:
            kk = np.concatenate([kk[:, 1024:2048], np.zeros_like(kk[:, 0:1024])], 1)
            vv = np.concatenate([vv[:, 1024:2048], np.zeros_like(vv[:, 0:1024])], 1)
        in_maps.append({
            "q": qs[c], "g": gs[c],
            "k": np.ascontiguousarray(kk), "v": np.ascontiguousarray(vv),
            "dl": dl, "osc": osc, "half": np.zeros((1, 1), np.float32),
            "Wo": np.ascontiguousarray(inputs["L0_attn_Wo"]),
        })
    res = run_bass_kernel_spmd(nc, in_maps, core_ids=list(range(NCORES)))
    return np.concatenate([res.results[c]["attn0"] for c in range(NCORES)], 0)



def build_C():
    nc = bass.Bass("TRN2", target_bir_lowering=False)
    xn0_in = nc.dram_tensor("xn0", [RPC, D], F32, kind="ExternalInput").ap()
    x_in = nc.dram_tensor("x", [RPC, D], F32, kind="ExternalInput").ap()
    at_in = nc.dram_tensor("attn0", [RPC, D], F32, kind="ExternalInput").ap()
    w1t_in = nc.dram_tensor("w1t", [D, INNER], BF16, kind="ExternalInput").ap()
    w2t_in = nc.dram_tensor("w2t", [INNER, D], BF16, kind="ExternalInput").ap()
    b1_in = nc.dram_tensor("b1", [1, INNER], F32, kind="ExternalInput").ap()
    lng_in = nc.dram_tensor("lng", [1, INNER], F32, kind="ExternalInput").ap()
    lnb_in = nc.dram_tensor("lnb", [1, INNER], F32, kind="ExternalInput").ap()
    b2_in = nc.dram_tensor("b2", [1, D], F32, kind="ExternalInput").ap()
    ws_in = nc.dram_tensor("mcl2", [1, 2], F32, kind="ExternalInput").ap()

    x1_out = nc.dram_tensor("x1", [RPC, D], F32, kind="ExternalOutput").ap()
    xn1_out = nc.dram_tensor("xn1", [RPC, D], F32, kind="ExternalOutput").ap()
    if C_DEBUG or C_DEBUG_LITE:
        f0dbg_out = nc.dram_tensor("f0dbg", [RPC, D], F32, kind="ExternalOutput").ap()
    if C_DEBUG:
        dbg_out = nc.dram_tensor("dbg", [128, 64], F32, kind="ExternalOutput").ap()
        hdbg_out = nc.dram_tensor("hdbg", [RPC, INNER], F32, kind="ExternalOutput").ap()
        hqdbg_out = nc.dram_tensor("hqdbg", [INNER, RPC], BF16, kind="ExternalOutput").ap()

    w1t_d = w1t_in
    w2t_d = w2t_in
    h_d = nc.dram_tensor("h_d", [RPC, INNER], F32, kind="Internal").ap()
    hq_d = nc.dram_tensor("hq_d", [INNER, RPC], BF16, kind="Internal").ap()

    with ExitStack() as stack:
        ctx = Ctx(nc, stack)
        se = ctx.se
        idn_i = ctx.sbuf("idn_i", [128, 128], F32)
        ident_b = ctx.sbuf("ident_b", [128, 128], BF16)
        xq_fm = ctx.sbuf("xq_fm", [128, DK * RPC], BF16)
        big_f1 = ctx.sbuf("big_f1", [128, INNER], F32)
        big_f2 = ctx.sbuf("big_f2", [128, D], F32)
        big_f3 = ctx.sbuf("big_f3", [128, D], F32)
        big_i = ctx.sbuf("big_i", [128, D], I32)
        big_b = ctx.sbuf("big_b", [128, INNER], BF16)
        hqT = ctx.sbuf("hqT", [128, D], BF16)
        gw = ctx.sbuf("gw", [128, D], F32)
        bw = ctx.sbuf("bw", [128, D], F32)
        b2r = ctx.sbuf("b2r", [128, D], F32)
        b1w = ctx.sbuf("b1w", [128, 512], F32)
        rowb1 = ctx.sbuf("rowb1", [1, 512], F32)
        rowbuf2 = ctx.sbuf("rowbuf2", [1, D], F32)
        b2row = ctx.sbuf("b2row", [1, D], F32)
        ws16 = ctx.sbuf("ws16", [1, 16], F32)
        wsred = ctx.sbuf("wsred", [1, 2], F32)
        swcol = ctx.sbuf("swcol", [128, 2], F32)
        mclcol = ctx.sbuf("mclcol", [128, 2], F32)
        cols = ctx.sbuf("cols", [128, 16], F32)
        runsum = ctx.sbuf("runsum", [128, NT], F32)
        runsq = ctx.sbuf("runsq", [128, NT], F32)
        amax2 = ctx.sbuf("amax2", [128, NT], F32)
        asq = ctx.sbuf("asq", [128, NT], F32)
        comb2 = ctx.sbuf("comb2", [128, NT], F32)
        mucol = ctx.sbuf("mucol", [128, NT], F32)
        rstd2 = ctx.sbuf("rstd2", [128, NT], F32)
        s2c = ctx.sbuf("s2c", [128, NT], F32)
        inv2c = ctx.sbuf("inv2c", [128, NT], F32)
        invs = ctx.sbuf("invs", [128, NT], F32)
        c1 = ctx.sbuf("c1", [128, 1], F32)
        c2 = ctx.sbuf("c2", [128, 1], F32)
        c3 = ctx.sbuf("c3", [128, 1], F32)

        pA = ctx.psum("pA", [128, 512], F32)
        pB = ctx.psum("pB", [128, 512], F32)
        pC = ctx.psum("pC", [128, 512], F32)
        pD = ctx.psum("pD", [128, 512], F32)
        pH = ctx.psum("pH", [128, 1024], BF16)
        pds = [pA, pB, pC, pD]

        ctx.emit_init()
        make_identity(ctx, idn_i, ident_b)

        se.dma("sync", wsred[:], ws_in[:])
        se.op("tensor", lambda h: h.matmul(
            out=pA[:, 0:2], lhsT=ctx.ones_row[:], rhs=wsred[:],
            start=True, stop=True))
        se.op("vector", lambda h: h.tensor_copy(out=mclcol[:], in_=pA[:, 0:2]))
        se.dma("sync", b2row[:], b2_in[:])
        for j in range(4):
            se.op("tensor", lambda h, j=j: h.matmul(
                out=pB[:], lhsT=ctx.ones_row[:],
                rhs=b2row[:, 512 * j:512 * (j + 1)], start=True, stop=True))
            se.op("vector", lambda h, j=j: h.tensor_copy(
                out=b2r[:, 512 * j:512 * (j + 1)], in_=pB[:]))
        se.op("vector", lambda h: h.memset(runsum[:], 0.0))
        se.op("vector", lambda h: h.memset(runsq[:], 0.0))
        se.op("vector", lambda h: h.memset(amax2[:], 0.0))
        se.op("vector", lambda h: h.memset(asq[:], 0.0))

        for it in range(NT):
            xt = big_f1[:, 0:D]
            se.dma("sync", xt, xn0_in[it * 128:(it + 1) * 128, :])
            se.op("scalar", lambda h, xt=xt: h.activation(
                out=big_f2[:, 0:D], in_=xt, func=AF.Square, accum_out=c1[:]))
            se.op("vector", lambda h: h.tensor_scalar(
                out=c1[:], in0=c1[:], scalar1=1.0 / D, scalar2=EPS,
                op0=ALU.mult, op1=ALU.add))
            se.op("scalar", lambda h: h.activation(
                out=c1[:], in_=c1[:], func=AF.Ln))
            se.op("scalar", lambda h: h.activation(
                out=c1[:], in_=c1[:], func=AF.Exp, scale=ctx.const(-0.5)))
            se.op("vector", lambda h, xt=xt: h.reduce_max(
                out=c2[:], in_=xt, axis=AX.X, apply_absolute_value=True))
            se.op("vector", lambda h: h.tensor_tensor(
                out=c2[:], in0=c2[:], in1=c1[:], op=ALU.mult))
            se.op("vector", lambda h: h.tensor_scalar_max(
                out=c2[:], in0=c2[:], scalar1=EPS))
            se.op("vector", lambda h, it=it: h.tensor_scalar(
                out=invs[:, it:it + 1], in0=c2[:], scalar1=1.0 / 127.0,
                scalar2=None, op0=ALU.mult))
            se.op("vector", lambda h: h.reciprocal(out=c2[:], in_=c2[:]))
            se.op("vector", lambda h: h.tensor_scalar(
                out=c2[:], in0=c2[:], scalar1=127.0, scalar2=None,
                op0=ALU.mult))
            se.op("vector", lambda h: h.tensor_tensor(
                out=c3[:], in0=c1[:], in1=c2[:], op=ALU.mult))
            se.op("vector", lambda h, xt=xt: h.tensor_scalar(
                out=big_f2[:, 0:D], in0=xt, scalar1=c3[:, 0:1], scalar2=None,
                op0=ALU.mult))
            ctx.emit_round(big_f2[:, 0:D], big_b[:, 0:D], big_i[:],
                           big_f3[:, 0:D], big_f1[:, D:2 * D])
            for k in range(DK):
                se.op("tensor", lambda h, k=k: h.transpose(
                    out=pH[:, 0:128], in_=big_b[:, 128 * k:128 * (k + 1)],
                    identity=ident_b[:]))
                se.op("vector", lambda h, k=k, it=it: h.tensor_copy(
                    out=xq_fm[:, k * RPC + it * 128:k * RPC + (it + 1) * 128],
                    in_=pH[:, 0:128]))

        free_tok = None
        sq_tok = None
        dma_tok = None
        mm_last = None
        for wg in range(16):
            rdeps = [se.last] if wg == 0 else [mm_last]
            for k in range(DK):
                rhs_tok = se.dma("sync", big_b[:, k * 512:(k + 1) * 512],
                                 w1t_d[k * 128:(k + 1) * 128,
                                       wg * 512:(wg + 1) * 512], deps=rdeps)
            bt = se.dma("sync", rowb1[:], b1_in[0:1, wg * 512:(wg + 1) * 512],
                        deps=rdeps)
            bt = se.op("tensor", lambda h: h.matmul(
                out=pC[:], lhsT=ctx.ones_row[:], rhs=rowb1[:],
                start=True, stop=True), deps=[bt, free_tok] if free_tok
                else [bt])
            b1_tok = se.op("vector", lambda h: h.tensor_copy(
                out=b1w[:], in_=pC[:]), deps=[bt])
            for it in range(NT):
                pm = pds[it % 2]
                mdeps = [rhs_tok, b1_tok] + ([free_tok] if free_tok else [])
                for k in range(DK):
                    mm_last = se.op("tensor", lambda h, k=k, it=it, pm=pm: h.matmul(
                        out=pm[:], lhsT=xq_fm[:, k * RPC + it * 128:
                                              k * RPC + (it + 1) * 128],
                        rhs=big_b[:, k * 512:(k + 1) * 512],
                        start=(k == 0), stop=(k == DK - 1)), deps=mdeps)
                hb = big_f1[:, 0:512]
                cdeps = [mm_last] + ([sq_tok] if sq_tok else []) + \
                    ([dma_tok] if dma_tok else [])
                free_tok = t = se.op("vector", lambda h, pm=pm, it=it, hb=hb:
                                     h.tensor_scalar(
                    out=hb, in0=pm[:], scalar1=invs[:, it:it + 1],
                    scalar2=mclcol[:, 0:1], op0=ALU.mult, op1=ALU.mult),
                    deps=cdeps)
                t = se.op("vector", lambda h, hb=hb: h.tensor_tensor(
                    out=hb, in0=hb, in1=b1w[:], op=ALU.add), deps=[t])
                t = se.op("scalar", lambda h, hb=hb: h.activation(
                    out=big_f3[:, 0:512], in_=hb, func=AF.Tanh,
                    scale=ctx.const(0.5)), deps=[t])
                t = se.op("vector", lambda h, hb=hb: h.scalar_tensor_tensor(
                    out=hb, in0=big_f3[:, 0:512], scalar=1.0, in1=hb,
                    op0=ALU.add, op1=ALU.mult), deps=[t])
                t = se.op("vector", lambda h, hb=hb: h.tensor_scalar(
                    out=hb, in0=hb, scalar1=0.5, scalar2=None, op0=ALU.mult),
                    deps=[t])
                t = se.op("scalar", lambda h, hb=hb: h.activation(
                    out=hb, in_=hb, func=AF.Gelu, accum_out=c1[:]), deps=[t])
                t = se.op("vector", lambda h, it=it: h.tensor_tensor(
                    out=runsum[:, it:it + 1], in0=runsum[:, it:it + 1],
                    in1=c1[:], op=ALU.add), deps=[t])
                sq_tok = se.op("scalar", lambda h, hb=hb: h.activation(
                    out=big_f3[:, 0:512], in_=hb, func=AF.Square,
                    accum_out=c2[:]), deps=[t])
                t = se.op("vector", lambda h, it=it: h.tensor_tensor(
                    out=runsq[:, it:it + 1], in0=runsq[:, it:it + 1],
                    in1=c2[:], op=ALU.add), deps=[sq_tok])
                dma_tok = se.dma("sync", h_d[it * 128:(it + 1) * 128,
                                             wg * 512:(wg + 1) * 512], hb,
                                 deps=[sq_tok])
        se.op("vector", lambda h: h.engine_nop(), deps=se.fence(), close=True)

        for it in range(NT):
            se.op("vector", lambda h, it=it: h.tensor_scalar(
                out=mucol[:, it:it + 1], in0=runsum[:, it:it + 1],
                scalar1=1.0 / INNER, scalar2=None, op0=ALU.mult))
            se.op("scalar", lambda h, it=it: h.activation(
                out=c1[:], in_=mucol[:, it:it + 1], func=AF.Square))
            se.op("vector", lambda h, it=it: h.tensor_scalar(
                out=c2[:], in0=runsq[:, it:it + 1], scalar1=1.0 / INNER,
                scalar2=None, op0=ALU.mult))
            se.op("vector", lambda h: h.tensor_tensor(
                out=c2[:], in0=c2[:], in1=c1[:], op=ALU.subtract))
            se.op("scalar", lambda h: h.activation(
                out=c2[:], in_=c2[:], func=AF.Ln, bias=ctx.const(EPS)))
            se.op("scalar", lambda h, it=it: h.activation(
                out=rstd2[:, it:it + 1], in_=c2[:], func=AF.Exp,
                scale=ctx.const(-0.5)))

        for win in range(4):
            se.dma("sync", rowbuf2[:], lng_in[0:1, win * D:(win + 1) * D])
            for j in range(4):
                se.op("tensor", lambda h, j=j: h.matmul(
                    out=pC[:], lhsT=ctx.ones_row[:],
                    rhs=rowbuf2[:, 512 * j:512 * (j + 1)], start=True, stop=True))
                se.op("vector", lambda h, j=j: h.tensor_copy(
                    out=gw[:, 512 * j:512 * (j + 1)], in_=pC[:]))
            se.dma("sync", rowbuf2[:], lnb_in[0:1, win * D:(win + 1) * D])
            for j in range(4):
                se.op("tensor", lambda h, j=j: h.matmul(
                    out=pC[:], lhsT=ctx.ones_row[:],
                    rhs=rowbuf2[:, 512 * j:512 * (j + 1)], start=True, stop=True))
                se.op("vector", lambda h, j=j: h.tensor_copy(
                    out=bw[:, 512 * j:512 * (j + 1)], in_=pC[:]))
            for it in range(NT):
                se.dma("sync", big_f1[:, 0:D],
                       h_d[it * 128:(it + 1) * 128, win * D:(win + 1) * D])
                se.op("vector", lambda h, it=it: h.tensor_scalar(
                    out=big_f2[:, 0:D], in0=big_f1[:, 0:D],
                    scalar1=mucol[:, it:it + 1], scalar2=rstd2[:, it:it + 1],
                    op0=ALU.subtract, op1=ALU.mult))
                se.op("vector", lambda h: h.tensor_tensor(
                    out=big_f2[:, 0:D], in0=big_f2[:, 0:D], in1=gw[:, 0:D],
                    op=ALU.mult))
                se.op("vector", lambda h: h.tensor_tensor(
                    out=big_f2[:, 0:D], in0=big_f2[:, 0:D], in1=bw[:, 0:D],
                    op=ALU.add))
                se.op("vector", lambda h: h.reduce_max(
                    out=c1[:], in_=big_f2[:, 0:D], axis=AX.X,
                    apply_absolute_value=True))
                se.op("vector", lambda h, it=it: h.tensor_tensor(
                    out=amax2[:, it:it + 1], in0=amax2[:, it:it + 1],
                    in1=c1[:], op=ALU.max))
                se.op("scalar", lambda h: h.activation(
                    out=big_f3[:, 0:D], in_=big_f2[:, 0:D], func=AF.Square,
                    accum_out=c2[:]))
                se.op("vector", lambda h, it=it: h.tensor_tensor(
                    out=asq[:, it:it + 1], in0=asq[:, it:it + 1],
                    in1=c2[:], op=ALU.add))

        for it in range(NT):
            se.op("vector", lambda h, it=it: h.tensor_scalar(
                out=c3[:], in0=asq[:, it:it + 1], scalar1=1.0 / INNER,
                scalar2=EPS, op0=ALU.mult, op1=ALU.add))
            se.op("scalar", lambda h: h.activation(out=c3[:], in_=c3[:],
                                                   func=AF.Ln))
            se.op("scalar", lambda h: h.activation(
                out=c3[:], in_=c3[:], func=AF.Exp, scale=ctx.const(-0.5)))
            se.op("vector", lambda h, it=it: h.tensor_tensor(
                out=c1[:], in0=amax2[:, it:it + 1], in1=c3[:], op=ALU.mult))
            se.op("vector", lambda h: h.tensor_scalar_max(
                out=c1[:], in0=c1[:], scalar1=EPS))
            se.op("vector", lambda h, it=it: h.tensor_scalar(
                out=inv2c[:, it:it + 1], in0=c1[:], scalar1=1.0 / 127.0,
                scalar2=None, op0=ALU.mult))
            se.op("vector", lambda h: h.reciprocal(out=c1[:], in_=c1[:]))
            se.op("vector", lambda h, it=it: h.tensor_scalar(
                out=s2c[:, it:it + 1], in0=c1[:], scalar1=127.0,
                scalar2=None, op0=ALU.mult))
            se.op("vector", lambda h, it=it: h.tensor_tensor(
                out=comb2[:, it:it + 1], in0=s2c[:, it:it + 1], in1=c3[:],
                op=ALU.mult))

        for win in range(4):
            se.dma("sync", rowbuf2[:], lng_in[0:1, win * D:(win + 1) * D])
            for j in range(4):
                se.op("tensor", lambda h, j=j: h.matmul(
                    out=pC[:], lhsT=ctx.ones_row[:],
                    rhs=rowbuf2[:, 512 * j:512 * (j + 1)], start=True, stop=True))
                se.op("vector", lambda h, j=j: h.tensor_copy(
                    out=gw[:, 512 * j:512 * (j + 1)], in_=pC[:]))
            se.dma("sync", rowbuf2[:], lnb_in[0:1, win * D:(win + 1) * D])
            for j in range(4):
                se.op("tensor", lambda h, j=j: h.matmul(
                    out=pC[:], lhsT=ctx.ones_row[:],
                    rhs=rowbuf2[:, 512 * j:512 * (j + 1)], start=True, stop=True))
                se.op("vector", lambda h, j=j: h.tensor_copy(
                    out=bw[:, 512 * j:512 * (j + 1)], in_=pC[:]))
            for it in range(NT):
                se.dma("sync", big_f1[:, 0:D],
                       h_d[it * 128:(it + 1) * 128, win * D:(win + 1) * D])
                se.op("vector", lambda h, it=it: h.tensor_scalar(
                    out=big_f2[:, 0:D], in0=big_f1[:, 0:D],
                    scalar1=mucol[:, it:it + 1], scalar2=rstd2[:, it:it + 1],
                    op0=ALU.subtract, op1=ALU.mult))
                se.op("vector", lambda h: h.tensor_tensor(
                    out=big_f2[:, 0:D], in0=big_f2[:, 0:D], in1=gw[:, 0:D],
                    op=ALU.mult))
                se.op("vector", lambda h: h.tensor_tensor(
                    out=big_f2[:, 0:D], in0=big_f2[:, 0:D], in1=bw[:, 0:D],
                    op=ALU.add))
                se.op("vector", lambda h, it=it: h.tensor_scalar(
                    out=big_f2[:, 0:D], in0=big_f2[:, 0:D],
                    scalar1=comb2[:, it:it + 1], scalar2=None, op0=ALU.mult))
                ctx.emit_round(big_f2[:, 0:D], big_b[:, 0:D], big_i[:],
                               big_f3[:, 0:D], big_f1[:, D:2 * D])
                for k in range(DK):
                    se.op("tensor", lambda h, k=k: h.transpose(
                        out=pH[:, 0:128], in_=big_b[:, 128 * k:128 * (k + 1)],
                        identity=ident_b[:]))
                    se.op("vector", lambda h, k=k: h.tensor_copy(
                        out=hqT[:, k * 128:(k + 1) * 128], in_=pH[:, 0:128]))
                se.dma("sync",
                       hq_d[win * D:(win + 1) * D,
                            it * 128:(it + 1) * 128].rearrange(
                                "(k p) c -> p k c", p=128),
                       hqT[:].rearrange("p (k c) -> p k c", c=128))

        for dch in range(4):
            for tp in range(2):
                nk = INNER // 128
                dtoks = {}
                mtoks = {}
                base = se.last
                for k in range(nk + 1):
                    if k < nk:
                        par = k % 2
                        deps = [base] if k < 2 else [mtoks[k - 2]]
                        d1 = se.dma("sync", big_b[:, par * 512:par * 512 + 512],
                                    w2t_d[k * 128:(k + 1) * 128,
                                          dch * 512:(dch + 1) * 512],
                                    deps=deps)
                        d2 = se.dma("sync",
                                    hqT[:, par * 512:par * 512 + 512],
                                    hq_d[k * 128:(k + 1) * 128,
                                         tp * 512:(tp + 1) * 512],
                                    deps=deps)
                        dtoks[k] = d2
                    if k >= 1:
                        kk = k - 1
                        par = kk % 2
                        for tt in range(4):
                            mtoks[kk] = se.op(
                                "tensor", lambda h, kk=kk, tt=tt, par=par: h.matmul(
                                    out=pds[tt][:],
                                    lhsT=hqT[:, par * 512 + tt * 128:
                                             par * 512 + (tt + 1) * 128],
                                    rhs=big_b[:, par * 512:par * 512 + 512],
                                    start=(kk == 0), stop=(kk == nk - 1)),
                                deps=[dtoks[kk]])
                se.op("vector", lambda h: h.engine_nop(), deps=[mtoks[nk - 1]],
                      close=True)
                for tt in range(4):
                    it = tp * 4 + tt
                    ob = big_f1[:, 0:512]
                    se.op("vector", lambda h, tt=tt, it=it, ob=ob: h.tensor_scalar(
                        out=ob, in0=pds[tt][:], scalar1=inv2c[:, it:it + 1],
                        scalar2=mclcol[:, 1:2], op0=ALU.mult, op1=ALU.mult))
                    se.op("vector", lambda h, dch=dch, ob=ob: h.tensor_tensor(
                        out=ob, in0=ob,
                        in1=b2r[:, dch * 512:(dch + 1) * 512], op=ALU.add))
                    if C_DEBUG or C_DEBUG_LITE:
                        se.dma("sync", f0dbg_out[it * 128:(it + 1) * 128,
                                                 dch * 512:(dch + 1) * 512], ob)
                    se.dma("sync", big_f2[:, 0:512],
                           x_in[it * 128:(it + 1) * 128,
                                dch * 512:(dch + 1) * 512])
                    se.op("vector", lambda h, ob=ob: h.tensor_tensor(
                        out=ob, in0=ob, in1=big_f2[:, 0:512], op=ALU.add))
                    se.dma("sync", big_f2[:, 0:512],
                           at_in[it * 128:(it + 1) * 128,
                                 dch * 512:(dch + 1) * 512])
                    se.op("vector", lambda h, ob=ob: h.tensor_tensor(
                        out=ob, in0=ob, in1=big_f2[:, 0:512], op=ALU.add))
                    se.dma("sync", x1_out[it * 128:(it + 1) * 128,
                                          dch * 512:(dch + 1) * 512], ob)

        if C_DEBUG:
            dbg = ctx.sbuf("dbg_sb", [128, 64], F32)
            se.op("vector", lambda h: h.tensor_copy(out=dbg[:, 0:2], in_=swcol[:]))
            se.op("vector", lambda h: h.tensor_copy(out=dbg[:, 2:4], in_=mclcol[:]))
            se.op("vector", lambda h: h.tensor_copy(out=dbg[:, 4:12], in_=mucol[:]))
            se.op("vector", lambda h: h.tensor_copy(out=dbg[:, 12:20], in_=rstd2[:]))
            se.op("vector", lambda h: h.tensor_copy(out=dbg[:, 20:28], in_=amax2[:]))
            se.op("vector", lambda h: h.tensor_copy(out=dbg[:, 28:36], in_=s2c[:]))
            se.op("vector", lambda h: h.tensor_copy(out=dbg[:, 36:44], in_=inv2c[:]))
            se.op("vector", lambda h: h.tensor_copy(out=dbg[:, 44:52], in_=invs[:]))
            se.op("vector", lambda h: h.tensor_copy(out=dbg[:, 52:60], in_=runsum[:]))
            se.dma("sync", dbg_out[:], dbg[:])
            for it in range(NT):
                se.dma("sync", hdbg_out[it * 128:(it + 1) * 128, :],
                       h_d[it * 128:(it + 1) * 128, :])
            for kk in range(INNER // 128):
                se.dma("sync", hqdbg_out[kk * 128:(kk + 1) * 128, :],
                       hq_d[kk * 128:(kk + 1) * 128, :])

        stats = big_f3[:, 0:24]
        mv = big_f3[:, 24:26]
        nmean = c1
        rstd = c2
        for it in range(NT):
            se.dma("sync", big_f1[:, 0:D], x1_out[it * 128:(it + 1) * 128, :])
            ctx.emit_lnstats(big_f1[:, 0:D], D, stats, mv, nmean, rstd)
            se.op("vector", lambda h: h.tensor_scalar(
                out=big_f2[:, 0:D], in0=big_f1[:, 0:D], scalar1=nmean[:, 0:1],
                scalar2=rstd[:, 0:1], op0=ALU.add, op1=ALU.mult))
            se.dma("sync", xn1_out[it * 128:(it + 1) * 128, :], big_f2[:, 0:D])

        se.finalize()
    return nc


def run_C(inputs, aux, prep):
    nc = build_C()
    xn0s = shard_rows(np.asarray(aux["xn0"]))
    xs = shard_rows(inputs["L0_attn_x"])
    ats = shard_rows(np.asarray(aux["attn0"]))
    in_maps = []
    for c in range(NCORES):
        in_maps.append({
            "xn0": xn0s[c], "x": xs[c], "attn0": ats[c],
            "w1t": prep["w1t"], "w2t": prep["w2t"],
            "b1": inputs["L0_ffn_b1"].reshape(1, -1),
            "lng": inputs["L0_ffn_ln_g"].reshape(1, -1),
            "lnb": inputs["L0_ffn_ln_b"].reshape(1, -1),
            "b2": inputs["L0_ffn_b2"].reshape(1, -1),
            "mcl2": np.asarray(aux["mcl"]).reshape(1, 2),
        })
    res = run_bass_kernel_spmd(nc, in_maps, core_ids=list(range(NCORES)))
    return {
        "x1": np.concatenate([res.results[c]["x1"] for c in range(NCORES)], 0),
        "xn1": np.concatenate([res.results[c]["xn1"] for c in range(NCORES)], 0),
    }



def build_D():
    nc = bass.Bass("TRN2", target_bir_lowering=False)
    xkv_in = nc.dram_tensor("xkv", [T, D], F32, kind="ExternalInput").ap()
    wq_in = nc.dram_tensor("Wq", [D, 256], F32, kind="ExternalInput").ap()
    wk_in = nc.dram_tensor("Wk", [D, 256], F32, kind="ExternalInput").ap()
    wv_in = nc.dram_tensor("Wv", [D, 256], F32, kind="ExternalInput").ap()
    wo_in = nc.dram_tensor("Wo", [256, D], F32, kind="ExternalInput").ap()
    wg_in = nc.dram_tensor("Wg", [D, D], F32, kind="ExternalInput").ap()
    ve0_in = nc.dram_tensor("ve0", [1, 256], F32, kind="ExternalInput").ap()
    ve1_in = nc.dram_tensor("ve1", [1, 256], F32, kind="ExternalInput").ap()
    km_in = nc.dram_tensor("kmask", [1, T], F32, kind="ExternalInput").ap()

    a_out = nc.dram_tensor("attn1", [RPC, D], F32, kind="ExternalOutput").ap()
    qp_d = nc.dram_tensor("qp_d", [16, 32 * RPC], BF16, kind="Internal").ap()
    kp_d = nc.dram_tensor("kp_d", [16, 32 * T], BF16, kind="Internal").ap()
    gate_d = nc.dram_tensor("gate_d", [RPC, D], F32, kind="Internal").ap()

    with ExitStack() as stack:
        ctx = Ctx(nc, stack)
        se = ctx.se
        idn_i = ctx.sbuf("idn_i", [128, 128], F32)
        ident_b = ctx.sbuf("ident_b", [128, 128], BF16)
        wq = ctx.sbuf("wq", [128, DK * 256], BF16)
        wk = ctx.sbuf("wk", [128, DK * 256], BF16)
        wv = ctx.sbuf("wv", [128, DK * 256], BF16)
        wo = ctx.sbuf("wo", [128, 2 * D], BF16)
        wgj = ctx.sbuf("wgj", [128, DK * 512], BF16)
        xt = ctx.sbuf("xt", [128, D], F32)
        xb = ctx.sbuf("xb", [128, D], BF16)
        fmt = ctx.sbuf("fmt", [128, DK * 128], BF16)
        fm_own = ctx.sbuf("fm_own", [128, NT * D], BF16)
        kq = ctx.sbuf("kq", [128, 256], F32)
        sa = ctx.sbuf("sa", [128, 256], BF16)
        sb_ = ctx.sbuf("sb_", [128, 256], BF16)
        trp = ctx.sbuf("trp", [128, 256], BF16)
        pv_tm = ctx.sbuf("pv_tm", [128, 16 * 256], BF16)
        kmr = ctx.sbuf("kmr", [128, T], F32)
        kmrow = ctx.sbuf("kmrow", [1, T], F32)
        verow = ctx.sbuf("verow", [1, 256], F32)
        ve0r = ctx.sbuf("ve0r", [128, 256], F32)
        dver = ctx.sbuf("dver", [128, 256], F32)
        qp_h = ctx.sbuf("qp_h", [16, RPC], BF16)
        kp_h = ctx.sbuf("kp_h", [16, T], BF16)
        t1b = ctx.sbuf("t1b", [128, 512], F32)
        getile = ctx.sbuf("getile", [128, 512], F32)
        srow = ctx.sbuf("srow", [128, T], F32)
        att_b = ctx.sbuf("att_b", [128, T], BF16)
        attT = ctx.sbuf("attT", [128, 1024], BF16)
        out1 = ctx.sbuf("out1", [128, NT * 256], F32)
        rs_r = ctx.sbuf("rs_r", [128, 1], F32)
        c1 = ctx.sbuf("c1", [128, 1], F32)
        ob = ctx.sbuf("ob", [128, 256], F32)
        obb = ctx.sbuf("obb", [128, 256], BF16)
        o_fm = ctx.sbuf("o_fm", [128, 256], BF16)
        gbuf = ctx.sbuf("gbuf", [128, 512], F32)
        att_f = ctx.sbuf("att_f", [128, 512], F32)

        pH = ctx.psum("pH", [128, 1024], BF16)
        pHd = ctx.psum("pHd", [128, 1024], BF16)
        pH2d = ctx.psum("pH2d", [128, 1024], BF16)
        pP = ctx.psum("pP", [128, 512], F32)
        pQ = ctx.psum("pQ", [128, 512], F32)
        pR = ctx.psum("pR", [128, 512], F32)

        ctx.emit_init()
        make_identity(ctx, idn_i, ident_b)

        se.dma("gpsimd", wq[:].rearrange("p (k f) -> p k f", f=256),
               wq_in.rearrange("(k p) f -> p k f", p=128))
        se.dma("gpsimd", wk[:].rearrange("p (k f) -> p k f", f=256),
               wk_in.rearrange("(k p) f -> p k f", p=128))
        se.dma("gpsimd", wv[:].rearrange("p (k f) -> p k f", f=256),
               wv_in.rearrange("(k p) f -> p k f", p=128))
        se.dma("gpsimd", wo[:].rearrange("p (k f) -> p k f", f=D),
               wo_in.rearrange("(k p) f -> p k f", p=128))
        se.dma("sync", kmrow[:], km_in[:])
        for j in range(4):
            se.op("tensor", lambda h, j=j: h.matmul(
                out=pP[:], lhsT=ctx.ones_row[:],
                rhs=kmrow[:, 512 * j:512 * (j + 1)], start=True, stop=True))
            se.op("vector", lambda h, j=j: h.tensor_copy(
                out=kmr[:, 512 * j:512 * (j + 1)], in_=pP[:]))
        se.dma("sync", verow[:], ve0_in[:])
        se.op("tensor", lambda h: h.matmul(
            out=pP[:, 0:256], lhsT=ctx.ones_row[:], rhs=verow[:],
            start=True, stop=True))
        se.op("vector", lambda h: h.tensor_copy(out=ve0r[:], in_=pP[:, 0:256]))
        se.dma("sync", verow[:], ve1_in[:])
        se.op("tensor", lambda h: h.matmul(
            out=pP[:, 0:256], lhsT=ctx.ones_row[:], rhs=verow[:],
            start=True, stop=True))
        se.op("vector", lambda h: h.tensor_copy(out=dver[:], in_=pP[:, 0:256]))
        se.op("vector", lambda h: h.tensor_tensor(
            out=dver[:], in0=dver[:], in1=ve0r[:], op=ALU.subtract))

        for st_ in range(16):
            se.dma("sync", xt[:], xkv_in[st_ * 128:(st_ + 1) * 128, :])
            se.op("vector", lambda h: h.tensor_copy(out=xb[:], in_=xt[:]))
            for k in range(DK):
                se.op("tensor", lambda h, k=k: h.transpose(
                    out=pH[:, 0:128], in_=xb[:, 128 * k:128 * (k + 1)],
                    identity=ident_b[:]))
                se.op("vector", lambda h, k=k: h.tensor_copy(
                    out=fmt[:, 128 * k:128 * (k + 1)], in_=pH[:, 0:128]))
            if st_ >= 8:
                se.op("vector", lambda h, st_=st_: h.tensor_copy(
                    out=fm_own[:, (st_ - 8) * D:(st_ - 7) * D], in_=fmt[:]))
            projs = [("k", wk, kp_d, T)]
            if st_ >= 8:
                projs.append(("q", wq, qp_d, RPC))
            for nm, wsb, dst, width in projs:
                for k in range(DK):
                    se.op("tensor", lambda h, k=k, wsb=wsb: h.matmul(
                        out=pP[:, 0:256], lhsT=fmt[:, 128 * k:128 * (k + 1)],
                        rhs=wsb[:, 256 * k:256 * (k + 1)],
                        start=(k == 0), stop=(k == DK - 1)))
                se.op("vector", lambda h: h.tensor_copy(out=kq[:], in_=pP[:, 0:256]))
                se.op("scalar", lambda h: h.activation(
                    out=sa[:], in_=kq[:], func=AF.Sigmoid))
                se.op("scalar", lambda h: h.activation(
                    out=sb_[:], in_=kq[:], func=AF.Sigmoid,
                    scale=ctx.const(-1.0)))
                soff = st_ * 128 if nm == "k" else (st_ - 8) * 128
                for sgn, src in ((0, sa), (8, sb_)):
                    for j in range(2):
                        se.op("tensor", lambda h, j=j, src=src: h.transpose(
                            out=pH[:, 0:128], in_=src[:, 128 * j:128 * (j + 1)],
                            identity=ident_b[:]))
                        se.op("vector", lambda h, j=j: h.tensor_copy(
                            out=trp[:, 128 * j:128 * (j + 1)], in_=pH[:, 0:128]))
                    for j in range(2):
                        dview = dst.rearrange("r (j2 g s) -> j2 g r s",
                                              j2=2, g=16)[
                            j, :, sgn:sgn + 8, soff:soff + 128]
                        se.dma("sync", dview,
                               trp[:, 128 * j:128 * (j + 1)])
            for k in range(DK):
                se.op("tensor", lambda h, k=k: h.matmul(
                    out=pP[:, 0:256], lhsT=fmt[:, 128 * k:128 * (k + 1)],
                    rhs=wv[:, 256 * k:256 * (k + 1)],
                    start=(k == 0), stop=(k == DK - 1)))
            se.op("scalar", lambda h, st_=st_: h.activation(
                out=pv_tm[:, 256 * st_:256 * (st_ + 1)], in_=pP[:, 0:256],
                func=AF.Sigmoid))

        for jc in range(4):
            se.dma("gpsimd", wgj[:].rearrange("p (k f) -> p k f", f=512),
                   wg_in[:, jc * 512:(jc + 1) * 512].rearrange(
                       "(k p) f -> p k f", p=128))
            for it in range(NT):
                for k in range(DK):
                    se.op("tensor", lambda h, k=k, it=it: h.matmul(
                        out=pQ[:], lhsT=fm_own[:, it * D + 128 * k:
                                               it * D + 128 * (k + 1)],
                        rhs=wgj[:, 512 * k:512 * (k + 1)],
                        start=(k == 0), stop=(k == DK - 1)))
                se.op("scalar", lambda h: h.activation(
                    out=gbuf[:], in_=pQ[:], func=AF.Sigmoid))
                se.dma("sync", gate_d[it * 128:(it + 1) * 128,
                                      jc * 512:(jc + 1) * 512], gbuf[:])

        se.op("gpsimd", lambda h: h.iota(
            t1b[:], pattern=[[-1, 512]], base=0, channel_multiplier=1,
            allow_small_or_imprecise_dtypes=True))

        for hh in range(32):
            se.dma("sync", qp_h[:], qp_d[:, hh * RPC:(hh + 1) * RPC])
            se.dma("sync", kp_h[:], kp_d[:, hh * T:(hh + 1) * T])
            amask_tok = None
            for qt in range(NT):
                t0g = 1024 + qt * 128
                chd = t0g // 512
                base0 = se.last
                d0g = float(t0g - chd * 512)
                a1 = se.op("vector", lambda h, d0g=d0g: h.tensor_scalar(
                    out=getile[:], in0=t1b[:], scalar1=d0g,
                    scalar2=0.0, op0=ALU.add, op1=ALU.is_ge), deps=[base0])
                a2 = se.op("vector", lambda h: h.tensor_scalar(
                    out=getile[:], in0=getile[:], scalar1=1e30,
                    scalar2=-1e30, op0=ALU.mult, op1=ALU.add), deps=[a1])
                amask_tok = se.op("vector", lambda h, s0=chd * 512:
                                  h.tensor_tensor(
                    out=getile[:], in0=getile[:],
                    in1=kmr[:, s0:s0 + 512], op=ALU.add), deps=[a2])
                ev_toks = []
                mm_tok = {}
                for ch in range(chd + 1):
                    s0 = ch * 512
                    pp = pP if ch % 2 == 0 else pQ
                    mdeps = [base0]
                    if ch >= 2:
                        mdeps = [ev_toks[ch - 2]]
                    mm_tok[ch] = se.op("tensor", lambda h, qt=qt, ch=ch,
                                       pp=pp: h.matmul(
                        out=pp[:], lhsT=qp_h[:, qt * 128:(qt + 1) * 128],
                        rhs=kp_h[:, ch * 512:(ch + 1) * 512],
                        start=True, stop=True), deps=mdeps)
                    if ch == chd:
                        ev = se.op("vector", lambda h, s0=s0, pp=pp:
                                   h.tensor_tensor(
                            out=srow[:, s0:s0 + 512], in0=pp[:],
                            in1=getile[:], op=ALU.add),
                            deps=[mm_tok[ch], amask_tok])
                    else:
                        ev = se.op("vector", lambda h, s0=s0, pp=pp:
                                   h.tensor_tensor(
                            out=srow[:, s0:s0 + 512], in0=pp[:],
                            in1=kmr[:, s0:s0 + 512], op=ALU.add),
                            deps=[mm_tok[ch]])
                    ev_toks.append(ev)
                se.op("vector", lambda h: h.engine_nop(), deps=[ev_toks[-1]],
                      close=True)
                used = (chd + 1) * 512
                se.op("scalar", lambda h, used=used: h.activation(
                    out=att_b[:, 0:used], in_=srow[:, 0:used], func=AF.Exp,
                    bias=ctx.const(-16.0), accum_out=rs_r[:]))
                se.op("vector", lambda h: h.reciprocal(out=rs_r[:], in_=rs_r[:]))
                nst = used // 128
                base = se.last
                ttoks = []
                etoks = {}
                mtoks = []
                for j in range(nst):
                    g = (j // 4) % 2
                    ph = pHd if g == 0 else pH2d
                    jj = j % 4
                    deps = [base]
                    if (j // 4) >= 2:
                        deps.append(etoks[(j // 4) - 2])
                    lastj = (jj == 3 or j == nst - 1)
                    t_tok = se.op("tensor", lambda h, j=j, jj=jj, ph=ph,
                                  lastj=lastj: h.matmul(
                        out=ph[:, jj * 128:(jj + 1) * 128],
                        lhsT=att_b[:, 128 * j:128 * (j + 1)],
                        rhs=ident_b[:], is_transpose=True,
                        start=(jj == 0), stop=lastj), deps=deps)
                    ttoks.append(t_tok)
                    if jj == 3 or j == nst - 1:
                        grp = j // 4
                        e_tok = se.op("vector", lambda h, g=g, ph=ph: h.tensor_copy(
                            out=attT[:, g * 512:(g + 1) * 512],
                            in_=ph[:, 0:512]),
                            deps=[t_tok])
                        etoks[grp] = e_tok
                        for j2 in range(grp * 4, min(grp * 4 + 4, nst)):
                            m_tok = se.op("tensor", lambda h, j2=j2, g=g,
                                          hh=hh, nst=nst: h.matmul(
                                out=pR[:, 0:8],
                                lhsT=attT[:, g * 512 + (j2 % 4) * 128:
                                          g * 512 + (j2 % 4 + 1) * 128],
                                rhs=pv_tm[:, j2 * 256 + hh * 8:
                                          j2 * 256 + hh * 8 + 8],
                                start=(j2 == 0), stop=(j2 == nst - 1)),
                                deps=[e_tok])
                            mtoks.append(m_tok)
                se.op("vector", lambda h, qt=qt, hh=hh: h.tensor_scalar(
                    out=out1[:, qt * 256 + hh * 8:qt * 256 + hh * 8 + 8],
                    in0=pR[:, 0:8], scalar1=rs_r[:, 0:1], scalar2=None,
                    op0=ALU.mult), deps=[mtoks[-1]], close=True)

        for qt in range(NT):
            se.op("vector", lambda h, qt=qt: h.tensor_tensor(
                out=ob[:], in0=out1[:, qt * 256:(qt + 1) * 256], in1=dver[:],
                op=ALU.mult))
            se.op("vector", lambda h: h.tensor_tensor(
                out=ob[:], in0=ob[:], in1=ve0r[:], op=ALU.add))
            se.op("vector", lambda h: h.tensor_copy(out=obb[:], in_=ob[:]))
            for j in range(2):
                se.op("tensor", lambda h, j=j: h.transpose(
                    out=pH[:, 0:128], in_=obb[:, 128 * j:128 * (j + 1)],
                    identity=ident_b[:]))
                se.op("vector", lambda h, j=j: h.tensor_copy(
                    out=o_fm[:, 128 * j:128 * (j + 1)], in_=pH[:, 0:128]))
            for jc in range(4):
                for k in range(2):
                    se.op("tensor", lambda h, jc=jc, k=k: h.matmul(
                        out=pQ[:], lhsT=o_fm[:, 128 * k:128 * (k + 1)],
                        rhs=wo[:, k * D + jc * 512:k * D + jc * 512 + 512],
                        start=(k == 0), stop=(k == 1)))
                se.dma("sync", gbuf[:], gate_d[qt * 128:(qt + 1) * 128,
                                               jc * 512:(jc + 1) * 512])
                se.op("vector", lambda h: h.tensor_tensor(
                    out=att_f[:], in0=pQ[:], in1=gbuf[:], op=ALU.mult))
                se.dma("sync", a_out[qt * 128:(qt + 1) * 128,
                                     jc * 512:(jc + 1) * 512], att_f[:])

        se.finalize()
    return nc


def run_D(inputs, aux):
    nc = build_D()
    xn1 = np.asarray(aux["xn1"]).reshape(B, T, D)
    in_maps = []
    for c in range(NCORES):
        b, half = c // 2, c % 2
        if half == 1:
            xkv = xn1[b]
            km = np.zeros((1, T), np.float32)
        else:
            xkv = np.concatenate([np.zeros_like(xn1[b][0:1024]),
                                  xn1[b][0:1024]], 0)
            km = np.zeros((1, T), np.float32)
            km[0, 0:1024] = -1e30
        in_maps.append({
            "xkv": np.ascontiguousarray(xkv), "kmask": km,
            "Wq": np.ascontiguousarray(inputs["L1_attn_Wq"]),
            "Wk": np.ascontiguousarray(inputs["L1_attn_Wk"]),
            "Wv": np.ascontiguousarray(inputs["L1_attn_Wv"]),
            "Wo": np.ascontiguousarray(inputs["L1_attn_Wo"]),
            "Wg": np.ascontiguousarray(inputs["L1_attn_Wg"]),
            "ve0": inputs["L1_attn_v_emb0"].reshape(1, -1),
            "ve1": inputs["L1_attn_v_emb1"].reshape(1, -1),
        })
    res = run_bass_kernel_spmd(nc, in_maps, core_ids=list(range(NCORES)))
    return np.concatenate([res.results[c]["attn1"] for c in range(NCORES)], 0)



def emit_mx_quant(ctx, t_ap, out_ap, W, m1, m2, m3, m4, m5, mi, bmax, sb, rs):
    se = ctx.se
    nb = W // 32
    se.op("vector", lambda h: h.reduce_max(
        out=bmax, in_=t_ap.rearrange("p (w k) -> p w k", k=32), axis=AX.X,
        apply_absolute_value=True))
    se.op("vector", lambda h: h.tensor_scalar(
        out=sb, in0=bmax, scalar1=1.0 / 448.0, scalar2=1e-30,
        op0=ALU.mult, op1=ALU.max))
    se.op("vector", lambda h: h.reciprocal(out=rs, in_=sb))
    t3 = t_ap.rearrange("p (w k) -> p w k", k=32)
    rs3 = rs.rearrange("p (w k) -> p w k", k=1)
    m13 = m1.rearrange("p (w k) -> p w k", k=32)
    a3b, b3b = bass.broadcast_tensor_aps(t3, rs3)
    se.op("vector", lambda h: h.tensor_tensor(out=m13, in0=a3b, in1=b3b,
                                              op=ALU.mult))
    se.op("scalar", lambda h: h.activation(out=m2, in_=m1, func=AF.Square))
    se.op("vector", lambda h: h.tensor_scalar_max(out=m2, in0=m2,
                                                  scalar1=1e-38))
    se.op("scalar", lambda h: h.activation(out=m2, in_=m2, func=AF.Ln))
    se.op("vector", lambda h: h.tensor_scalar(
        out=m2, in0=m2, scalar1=0.5 / LN2, scalar2=20.0,
        op0=ALU.mult, op1=ALU.add))
    se.op("vector", lambda h: h.tensor_copy(out=mi, in_=m2))
    se.op("vector", lambda h: h.tensor_tensor(out=m3, in0=m2, in1=mi,
                                              op=ALU.subtract))
    se.op("vector", lambda h: h.tensor_scalar(
        out=m3, in0=m3, scalar1=0.0, scalar2=None, op0=ALU.is_lt))
    se.op("vector", lambda h: h.scalar_tensor_tensor(
        out=m2, in0=m3, scalar=-1.0, in1=mi, op0=ALU.mult, op1=ALU.add))
    se.op("vector", lambda h: h.tensor_scalar_max(out=m2, in0=m2,
                                                  scalar1=14.0))
    se.op("scalar", lambda h: h.activation(
        out=m3, in_=m2, func=AF.Exp, scale=ctx.const(-LN2),
        bias=ctx.const(23.0 * LN2)))
    se.op("vector", lambda h: h.tensor_tensor(out=m1, in0=m1, in1=m3,
                                              op=ALU.mult))
    ctx.emit_round(m1, m5, mi, m4, m3)
    se.op("scalar", lambda h: h.activation(
        out=m3, in_=m2, func=AF.Exp, scale=ctx.const(LN2),
        bias=ctx.const(-23.0 * LN2)))
    se.op("vector", lambda h: h.tensor_tensor(out=m1, in0=m5, in1=m3,
                                              op=ALU.mult))
    m13b = m1.rearrange("p (w k) -> p w k", k=32)
    sb3 = sb.rearrange("p (w k) -> p w k", k=1)
    o3 = out_ap.rearrange("p (w k) -> p w k", k=32)
    a3c, b3c = bass.broadcast_tensor_aps(m13b, sb3)
    se.op("vector", lambda h: h.tensor_tensor(out=o3, in0=a3c, in1=b3c,
                                              op=ALU.mult))


def build_E():
    nc = bass.Bass("TRN2", target_bir_lowering=False)
    xn1_in = nc.dram_tensor("xn1", [RPC, D], F32, kind="ExternalInput").ap()
    x1_in = nc.dram_tensor("x1", [RPC, D], F32, kind="ExternalInput").ap()
    a1_in = nc.dram_tensor("attn1", [RPC, D], F32, kind="ExternalInput").ap()
    gq_t = nc.dram_tensor("gq_t", [D, INNER], BF16, kind="ExternalInput").ap()
    uq_t = nc.dram_tensor("uq_t", [D, INNER], BF16, kind="ExternalInput").ap()
    dq_t = nc.dram_tensor("dq_t", [INNER, D], BF16, kind="ExternalInput").ap()

    out_o = nc.dram_tensor("out", [RPC, D], F32, kind="ExternalOutput").ap()
    hq_d2 = nc.dram_tensor("hq_d2", [INNER, RPC], BF16, kind="Internal").ap()

    with ExitStack() as stack:
        ctx = Ctx(nc, stack)
        se = ctx.se
        idn_i = ctx.sbuf("idn_i", [128, 128], F32)
        ident_b = ctx.sbuf("ident_b", [128, 128], BF16)
        xq_fm = ctx.sbuf("xq_fm", [128, DK * RPC], BF16)
        wld = ctx.sbuf("wld", [128, INNER], F32)
        m1 = ctx.sbuf("m1", [128, D], F32)
        m2 = ctx.sbuf("m2", [128, D], F32)
        m3 = ctx.sbuf("m3", [128, D], F32)
        m4 = ctx.sbuf("m4", [128, D], F32)
        m5 = ctx.sbuf("m5", [128, D], F32)
        mi = ctx.sbuf("mi", [128, D], I32)
        bmax = ctx.sbuf("bmax", [128, 64], F32)
        sbs = ctx.sbuf("sbs", [128, 64], F32)
        rss = ctx.sbuf("rss", [128, 64], F32)
        qb = ctx.sbuf("qb", [128, D], BF16)
        grhs = ctx.sbuf("grhs", [128, INNER], BF16)
        urhs = ctx.sbuf("urhs", [128, INNER], BF16)
        hbuf = ctx.sbuf("hbuf", [128, 512], F32)
        hbuf4 = ctx.sbuf("hbuf4", [128, 2048], F32)
        qb4 = ctx.sbuf("qb4", [128, 2048], BF16)
        stg16 = ctx.sbuf("stg16", [128, 2048], BF16)
        hqv = ctx.sbuf("hqv", [128, 512], BF16)
        stg4 = ctx.sbuf("stg4", [128, 512], BF16)
        hql2 = ctx.sbuf("hql2", [128, 1024], BF16)
        resb = ctx.sbuf("resb", [128, 512], F32)
        ob = ctx.sbuf("ob", [128, 512], F32)

        pA = ctx.psum("pA", [128, 512], F32)
        pB = ctx.psum("pB", [128, 512], F32)
        pC = ctx.psum("pC", [128, 512], F32)
        pD = ctx.psum("pD", [128, 512], F32)
        pH = ctx.psum("pH", [128, 1024], BF16)
        pH2 = ctx.psum("pH2", [128, 1024], BF16)
        pds = [pA, pB, pC, pD]

        ctx.emit_init()
        make_identity(ctx, idn_i, ident_b)

        for it in range(NT):
            se.dma("sync", m1[:], xn1_in[it * 128:(it + 1) * 128, :])
            se.op("vector", lambda h: h.tensor_copy(out=m4[:], in_=m1[:]))
            emit_mx_quant(ctx, m4[:], qb[:], D, m1[:], m2[:], m3[:], m4[:],
                          m5[:], mi[:], bmax[:], sbs[:], rss[:])
            for k in range(DK):
                se.op("tensor", lambda h, k=k: h.transpose(
                    out=pH[:, 0:128], in_=qb[:, 128 * k:128 * (k + 1)],
                    identity=ident_b[:]))
                se.op("vector", lambda h, k=k, it=it: h.tensor_copy(
                    out=xq_fm[:, k * RPC + it * 128:k * RPC + (it + 1) * 128],
                    in_=pH[:, 0:128]))

        free_tok = None
        tail_tok = None
        store_tok = None
        mm_last = None
        W = 4 * 512
        nb = W // 32
        for wg in range(16):
            rdeps = [se.last] if wg == 0 else [mm_last]
            se.dma("sync", grhs[:].rearrange("p (k c) -> p k c", c=512),
                   gq_t.rearrange("(k p) c -> p k c", p=128)[
                       :, :, wg * 512:(wg + 1) * 512], deps=rdeps)
            rhs_tok = se.dma("sync", urhs[:].rearrange("p (k c) -> p k c", c=512),
                             uq_t.rearrange("(k p) c -> p k c", p=128)[
                                 :, :, wg * 512:(wg + 1) * 512], deps=rdeps)
            for g4 in range(2):
                hm_tok = None
                for i4 in range(4):
                    it = g4 * 4 + i4
                    mdeps = [rhs_tok] + ([free_tok] if free_tok else [])
                    for k in range(DK):
                        se.op("tensor", lambda h, k=k, it=it: h.matmul(
                            out=pA[:], lhsT=xq_fm[:, k * RPC + it * 128:
                                                  k * RPC + (it + 1) * 128],
                            rhs=grhs[:, 512 * k:512 * (k + 1)],
                            start=(k == 0), stop=(k == DK - 1)), deps=mdeps)
                    for k in range(DK):
                        mm_last = se.op("tensor", lambda h, k=k, it=it: h.matmul(
                            out=pB[:], lhsT=xq_fm[:, k * RPC + it * 128:
                                                  k * RPC + (it + 1) * 128],
                            rhs=urhs[:, 512 * k:512 * (k + 1)],
                            start=(k == 0), stop=(k == DK - 1)), deps=mdeps)
                    sl = slice(i4 * 512, (i4 + 1) * 512)
                    cdeps = [mm_last] + ([tail_tok] if tail_tok else [])
                    t = se.op("scalar", lambda h, sl=sl: h.activation(
                        out=hbuf4[:, sl], in_=pA[:], func=AF.Exp,
                        scale=ctx.const(-1.0)), deps=cdeps)
                    t = se.op("vector", lambda h, sl=sl: h.tensor_scalar(
                        out=hbuf4[:, sl], in0=hbuf4[:, sl], scalar1=1.0,
                        scalar2=None, op0=ALU.add), deps=[t])
                    t = se.op("vector", lambda h, sl=sl: h.reciprocal(
                        out=hbuf4[:, sl], in_=hbuf4[:, sl]), deps=[t])
                    t = se.op("vector", lambda h, sl=sl: h.tensor_tensor(
                        out=hbuf4[:, sl], in0=hbuf4[:, sl], in1=pA[:],
                        op=ALU.mult), deps=[t])
                    free_tok = t = se.op("vector", lambda h, sl=sl: h.tensor_tensor(
                        out=hbuf4[:, sl], in0=hbuf4[:, sl], in1=pB[:],
                        op=ALU.mult), deps=[t])
                    hm_tok = t
                t = hm_tok
                t = se.op("vector", lambda h: h.reduce_max(
                    out=bmax[:, 0:nb],
                    in_=hbuf4[:].rearrange("p (w k) -> p w k", k=32),
                    axis=AX.X, apply_absolute_value=True), deps=[t])
                t = se.op("vector", lambda h: h.tensor_scalar(
                    out=sbs[:, 0:nb], in0=bmax[:, 0:nb], scalar1=1.0 / 448.0,
                    scalar2=1e-30, op0=ALU.mult, op1=ALU.max), deps=[t])
                t = se.op("vector", lambda h: h.reciprocal(
                    out=rss[:, 0:nb], in_=sbs[:, 0:nb]), deps=[t])
                t3 = hbuf4[:].rearrange("p (w k) -> p w k", k=32)
                rs3 = rss[:, 0:nb].rearrange("p (w k) -> p w k", k=1)
                m13 = m1[:].rearrange("p (w k) -> p w k", k=32)
                a3b, b3b = bass.broadcast_tensor_aps(t3, rs3)
                t = se.op("vector", lambda h: h.tensor_tensor(
                    out=m13, in0=a3b, in1=b3b, op=ALU.mult), deps=[t])
                t = se.op("scalar", lambda h: h.activation(
                    out=m2[:], in_=m1[:], func=AF.Square), deps=[t])
                t = se.op("vector", lambda h: h.tensor_scalar_max(
                    out=m2[:], in0=m2[:], scalar1=1e-38), deps=[t])
                t = se.op("scalar", lambda h: h.activation(
                    out=m2[:], in_=m2[:], func=AF.Ln), deps=[t])
                t = se.op("vector", lambda h: h.tensor_scalar(
                    out=m2[:], in0=m2[:], scalar1=0.5 / LN2, scalar2=20.0,
                    op0=ALU.mult, op1=ALU.add), deps=[t])
                t = se.op("vector", lambda h: h.tensor_copy(
                    out=mi[:], in_=m2[:]), deps=[t])
                t = se.op("vector", lambda h: h.tensor_tensor(
                    out=m3[:], in0=m2[:], in1=mi[:], op=ALU.subtract), deps=[t])
                t = se.op("vector", lambda h: h.tensor_scalar(
                    out=m3[:], in0=m3[:], scalar1=0.0, scalar2=None,
                    op0=ALU.is_lt), deps=[t])
                t = se.op("vector", lambda h: h.scalar_tensor_tensor(
                    out=m2[:], in0=m3[:], scalar=-1.0, in1=mi[:],
                    op0=ALU.mult, op1=ALU.add), deps=[t])
                t = se.op("vector", lambda h: h.tensor_scalar_max(
                    out=m2[:], in0=m2[:], scalar1=14.0), deps=[t])
                t = se.op("scalar", lambda h: h.activation(
                    out=m3[:], in_=m2[:], func=AF.Exp, scale=ctx.const(-LN2),
                    bias=ctx.const(23.0 * LN2)), deps=[t])
                t = se.op("vector", lambda h: h.tensor_tensor(
                    out=m1[:], in0=m1[:], in1=m3[:], op=ALU.mult), deps=[t])
                t = se.op("vector", lambda h: h.tensor_copy(
                    out=mi[:], in_=m1[:]), deps=[t])
                t = se.op("vector", lambda h: h.tensor_tensor(
                    out=m4[:], in0=m1[:], in1=mi[:], op=ALU.subtract), deps=[t])
                t = se.op("vector", lambda h: h.tensor_scalar(
                    out=m3[:], in0=m4[:], scalar1=-0.5, scalar2=None,
                    op0=ALU.is_lt), deps=[t])
                t = se.op("vector", lambda h: h.scalar_tensor_tensor(
                    out=m4[:], in0=m4[:], scalar=0.5, in1=m3[:],
                    op0=ALU.is_gt, op1=ALU.subtract), deps=[t])
                t = se.op("vector", lambda h: h.tensor_tensor(
                    out=m5[:], in0=mi[:], in1=m4[:], op=ALU.add), deps=[t])
                t = se.op("scalar", lambda h: h.activation(
                    out=m3[:], in_=m2[:], func=AF.Exp, scale=ctx.const(LN2),
                    bias=ctx.const(-23.0 * LN2)), deps=[t])
                t = se.op("vector", lambda h: h.tensor_tensor(
                    out=m1[:], in0=m5[:], in1=m3[:], op=ALU.mult), deps=[t])
                m13b = m1[:].rearrange("p (w k) -> p w k", k=32)
                sb3 = sbs[:, 0:nb].rearrange("p (w k) -> p w k", k=1)
                o3 = qb4[:].rearrange("p (w k) -> p w k", k=32)
                a3c, b3c = bass.broadcast_tensor_aps(m13b, sb3)
                t = se.op("vector", lambda h: h.tensor_tensor(
                    out=o3, in0=a3c, in1=b3c, op=ALU.mult), deps=[t])
                tail_tok = t
                ttoks = []
                for half in range(2):
                    ph = pH if half == 0 else pH2
                    tt0 = None
                    for j in range(8):
                        idx = half * 8 + j
                        it4, kk4 = divmod(idx, 4)
                        src = qb4[:, it4 * 512 + kk4 * 128:
                                  it4 * 512 + (kk4 + 1) * 128]
                        prevev = [ttoks[-2]] if len(ttoks) >= 2 else []
                        tt0 = se.op("tensor", lambda h, j=j, src=src, ph=ph:
                                    h.matmul(
                            out=ph[:, j * 128:(j + 1) * 128], lhsT=src,
                            rhs=ident_b[:], is_transpose=True,
                            start=(j == 0), stop=(j == 7)),
                            deps=[t] + prevev)
                    evd = [tt0] + ([store_tok] if store_tok else [])
                    ev = se.op("vector", lambda h, half=half, ph=ph:
                               h.tensor_copy(
                        out=stg16[:, half * 1024:(half + 1) * 1024],
                        in_=ph[:, 0:1024]), deps=evd)
                    ttoks.append(ev)
                for i4 in range(4):
                    itg = g4 * 4 + i4
                    store_tok = se.dma(
                        "sync",
                        hq_d2[wg * 512:(wg + 1) * 512,
                              itg * 128:(itg + 1) * 128].rearrange(
                            "(k p) c -> p k c", p=128),
                        stg16[:, i4 * 512:(i4 + 1) * 512].rearrange(
                            "p (k c) -> p k c", c=128),
                        deps=ttoks)
        se.op("vector", lambda h: h.engine_nop(), deps=se.fence(), close=True)

        for dch in range(4):
            for tp in range(2):
                nk = INNER // 128
                dtoks = {}
                mtoks = {}
                base = se.last
                for k in range(nk + 1):
                    if k < nk:
                        par = k % 2
                        deps = [base] if k < 2 else [mtoks[k - 2]]
                        se.dma("sync", qb[:, par * 512:par * 512 + 512],
                               dq_t[k * 128:(k + 1) * 128,
                                    dch * 512:(dch + 1) * 512], deps=deps)
                        dtoks[k] = se.dma(
                            "sync", hql2[:, par * 512:par * 512 + 512],
                            hq_d2[k * 128:(k + 1) * 128,
                                  tp * 512:(tp + 1) * 512], deps=deps)
                    if k >= 1:
                        kk = k - 1
                        par = kk % 2
                        for tt in range(4):
                            mtoks[kk] = se.op(
                                "tensor", lambda h, kk=kk, tt=tt, par=par: h.matmul(
                                    out=pds[tt][:],
                                    lhsT=hql2[:, par * 512 + tt * 128:
                                              par * 512 + (tt + 1) * 128],
                                    rhs=qb[:, par * 512:par * 512 + 512],
                                    start=(kk == 0), stop=(kk == nk - 1)),
                                deps=[dtoks[kk]])
                se.op("vector", lambda h: h.engine_nop(), deps=[mtoks[nk - 1]],
                      close=True)
                for tt in range(4):
                    it = tp * 4 + tt
                    se.dma("sync", resb[:],
                           x1_in[it * 128:(it + 1) * 128,
                                 dch * 512:(dch + 1) * 512])
                    se.op("vector", lambda h, tt=tt: h.tensor_tensor(
                        out=ob[:], in0=pds[tt][:], in1=resb[:], op=ALU.add))
                    se.dma("sync", resb[:],
                           a1_in[it * 128:(it + 1) * 128,
                                 dch * 512:(dch + 1) * 512])
                    se.op("vector", lambda h: h.tensor_tensor(
                        out=ob[:], in0=ob[:], in1=resb[:], op=ALU.add))
                    se.dma("sync", out_o[it * 128:(it + 1) * 128,
                                         dch * 512:(dch + 1) * 512], ob[:])

        se.finalize()
    return nc


def run_E(inputs, aux, prep):
    nc = build_E()
    xn1s = shard_rows(np.asarray(aux["xn1"]))
    x1s = shard_rows(np.asarray(aux["x1"]))
    a1s = shard_rows(np.asarray(aux["attn1"]))
    in_maps = []
    for c in range(NCORES):
        in_maps.append({
            "xn1": xn1s[c], "x1": x1s[c], "attn1": a1s[c],
            "gq_t": prep["gq"], "uq_t": prep["uq"], "dq_t": prep["dq"],
        })
    res = run_bass_kernel_spmd(nc, in_maps, core_ids=list(range(NCORES)))
    return np.concatenate([res.results[c]["out"] for c in range(NCORES)], 0)


def kernel(**inputs):
    aux = run_A(inputs)
    prep = run_P(inputs, aux)
    aux["attn0"] = run_B(inputs, aux)
    auxC = run_C(inputs, aux, prep)
    aux.update(auxC)
    aux["attn1"] = run_D(inputs, aux)
    out = run_E(inputs, aux, prep)
    return out.reshape(B, T, D)


# version 09
# speedup vs baseline: 1.1851x; 1.0031x over previous
import numpy as np
import ml_dtypes
from contextlib import ExitStack

import concourse.bass as bass
import concourse.mybir as mybir
from concourse.bass_utils import run_bass_kernel_spmd

F32 = mybir.dt.float32
BF16 = mybir.dt.bfloat16
I32 = mybir.dt.int32
AF = mybir.ActivationFunctionType
ALU = mybir.AluOpType
AX = mybir.AxisListType

B, T, D = 4, 2048, 2048
B_DEBUG = False
B_DEBUG_QT = 7
C_DEBUG = False
C_DEBUG_LITE = False
C_SCRATCH_INTERNAL = True
NCORES = 8
RPC = (B * T) // NCORES
NT = RPC // 128
DK = D // 128
INNER = 8192
EPS = 1e-5
LN2 = float(np.log(2.0))

CONST_VALS = sorted(set([EPS, -0.5, 0.5, -1.0, 1.0, 2.0, LN2, -LN2, 1.0 / LN2, 1e-38,
                         0.5 / LN2, 127.0, 1.0 / 127.0, 1e-30, -1e30, 1.0 / 448.0,
                         20.0, 14.0, -16.0, 23.0 * LN2, -23.0 * LN2, 1.0 / 2048.0, 1.0 / 8192.0,
                         1.0 / (INNER * D), 1.0 / INNER]))

ENGINES = ("sync", "vector", "scalar", "tensor", "gpsimd")


class SE:

    def __init__(self, nc, stack):
        self.nc = nc
        self.stack = stack
        self.ops = {e: [] for e in ENGINES}
        self.count = {e: 0 for e in ENGINES}
        self.dmacount = {e: 0 for e in ENGINES}
        self.esem = {e: stack.enter_context(nc.semaphore(f"es_{e}"))
                     for e in ENGINES}
        self.dsem = {e: stack.enter_context(nc.semaphore(f"ds_{e}"))
                     for e in ENGINES}
        self.last = None
        self.extra_sem = {}

    def _waits(self, eng):
        w = []
        if self.last is None:
            return w
        kind = self.last[0]
        if kind == "dma":
            _, deng, dcount = self.last
            w.append((self.dsem[deng], 16 * dcount))
        else:
            _, peng, pcount = self.last
            w.append((self.esem[peng], pcount))
        return w

    def _dep_waits(self, eng, deps):
        w = []
        if self.count[eng] > 0:
            w.append((self.esem[eng], self.count[eng]))
        seen = {}
        for d in deps:
            if d is None:
                continue
            kind = d[0]
            if kind == "dma":
                _, deng, dcount = d
                key = ("d", deng)
                seen[key] = max(seen.get(key, 0), 16 * dcount)
            else:
                _, peng, pcount = d
                if peng == eng:
                    continue
                key = ("e", peng)
                seen[key] = max(seen.get(key, 0), pcount)
        for (k, e2), v in seen.items():
            w.append(((self.dsem if k == "d" else self.esem)[e2], v))
        return w

    def op(self, eng, fn, deps=None, close=False):
        w = self._dep_waits(eng, deps) if deps is not None else self._waits(eng)
        self.count[eng] += 1
        n = self.count[eng]
        sem = self.esem[eng]

        def run(handle, w=w, fn=fn, sem=sem, n=n):
            for s, v in w:
                handle.wait_ge(s, v)
            inst = fn(handle)
            inst.then_inc(sem, 1)
        self.ops[eng].append(run)
        tok = ("eng", eng, n)
        if deps is None or close:
            self.last = tok
        return tok

    def dma(self, eng, out, in_, deps=None, close=False, **kw):
        w = self._dep_waits(eng, deps) if deps is not None else self._waits(eng)
        self.dmacount[eng] += 1
        n = self.dmacount[eng]
        sem = self.dsem[eng]

        def run(handle, w=w, out=out, in_=in_, sem=sem, kw=kw):
            for s, v in w:
                handle.wait_ge(s, v)
            handle.dma_start(out=out, in_=in_, **kw).then_inc(sem, 16)
        self.ops[eng].append(run)
        tok = ("dma", eng, n)
        if deps is None or close:
            self.last = tok
        return tok

    def fence(self):
        toks = []
        for e in ENGINES:
            if self.count[e]:
                toks.append(("eng", e, self.count[e]))
            if self.dmacount[e]:
                toks.append(("dma", e, self.dmacount[e]))
        return toks

    def barrier_wait(self, eng):
        snap = dict(self.count), dict(self.dmacount)

        def run(handle, snap=snap):
            counts, dcounts = snap
            for e in ENGINES:
                if counts[e] and e != eng:
                    handle.wait_ge(self.esem[e], counts[e])
                if dcounts[e]:
                    handle.wait_ge(self.dsem[e], 16 * dcounts[e])
        self.ops[eng].append(run)

    def finalize(self):
        nc = self.nc
        with nc.Block() as block:
            for e in ENGINES:
                lst = self.ops[e]
                if not lst and e != "sync":
                    continue

                def body(handle, lst=lst, e=e):
                    for fn in lst:
                        fn(handle)
                    if e == "sync":
                        for e2 in ENGINES:
                            if self.dmacount[e2]:
                                handle.wait_ge(self.dsem[e2],
                                               16 * self.dmacount[e2])
                            if self.count[e2] and e2 != "sync":
                                handle.wait_ge(self.esem[e2], self.count[e2])
                getattr(block, e)(body)


class Ctx:
    def __init__(self, nc, stack):
        self.nc = nc
        self.stack = stack
        self.se = SE(nc, stack)
        self.cpool = self.sbuf("cpool", [128, len(CONST_VALS)], F32)
        self.cidx = {float(v): j for j, v in enumerate(CONST_VALS)}
        self.ones_row = self.sbuf("ones_row", [1, 128], F32)
        self.ones_row_b = self.sbuf("ones_row_b", [1, 128], BF16)
        self.ones_col = self.sbuf("ones_col", [128, 1], F32)
        self._scr = {}

    def sbuf(self, name, shape, dtype):
        return self.stack.enter_context(self.nc.sbuf_tensor(name, shape, dtype))

    def psum(self, name, shape, dtype=F32):
        return self.stack.enter_context(self.nc.psum_tensor(name, shape, dtype))

    def const(self, val):
        val = float(val)
        return self.cpool[:, self.cidx[val]:self.cidx[val] + 1]

    def emit_init(self):
        se = self.se
        for j, v in enumerate(CONST_VALS):
            se.op("vector", lambda h, j=j, v=v: h.memset(
                self.cpool[:, j:j + 1], float(v)))
            self.nc.const_aps.register(F32, float(v), self.cpool[:, j:j + 1])
        se.op("vector", lambda h: h.memset(self.ones_row[:], 1.0))
        se.op("vector", lambda h: h.memset(self.ones_row_b[:], 1.0))
        se.op("vector", lambda h: h.memset(self.ones_col[:], 1.0))

    def emit_bcast(self, row_ap, out_psum_ap):
        n = row_ap.shape[-1]
        self.se.op("tensor", lambda h: h.matmul(
            out=out_psum_ap, lhsT=self.ones_row[:, 0:128], rhs=row_ap,
            start=True, stop=True))

    def emit_round(self, t_ap, out_ap, scr_i32, scr_a, scr_b):
        se = self.se
        se.op("vector", lambda h: h.tensor_copy(out=scr_i32, in_=t_ap))
        se.op("vector", lambda h: h.tensor_tensor(
            out=scr_a, in0=t_ap, in1=scr_i32, op=ALU.subtract))
        se.op("vector", lambda h: h.tensor_scalar(
            out=scr_b, in0=scr_a, scalar1=-0.5, scalar2=None, op0=ALU.is_lt))
        se.op("vector", lambda h: h.scalar_tensor_tensor(
            out=scr_a, in0=scr_a, scalar=0.5, in1=scr_b,
            op0=ALU.is_gt, op1=ALU.subtract))
        se.op("vector", lambda h: h.tensor_tensor(
            out=out_ap, in0=scr_i32, in1=scr_a, op=ALU.add))

    def emit_lnstats(self, x_ap, w, stats, mv, nmean, rstd, eps=EPS):
        se = self.se
        ng = w // 512
        for g in range(ng):
            se.op("vector", lambda h, g=g: h.bn_stats(
                out=stats[:, 6 * g:6 * (g + 1)], in_=x_ap[:, 512 * g:512 * (g + 1)]))
        se.op("vector", lambda h: h.bn_aggr(out=mv[:], in_=stats[:, 0:6 * ng]))
        se.op("vector", lambda h: h.tensor_scalar(
            out=nmean[:], in0=mv[:, 0:1], scalar1=-1.0, scalar2=None,
            op0=ALU.mult))
        se.op("scalar", lambda h: h.activation(
            out=rstd[:], in_=mv[:, 1:2], func=AF.Ln, bias=self.const(eps)))
        se.op("scalar", lambda h: h.activation(
            out=rstd[:], in_=rstd[:], func=AF.Exp, scale=self.const(-0.5)))

    def emit_transpose(self, in_sb, out_sb, n, pt, ident_b, in_off=0, out_off=0):
        se = self.se
        for j in range(n):
            se.op("tensor", lambda h, j=j: h.transpose(
                out=pt[:, 0:128], in_=in_sb[:, in_off + 128 * j:in_off + 128 * (j + 1)],
                identity=ident_b[:]))
            se.op("vector", lambda h, j=j: h.tensor_copy(
                out=out_sb[:, out_off + 128 * j:out_off + 128 * (j + 1)],
                in_=pt[:, 0:128]))


def make_identity(ctx, idn_i, ident_b):
    se = ctx.se
    se.op("gpsimd", lambda h: h.iota(
        idn_i[:], pattern=[[1, 128]], base=0, channel_multiplier=-1,
        allow_small_or_imprecise_dtypes=True))
    se.op("vector", lambda h: h.tensor_scalar(
        out=ident_b[:], in0=idn_i[:], scalar1=0.0, scalar2=None,
        op0=ALU.is_equal))


def shard_rows(a, n=NCORES):
    a = np.ascontiguousarray(a)
    a2 = a.reshape(B * T, -1)
    r = a2.shape[0] // n
    return [np.ascontiguousarray(a2[c * r:(c + 1) * r]) for c in range(n)]



def build_A():
    nc = bass.Bass("TRN2", target_bir_lowering=False)
    x_in = nc.dram_tensor("x", [RPC, D], F32, kind="ExternalInput").ap()
    wq_in = nc.dram_tensor("Wq", [D, 512], F32, kind="ExternalInput").ap()
    wk_in = nc.dram_tensor("Wk", [3, D, 512], F32, kind="ExternalInput").ap()
    wv_in = nc.dram_tensor("Wv", [3, D, 512], F32, kind="ExternalInput").ap()
    wg_in = nc.dram_tensor("Wg", [D, 4], F32, kind="ExternalInput").ap()
    w1s_in = nc.dram_tensor("w1s", [INNER, D], F32, kind="ExternalInput").ap()
    w2s_in = nc.dram_tensor("w2s", [D, INNER], F32, kind="ExternalInput").ap()

    xn0_out = nc.dram_tensor("xn0", [RPC, D], F32, kind="ExternalOutput").ap()
    q_out = nc.dram_tensor("q0", [RPC, 512], BF16, kind="ExternalOutput").ap()
    k_out = nc.dram_tensor("k0", [3, RPC, 512], BF16, kind="ExternalOutput").ap()
    v_out = nc.dram_tensor("v0", [3, RPC, 512], BF16, kind="ExternalOutput").ap()
    g_out = nc.dram_tensor("g0", [RPC, 4], F32, kind="ExternalOutput").ap()
    ws_out = nc.dram_tensor("mcl", [1, 2], F32, kind="ExternalOutput").ap()

    with ExitStack() as stack:
        ctx = Ctx(nc, stack)
        se = ctx.se
        xt = ctx.sbuf("xt", [128, D], F32)
        wq = ctx.sbuf("wq", [128, DK * 512], BF16)
        wk = [ctx.sbuf(f"wk{i}", [128, DK * 512], BF16) for i in range(3)]
        wv = [ctx.sbuf(f"wv{i}", [128, DK * 512], BF16) for i in range(3)]
        wg = ctx.sbuf("wg", [128, DK * 4], BF16)
        wsl1 = ctx.sbuf("wsl1", [128, D], F32)
        wsl2 = ctx.sbuf("wsl2", [128, INNER], F32)
        stats = ctx.sbuf("stats", [128, 24], F32)
        mv = ctx.sbuf("mv", [128, 2], F32)
        nmean = ctx.sbuf("nmean", [128, 1], F32)
        rstd = ctx.sbuf("rstd", [128, 1], F32)
        xn_f = ctx.sbuf("xn_f", [128, D], F32)
        xn_b = ctx.sbuf("xn_b", [128, D], BF16)
        xn_fm = ctx.sbuf("xn_fm", [128, DK * 128], BF16)
        idn_i = ctx.sbuf("idn_i", [128, 128], F32)
        ident_b = ctx.sbuf("ident_b", [128, 128], BF16)
        ev_b = ctx.sbuf("ev_b", [128, 512], BF16)
        g_ev = ctx.sbuf("g_ev", [128, 4], F32)
        gmax = ctx.sbuf("gmax", [128, 1], F32)
        gsum = ctx.sbuf("gsum", [128, 1], F32)
        col = ctx.sbuf("col", [128, 2], F32)
        wsacc = ctx.sbuf("wsacc", [128, 2], F32)
        wsfin = ctx.sbuf("wsfin", [1, 2], F32)

        pt = ctx.psum("pt", [128, 128], BF16)
        pp = ctx.psum("pp", [128, 512], F32)
        pg = ctx.psum("pg", [128, 4], F32)
        pws = ctx.psum("pws", [1, 2], F32)

        ctx.emit_init()
        make_identity(ctx, idn_i, ident_b)

        se.dma("gpsimd", wq[:].rearrange("p (k f) -> p k f", f=512),
               wq_in.rearrange("(k p) f -> p k f", p=128))
        for i in range(3):
            se.dma("gpsimd", wk[i][:].rearrange("p (k f) -> p k f", f=512),
                   wk_in[i].rearrange("(k p) f -> p k f", p=128))
            se.dma("gpsimd", wv[i][:].rearrange("p (k f) -> p k f", f=512),
                   wv_in[i].rearrange("(k p) f -> p k f", p=128))
        se.dma("gpsimd", wg[:].rearrange("p (k f) -> p k f", f=4),
               wg_in.rearrange("(k p) f -> p k f", p=128))
        se.op("vector", lambda h: h.memset(g_ev[:], 0.0))
        se.op("vector", lambda h: h.memset(wsacc[:], 0.0))

        for it in range(NT):
            se.dma("sync", xt[:], x_in[it * 128:(it + 1) * 128, :])
            ctx.emit_lnstats(xt[:], D, stats, mv, nmean, rstd)
            se.op("vector", lambda h: h.tensor_scalar(
                out=xn_f[:], in0=xt[:], scalar1=nmean[:, 0:1],
                scalar2=rstd[:, 0:1], op0=ALU.add, op1=ALU.mult))
            se.dma("sync", xn0_out[it * 128:(it + 1) * 128, :], xn_f[:])
            se.op("vector", lambda h: h.tensor_copy(out=xn_b[:], in_=xn_f[:]))
            ctx.emit_transpose(xn_b, xn_fm, DK, pt, ident_b)
            mats = [(wq, q_out)] + [(wk[i], (k_out, i)) for i in range(3)] \
                + [(wv[i], (v_out, i)) for i in range(3)]
            for w_sb, dst in mats:
                for k in range(DK):
                    se.op("tensor", lambda h, k=k, w_sb=w_sb: h.matmul(
                        out=pp[:], lhsT=xn_fm[:, 128 * k:128 * (k + 1)],
                        rhs=w_sb[:, 512 * k:512 * (k + 1)],
                        start=(k == 0), stop=(k == DK - 1)))
                se.op("vector", lambda h: h.tensor_copy(out=ev_b[:], in_=pp[:]))
                if isinstance(dst, tuple):
                    d, i = dst
                    se.dma("sync", d[i, it * 128:(it + 1) * 128, :], ev_b[:])
                else:
                    se.dma("sync", dst[it * 128:(it + 1) * 128, :], ev_b[:])
            for k in range(DK):
                se.op("tensor", lambda h, k=k: h.matmul(
                    out=pg[:], lhsT=xn_fm[:, 128 * k:128 * (k + 1)],
                    rhs=wg[:, 4 * k:4 * (k + 1)],
                    start=(k == 0), stop=(k == DK - 1)))
            se.op("vector", lambda h: h.reduce_max(
                out=gmax[:], in_=pg[:, 0:3], axis=AX.X))
            se.op("vector", lambda h: h.tensor_scalar(
                out=gmax[:], in0=gmax[:], scalar1=-1.0, scalar2=None,
                op0=ALU.mult))
            se.op("scalar", lambda h: h.activation(
                out=g_ev[:, 0:3], in_=pg[:, 0:3], func=AF.Exp,
                bias=gmax[:, 0:1], accum_out=gsum[:]))
            se.op("vector", lambda h: h.reciprocal(out=gsum[:], in_=gsum[:]))
            se.op("vector", lambda h: h.tensor_scalar(
                out=g_ev[:, 0:3], in0=g_ev[:, 0:3], scalar1=gsum[:, 0:1],
                scalar2=None, op0=ALU.mult))
            se.dma("sync", g_out[it * 128:(it + 1) * 128, :], g_ev[:])

        red_tok = {}
        dma_tok = {}
        base = se.last
        for j in range(INNER // 128 + 1):
            par = j % 2
            if j < INNER // 128:
                deps = [base] if j < 2 else [red_tok[j - 2]]
                dma_tok[j] = se.dma("sync", wsl2[:, par * D:(par + 1) * D],
                                    w1s_in[j * 128:(j + 1) * 128, :],
                                    deps=deps)
            if j >= 1:
                jj = j - 1
                ppar = jj % 2
                t = se.op("vector", lambda h, ppar=ppar: h.reduce_sum(
                    out=col[:, 0:1], in_=wsl2[:, ppar * D:(ppar + 1) * D],
                    axis=AX.X, apply_absolute_value=True), deps=[dma_tok[jj]])
                red_tok[jj] = se.op("vector", lambda h: h.tensor_tensor(
                    out=wsacc[:, 0:1], in0=wsacc[:, 0:1], in1=col[:, 0:1],
                    op=ALU.add), deps=[t])
        se.op("vector", lambda h: h.engine_nop(),
              deps=[red_tok[INNER // 128 - 1]], close=True)
        for j in range(D // 128):
            d1 = se.dma("sync", wsl2[:, 0:4096],
                        w2s_in[j * 128:(j + 1) * 128, 0:4096],
                        deps=[se.last])
            d2 = se.dma("sync", wsl2[:, 4096:8192],
                        w2s_in[j * 128:(j + 1) * 128, 4096:8192],
                        deps=[d1])
            t = se.op("vector", lambda h: h.reduce_sum(
                out=col[:, 1:2], in_=wsl2[:], axis=AX.X,
                apply_absolute_value=True), deps=[d2])
            se.op("vector", lambda h: h.tensor_tensor(
                out=wsacc[:, 1:2], in0=wsacc[:, 1:2], in1=col[:, 1:2],
                op=ALU.add), deps=[t], close=True)
        se.op("tensor", lambda h: h.matmul(
            out=pws[:], lhsT=ctx.ones_col[:], rhs=wsacc[:],
            start=True, stop=True))
        se.op("vector", lambda h: h.tensor_scalar(
            out=wsfin[:], in0=pws[:], scalar1=1.0 / (INNER * D), scalar2=EPS,
            op0=ALU.mult, op1=ALU.max))
        se.dma("sync", ws_out[:], wsfin[:])

        se.finalize()
    return nc


def run_A(inputs):
    nc = build_A()
    xs = shard_rows(inputs["L0_attn_x"])
    wgp = np.zeros((D, 4), np.float32)
    wgp[:, 0:3] = inputs["L0_attn_Wg"]
    in_maps = []
    for c in range(NCORES):
        in_maps.append({
            "x": xs[c],
            "Wq": np.ascontiguousarray(inputs["L0_attn_Wq"]),
            "Wk": np.ascontiguousarray(inputs["L0_attn_Wk"]),
            "Wv": np.ascontiguousarray(inputs["L0_attn_Wv"]),
            "Wg": wgp,
            "w1s": np.ascontiguousarray(inputs["L0_ffn_w1"]),
            "w2s": np.ascontiguousarray(inputs["L0_ffn_w2"]),
        })
    res = run_bass_kernel_spmd(nc, in_maps, core_ids=list(range(NCORES)))
    r = res.results
    out = {
        "xn0": np.concatenate([r[c]["xn0"] for c in range(NCORES)], 0),
        "q0": np.concatenate([r[c]["q0"] for c in range(NCORES)], 0),
        "k0": np.concatenate([r[c]["k0"] for c in range(NCORES)], 1),
        "v0": np.concatenate([r[c]["v0"] for c in range(NCORES)], 1),
        "g0": np.concatenate([r[c]["g0"] for c in range(NCORES)], 0),
        "mcl": r[0]["mcl"],
    }
    return out



def build_P():
    nc = bass.Bass("TRN2", target_bir_lowering=False)
    w1_in = nc.dram_tensor("w1s", [INNER // 8, D], F32, kind="ExternalInput").ap()
    w2_in = nc.dram_tensor("w2s", [D // 8, INNER], F32, kind="ExternalInput").ap()
    gw_in = nc.dram_tensor("gws", [INNER // 8, D], F32, kind="ExternalInput").ap()
    uw_in = nc.dram_tensor("uws", [INNER // 8, D], F32, kind="ExternalInput").ap()
    dw_in = nc.dram_tensor("dws", [D // 8, INNER], F32, kind="ExternalInput").ap()
    mcl_in = nc.dram_tensor("mcl", [1, 2], F32, kind="ExternalInput").ap()

    w1t_o = nc.dram_tensor("w1t_s", [D, INNER // 8], BF16, kind="ExternalOutput").ap()
    w2t_o = nc.dram_tensor("w2t_s", [INNER, D // 8], BF16, kind="ExternalOutput").ap()
    gq_o = nc.dram_tensor("gq_s", [D, INNER // 8], BF16, kind="ExternalOutput").ap()
    uq_o = nc.dram_tensor("uq_s", [D, INNER // 8], BF16, kind="ExternalOutput").ap()
    dq_o = nc.dram_tensor("dq_s", [INNER, D // 8], BF16, kind="ExternalOutput").ap()

    with ExitStack() as stack:
        ctx = Ctx(nc, stack)
        se = ctx.se
        idn_i = ctx.sbuf("idn_i", [128, 128], F32)
        ident_b = ctx.sbuf("ident_b", [128, 128], BF16)
        wld = ctx.sbuf("wld", [128, INNER], F32)
        m1 = ctx.sbuf("m1", [128, D], F32)
        m2 = ctx.sbuf("m2", [128, D], F32)
        m3 = ctx.sbuf("m3", [128, D], F32)
        m4 = ctx.sbuf("m4", [128, D], F32)
        m5 = ctx.sbuf("m5", [128, D], F32)
        mi = ctx.sbuf("mi", [128, D], I32)
        bmax = ctx.sbuf("bmax", [128, 64], F32)
        sbs = ctx.sbuf("sbs", [128, 64], F32)
        rss = ctx.sbuf("rss", [128, 64], F32)
        qb = ctx.sbuf("qb", [128, INNER], BF16)
        stg = ctx.sbuf("stg", [128, INNER], BF16)
        mclsb = ctx.sbuf("mclsb", [1, 2], F32)
        swcol = ctx.sbuf("swcol", [128, 2], F32)
        pH = ctx.psum("pH", [128, 1024], BF16)
        pP = ctx.psum("pP", [128, 512], F32)

        ctx.emit_init()
        make_identity(ctx, idn_i, ident_b)
        se.dma("sync", mclsb[:], mcl_in[:])
        se.op("tensor", lambda h: h.matmul(
            out=pP[:, 0:2], lhsT=ctx.ones_row[:], rhs=mclsb[:],
            start=True, stop=True))
        se.op("vector", lambda h: h.tensor_copy(out=swcol[:], in_=pP[:, 0:2]))
        se.op("vector", lambda h: h.reciprocal(out=swcol[:], in_=swcol[:]))

        def tern_window(src_ap, dst_ap, sw_col):
            se.op("vector", lambda h: h.tensor_scalar(
                out=m2[:], in0=src_ap, scalar1=sw_col, scalar2=None,
                op0=ALU.mult))
            se.op("vector", lambda h: h.tensor_scalar(
                out=m3[:], in0=m2[:], scalar1=-0.5, scalar2=None,
                op0=ALU.is_le))
            se.op("vector", lambda h: h.tensor_scalar(
                out=m2[:], in0=m2[:], scalar1=0.5, scalar2=None,
                op0=ALU.is_ge))
            se.op("vector", lambda h: h.tensor_tensor(
                out=dst_ap, in0=m2[:], in1=m3[:], op=ALU.subtract))

        for cch in range(8):
            se.dma("sync", wld[:, 0:D], w1_in[cch * 128:(cch + 1) * 128, :])
            tern_window(wld[:, 0:D], qb[:, 0:D], swcol[:, 0:1])
            for k in range(DK):
                se.op("tensor", lambda h, k=k: h.transpose(
                    out=pH[:, 0:128], in_=qb[:, 128 * k:128 * (k + 1)],
                    identity=ident_b[:]))
                se.op("vector", lambda h, k=k: h.tensor_copy(
                    out=stg[:, k * 128:(k + 1) * 128], in_=pH[:, 0:128]))
            se.dma("sync",
                   w1t_o[:, cch * 128:(cch + 1) * 128].rearrange(
                       "(k p) c -> p k c", p=128),
                   stg[:, 0:D].rearrange("p (k c) -> p k c", c=128))
        for cch in range(2):
            se.dma("sync", wld[:], w2_in[cch * 128:(cch + 1) * 128, :])
            for w in range(4):
                sl = slice(w * D, (w + 1) * D)
                tern_window(wld[:, sl], qb[:, sl], swcol[:, 1:2])
            for k in range(INNER // 128):
                se.op("tensor", lambda h, k=k: h.transpose(
                    out=pH[:, 0:128], in_=qb[:, 128 * k:128 * (k + 1)],
                    identity=ident_b[:]))
                se.op("vector", lambda h, k=k: h.tensor_copy(
                    out=stg[:, k * 128:(k + 1) * 128], in_=pH[:, 0:128]))
            se.dma("sync",
                   w2t_o[:, cch * 128:(cch + 1) * 128].rearrange(
                       "(k p) c -> p k c", p=128),
                   stg[:].rearrange("p (k c) -> p k c", c=128))
        for w_in, w_o in ((gw_in, gq_o), (uw_in, uq_o)):
            for cch in range(8):
                se.dma("sync", m4[:], w_in[cch * 128:(cch + 1) * 128, :])
                emit_mx_quant(ctx, m4[:], qb[:, 0:D], D, m1[:], m2[:], m3[:],
                              m4[:], m5[:], mi[:], bmax[:], sbs[:], rss[:])
                for k in range(DK):
                    se.op("tensor", lambda h, k=k: h.transpose(
                        out=pH[:, 0:128], in_=qb[:, 128 * k:128 * (k + 1)],
                        identity=ident_b[:]))
                    se.op("vector", lambda h, k=k: h.tensor_copy(
                        out=stg[:, k * 128:(k + 1) * 128], in_=pH[:, 0:128]))
                se.dma("sync",
                       w_o[:, cch * 128:(cch + 1) * 128].rearrange(
                           "(k p) c -> p k c", p=128),
                       stg[:, 0:D].rearrange("p (k c) -> p k c", c=128))
        for cch in range(2):
            se.dma("sync", wld[:], dw_in[cch * 128:(cch + 1) * 128, :])
            for w in range(4):
                sl = slice(w * D, (w + 1) * D)
                se.op("vector", lambda h, sl=sl: h.tensor_copy(
                    out=m4[:], in_=wld[:, sl]))
                emit_mx_quant(ctx, m4[:], qb[:, sl], D, m1[:], m2[:], m3[:],
                              m4[:], m5[:], mi[:], bmax[:], sbs[:], rss[:])
            for k in range(INNER // 128):
                se.op("tensor", lambda h, k=k: h.transpose(
                    out=pH[:, 0:128], in_=qb[:, 128 * k:128 * (k + 1)],
                    identity=ident_b[:]))
                se.op("vector", lambda h, k=k: h.tensor_copy(
                    out=stg[:, k * 128:(k + 1) * 128], in_=pH[:, 0:128]))
            se.dma("sync",
                   dq_o[:, cch * 128:(cch + 1) * 128].rearrange(
                       "(k p) c -> p k c", p=128),
                   stg[:].rearrange("p (k c) -> p k c", c=128))
        se.finalize()
    return nc


def run_P(inputs, aux):
    nc = build_P()
    in_maps = []
    for c in range(NCORES):
        i8, d8 = INNER // 8, D // 8
        in_maps.append({
            "w1s": np.ascontiguousarray(inputs["L0_ffn_w1"][c * i8:(c + 1) * i8]),
            "w2s": np.ascontiguousarray(inputs["L0_ffn_w2"][c * d8:(c + 1) * d8]),
            "gws": np.ascontiguousarray(inputs["L1_ffn_gate_w"][c * i8:(c + 1) * i8]),
            "uws": np.ascontiguousarray(inputs["L1_ffn_up_w"][c * i8:(c + 1) * i8]),
            "dws": np.ascontiguousarray(inputs["L1_ffn_down_w"][c * d8:(c + 1) * d8]),
            "mcl": np.asarray(aux["mcl"]).reshape(1, 2),
        })
    res = run_bass_kernel_spmd(nc, in_maps, core_ids=list(range(NCORES)))
    r = res.results
    return {
        "w1t": np.concatenate([r[c]["w1t_s"] for c in range(NCORES)], 1),
        "w2t": np.concatenate([r[c]["w2t_s"] for c in range(NCORES)], 1),
        "gq": np.concatenate([r[c]["gq_s"] for c in range(NCORES)], 1),
        "uq": np.concatenate([r[c]["uq_s"] for c in range(NCORES)], 1),
        "dq": np.concatenate([r[c]["dq_s"] for c in range(NCORES)], 1),
    }



def build_B():
    nc = bass.Bass("TRN2", target_bir_lowering=False)
    q_in = nc.dram_tensor("q", [RPC, 512], BF16, kind="ExternalInput").ap()
    k_in = nc.dram_tensor("k", [3, T, 512], BF16, kind="ExternalInput").ap()
    v_in = nc.dram_tensor("v", [3, T, 512], BF16, kind="ExternalInput").ap()
    g_in = nc.dram_tensor("g", [RPC, 4], F32, kind="ExternalInput").ap()
    dl_in = nc.dram_tensor("dl", [1, 4], F32, kind="ExternalInput").ap()
    osc_in = nc.dram_tensor("osc", [1, 1], F32, kind="ExternalInput").ap()
    wo_in = nc.dram_tensor("Wo", [512, D], F32, kind="ExternalInput").ap()
    half_in = nc.dram_tensor("half", [1, 1], F32, kind="ExternalInput").ap()

    a_out = nc.dram_tensor("attn0", [RPC, D], F32, kind="ExternalOutput").ap()
    if B_DEBUG:
        bl_dbg = nc.dram_tensor("bl_dbg", [NT, 128, 512], F32,
                                kind="ExternalOutput").ap()
        mk_dbg = nc.dram_tensor("mk_dbg", [3, 4, 128, 512], F32,
                                kind="ExternalOutput").ap()
        sc_dbg = nc.dram_tensor("sc_dbg", [3, 4, 128, 512], F32,
                                kind="ExternalOutput").ap()
        wt_dbg = nc.dram_tensor("wt_dbg", [3, 4, 128, 512], F32,
                                kind="ExternalOutput").ap()

    with ExitStack() as stack:
        ctx = Ctx(nc, stack)
        se = ctx.se
        idn_i = ctx.sbuf("idn_i", [128, 128], F32)
        ident_b = ctx.sbuf("ident_b", [128, 128], BF16)
        q_fm = ctx.sbuf("q_fm", [128, 4 * RPC], BF16)
        k_fm = [ctx.sbuf(f"k_fm{i}", [128, 4 * T], BF16) for i in range(3)]
        v_tm = [ctx.sbuf(f"v_tm{i}", [128, 16 * 512], BF16) for i in range(3)]
        gt = ctx.sbuf("gt", [128, NT * 4], F32)
        ld_tm = ctx.sbuf("ld_tm", [128, 512], BF16)
        t1 = ctx.sbuf("t1", [128, 512], F32)
        wtile = ctx.sbuf("wtile", [128, 512], F32)
        getile = ctx.sbuf("getile", [128, 512], F32)
        mk_b = ctx.sbuf("mk_b", [128, 512], BF16)
        mk_f32 = ctx.sbuf("mk_f32", [128, 512], F32)
        mkT = ctx.sbuf("mkT", [128, 512], BF16)
        blend = ctx.sbuf("blend", [128, 512], F32)
        blend_b = ctx.sbuf("blend_b", [128, 512], BF16)
        blendT = ctx.sbuf("blendT", [128, 512], BF16)
        att_f = ctx.sbuf("att_f", [128, 512], F32)
        wo = ctx.sbuf("wo", [128, 4 * D], BF16)
        sig = ctx.sbuf("sig", [1, 4], F32)
        lnd_row = ctx.sbuf("lnd_row", [1, 4], F32)
        lnd_col = ctx.sbuf("lnd_col", [128, 4], F32)
        bias_col = ctx.sbuf("bias_col", [128, 1], F32)
        osc_sb = ctx.sbuf("osc_sb", [1, 1], F32)
        osc_col = ctx.sbuf("osc_col", [128, 1], F32)

        pt = ctx.psum("pt", [128, 128], BF16)
        psc = ctx.psum("psc", [128, 512], F32)
        pav = ctx.psum("pav", [128, 512], F32)
        po = ctx.psum("po", [128, 512], F32)
        pb = ctx.psum("pb", [128, 4], F32)

        ctx.emit_init()
        make_identity(ctx, idn_i, ident_b)

        HALF = None
        ctx.emit_init2 = None

        se.dma("sync", osc_sb[:], osc_in[:])
        se.dma("sync", sig[:], dl_in[:])
        se.op("scalar", lambda h: h.activation(out=sig[:], in_=sig[:],
                                               func=AF.Sigmoid))
        se.op("scalar", lambda h: h.activation(out=lnd_row[:], in_=sig[:],
                                               func=AF.Ln))
        se.op("tensor", lambda h: h.matmul(out=pb[:], lhsT=ctx.ones_row[:],
                                           rhs=lnd_row[:], start=True, stop=True))
        se.op("vector", lambda h: h.tensor_copy(out=lnd_col[:], in_=pb[:]))
        se.op("tensor", lambda h: h.matmul(out=pb[:, 0:1], lhsT=ctx.ones_row[:],
                                           rhs=osc_sb[:], start=True, stop=True))
        se.op("vector", lambda h: h.tensor_copy(out=osc_col[:], in_=pb[:, 0:1]))

        se.dma("gpsimd", wo[:].rearrange("p (k j) -> p k j", j=D),
               wo_in.rearrange("(k p) j -> p k j", p=128))
        for it in range(NT):
            se.dma("sync", gt[:, 4 * it:4 * (it + 1)],
                   g_in[128 * it:128 * (it + 1), :])
        for it in range(NT):
            se.dma("sync", ld_tm[:], q_in[128 * it:128 * (it + 1), :])
            for j in range(4):
                se.op("tensor", lambda h, j=j: h.transpose(
                    out=pt[:], in_=ld_tm[:, 128 * j:128 * (j + 1)],
                    identity=ident_b[:]))
                se.op("vector", lambda h, j=j, it=it: h.tensor_copy(
                    out=q_fm[:, j * RPC + it * 128: j * RPC + (it + 1) * 128],
                    in_=pt[:]))
        for i in range(3):
            for st_ in range(16):
                se.dma("sync", v_tm[i][:, 512 * st_:512 * (st_ + 1)],
                       k_in[i, 128 * st_:128 * (st_ + 1), :])
                for j in range(4):
                    se.op("tensor", lambda h, j=j, i=i, st_=st_: h.transpose(
                        out=pt[:], in_=v_tm[i][:, 512 * st_ + 128 * j:
                                               512 * st_ + 128 * (j + 1)],
                        identity=ident_b[:]))
                    se.op("vector", lambda h, j=j, i=i, st_=st_: h.tensor_copy(
                        out=k_fm[i][:, j * T + st_ * 128:j * T + (st_ + 1) * 128],
                        in_=pt[:]))
            for st_ in range(16):
                se.dma("sync", v_tm[i][:, 512 * st_:512 * (st_ + 1)],
                       v_in[i, 128 * st_:128 * (st_ + 1), :])

        se.op("gpsimd", lambda h: h.iota(
            t1[:], pattern=[[1, 512]], base=-1, channel_multiplier=-1,
            allow_small_or_imprecise_dtypes=True))

        for qt in range(NT):
            t0 = qt * 128
            ch_lo = t0 // 512
            se.op("vector", lambda h: h.memset(blend[:], 0.0))
            for i in range(3):
                first_av = True
                for ch in range(ch_lo, 4):
                    s0 = ch * 512
                    for j in range(4):
                        se.op("tensor", lambda h, j=j, i=i, qt=qt, s0=s0: h.matmul(
                            out=psc[:],
                            lhsT=q_fm[:, j * RPC + qt * 128:j * RPC + (qt + 1) * 128],
                            rhs=k_fm[i][:, j * T + s0:j * T + s0 + 512],
                            start=(j == 0), stop=(j == 3)))
                    d0 = float(s0 - t0)
                    se.op("vector", lambda h, i=i, d0=d0: h.tensor_scalar(
                        out=bias_col[:], in0=lnd_col[:, i:i + 1], scalar1=d0,
                        scalar2=None, op0=ALU.mult))
                    se.op("vector", lambda h, i=i: h.tensor_scalar(
                        out=wtile[:], in0=t1[:], scalar1=lnd_col[:, i:i + 1],
                        scalar2=bias_col[:, 0:1], op0=ALU.mult, op1=ALU.add))
                    se.op("vector", lambda h: h.tensor_scalar_min(
                        out=wtile[:], in0=wtile[:], scalar1=0.0))
                    se.op("scalar", lambda h: h.activation(
                        out=wtile[:], in_=wtile[:], func=AF.Exp))
                    if s0 <= t0:
                        se.op("vector", lambda h, d0=d0: h.tensor_scalar(
                            out=getile[:], in0=t1[:], scalar1=d0, scalar2=0.0,
                            op0=ALU.add, op1=ALU.is_ge))
                        se.op("vector", lambda h: h.tensor_tensor(
                            out=wtile[:], in0=wtile[:], in1=getile[:],
                            op=ALU.mult))
                    se.op("vector", lambda h: h.tensor_tensor(
                        out=mk_b[:], in0=psc[:], in1=wtile[:], op=ALU.mult))
                    if B_DEBUG and qt == B_DEBUG_QT:
                        se.op("vector", lambda h: h.tensor_copy(
                            out=mk_f32[:], in_=mk_b[:]))
                        se.dma("sync", mk_dbg[i, ch], mk_f32[:])
                        se.op("vector", lambda h: h.tensor_copy(
                            out=mk_f32[:], in_=psc[:]))
                        se.dma("sync", sc_dbg[i, ch], mk_f32[:])
                        se.dma("sync", wt_dbg[i, ch], wtile[:])
                    for j in range(4):
                        se.op("tensor", lambda h, j=j: h.transpose(
                            out=pt[:], in_=mk_b[:, 128 * j:128 * (j + 1)],
                            identity=ident_b[:]))
                        se.op("vector", lambda h, j=j: h.tensor_copy(
                            out=mkT[:, 128 * j:128 * (j + 1)], in_=pt[:]))
                    for j in range(4):
                        stile = ch * 4 + j
                        se.op("tensor", lambda h, j=j, i=i, stile=stile: h.matmul(
                            out=pav[:], lhsT=mkT[:, 128 * j:128 * (j + 1)],
                            rhs=v_tm[i][:, 512 * stile:512 * (stile + 1)],
                            start=(j == 0), stop=(j == 3)))
                    first_av = False
                    se.op("vector", lambda h, i=i, qt=qt: h.scalar_tensor_tensor(
                        out=blend[:], in0=pav[:],
                        scalar=gt[:, 4 * qt + i:4 * qt + i + 1],
                        in1=blend[:], op0=ALU.mult, op1=ALU.add))
            if B_DEBUG:
                se.dma("sync", bl_dbg[qt], blend[:])
            se.op("vector", lambda h: h.tensor_copy(out=blend_b[:], in_=blend[:]))
            for j in range(4):
                se.op("tensor", lambda h, j=j: h.transpose(
                    out=pt[:], in_=blend_b[:, 128 * j:128 * (j + 1)],
                    identity=ident_b[:]))
                se.op("vector", lambda h, j=j: h.tensor_copy(
                    out=blendT[:, 128 * j:128 * (j + 1)], in_=pt[:]))
            for jc in range(4):
                for kk in range(4):
                    se.op("tensor", lambda h, jc=jc, kk=kk: h.matmul(
                        out=po[:], lhsT=blendT[:, 128 * kk:128 * (kk + 1)],
                        rhs=wo[:, kk * D + jc * 512:kk * D + jc * 512 + 512],
                        start=(kk == 0), stop=(kk == 3)))
                se.op("vector", lambda h: h.tensor_scalar(
                    out=att_f[:], in0=po[:], scalar1=osc_col[:, 0:1],
                    scalar2=None, op0=ALU.mult))
                se.dma("sync", a_out[128 * qt:128 * (qt + 1),
                                     512 * jc:512 * (jc + 1)], att_f[:])

        se.finalize()
    return nc


def run_B(inputs, aux):
    nc = build_B()
    qs = shard_rows(np.asarray(aux["q0"]))
    gs = shard_rows(np.asarray(aux["g0"]))
    k0 = np.asarray(aux["k0"]).reshape(3, B, T, 512)
    v0 = np.asarray(aux["v0"]).reshape(3, B, T, 512)
    dl = np.zeros((1, 4), np.float32)
    dl[0, 0:3] = inputs["L0_attn_decay_logits"]
    osc = np.asarray(inputs["L0_attn_out_scale"], np.float32).reshape(1, 1)
    in_maps = []
    for c in range(NCORES):
        b, half = c // 2, c % 2
        kk = k0[:, b]
        vv = v0[:, b]
        if half == 1:
            kk = np.concatenate([kk[:, 1024:2048], np.zeros_like(kk[:, 0:1024])], 1)
            vv = np.concatenate([vv[:, 1024:2048], np.zeros_like(vv[:, 0:1024])], 1)
        in_maps.append({
            "q": qs[c], "g": gs[c],
            "k": np.ascontiguousarray(kk), "v": np.ascontiguousarray(vv),
            "dl": dl, "osc": osc, "half": np.zeros((1, 1), np.float32),
            "Wo": np.ascontiguousarray(inputs["L0_attn_Wo"]),
        })
    res = run_bass_kernel_spmd(nc, in_maps, core_ids=list(range(NCORES)))
    return np.concatenate([res.results[c]["attn0"] for c in range(NCORES)], 0)



def build_C():
    nc = bass.Bass("TRN2", target_bir_lowering=False)
    xn0_in = nc.dram_tensor("xn0", [RPC, D], F32, kind="ExternalInput").ap()
    x_in = nc.dram_tensor("x", [RPC, D], F32, kind="ExternalInput").ap()
    at_in = nc.dram_tensor("attn0", [RPC, D], F32, kind="ExternalInput").ap()
    w1t_in = nc.dram_tensor("w1t", [D, INNER], BF16, kind="ExternalInput").ap()
    w2t_in = nc.dram_tensor("w2t", [INNER, D], BF16, kind="ExternalInput").ap()
    b1_in = nc.dram_tensor("b1", [1, INNER], F32, kind="ExternalInput").ap()
    lng_in = nc.dram_tensor("lng", [1, INNER], F32, kind="ExternalInput").ap()
    lnb_in = nc.dram_tensor("lnb", [1, INNER], F32, kind="ExternalInput").ap()
    b2_in = nc.dram_tensor("b2", [1, D], F32, kind="ExternalInput").ap()
    ws_in = nc.dram_tensor("mcl2", [1, 2], F32, kind="ExternalInput").ap()

    x1_out = nc.dram_tensor("x1", [RPC, D], F32, kind="ExternalOutput").ap()
    xn1_out = nc.dram_tensor("xn1", [RPC, D], F32, kind="ExternalOutput").ap()
    if C_DEBUG or C_DEBUG_LITE:
        f0dbg_out = nc.dram_tensor("f0dbg", [RPC, D], F32, kind="ExternalOutput").ap()
    if C_DEBUG:
        dbg_out = nc.dram_tensor("dbg", [128, 64], F32, kind="ExternalOutput").ap()
        hdbg_out = nc.dram_tensor("hdbg", [RPC, INNER], F32, kind="ExternalOutput").ap()
        hqdbg_out = nc.dram_tensor("hqdbg", [INNER, RPC], BF16, kind="ExternalOutput").ap()

    w1t_d = w1t_in
    w2t_d = w2t_in
    h_d = nc.dram_tensor("h_d", [RPC, INNER], F32, kind="Internal").ap()
    hq_d = nc.dram_tensor("hq_d", [INNER, RPC], BF16, kind="Internal").ap()

    with ExitStack() as stack:
        ctx = Ctx(nc, stack)
        se = ctx.se
        idn_i = ctx.sbuf("idn_i", [128, 128], F32)
        ident_b = ctx.sbuf("ident_b", [128, 128], BF16)
        xq_fm = ctx.sbuf("xq_fm", [128, DK * RPC], BF16)
        big_f1 = ctx.sbuf("big_f1", [128, INNER], F32)
        big_f2 = ctx.sbuf("big_f2", [128, D], F32)
        big_f3 = ctx.sbuf("big_f3", [128, D], F32)
        big_i = ctx.sbuf("big_i", [128, D], I32)
        big_b = ctx.sbuf("big_b", [128, INNER], BF16)
        hqT = ctx.sbuf("hqT", [128, D], BF16)
        gw = ctx.sbuf("gw", [128, D], F32)
        bw = ctx.sbuf("bw", [128, D], F32)
        b2r = ctx.sbuf("b2r", [128, D], F32)
        b1w = ctx.sbuf("b1w", [128, 512], F32)
        rowb1 = ctx.sbuf("rowb1", [1, 512], F32)
        rowbuf2 = ctx.sbuf("rowbuf2", [1, D], F32)
        b2row = ctx.sbuf("b2row", [1, D], F32)
        ws16 = ctx.sbuf("ws16", [1, 16], F32)
        wsred = ctx.sbuf("wsred", [1, 2], F32)
        swcol = ctx.sbuf("swcol", [128, 2], F32)
        mclcol = ctx.sbuf("mclcol", [128, 2], F32)
        cols = ctx.sbuf("cols", [128, 16], F32)
        runsum = ctx.sbuf("runsum", [128, NT], F32)
        runsq = ctx.sbuf("runsq", [128, NT], F32)
        amax2 = ctx.sbuf("amax2", [128, NT], F32)
        asq = ctx.sbuf("asq", [128, NT], F32)
        comb2 = ctx.sbuf("comb2", [128, NT], F32)
        mucol = ctx.sbuf("mucol", [128, NT], F32)
        rstd2 = ctx.sbuf("rstd2", [128, NT], F32)
        s2c = ctx.sbuf("s2c", [128, NT], F32)
        inv2c = ctx.sbuf("inv2c", [128, NT], F32)
        invs = ctx.sbuf("invs", [128, NT], F32)
        c1 = ctx.sbuf("c1", [128, 1], F32)
        c2 = ctx.sbuf("c2", [128, 1], F32)
        c3 = ctx.sbuf("c3", [128, 1], F32)

        pA = ctx.psum("pA", [128, 512], F32)
        pB = ctx.psum("pB", [128, 512], F32)
        pC = ctx.psum("pC", [128, 512], F32)
        pD = ctx.psum("pD", [128, 512], F32)
        pH = ctx.psum("pH", [128, 1024], BF16)
        pH2 = ctx.psum("pH2", [128, 1024], BF16)
        pds = [pA, pB, pC, pD]

        ctx.emit_init()
        make_identity(ctx, idn_i, ident_b)

        se.dma("sync", wsred[:], ws_in[:])
        se.op("tensor", lambda h: h.matmul(
            out=pA[:, 0:2], lhsT=ctx.ones_row[:], rhs=wsred[:],
            start=True, stop=True))
        se.op("vector", lambda h: h.tensor_copy(out=mclcol[:], in_=pA[:, 0:2]))
        se.dma("sync", b2row[:], b2_in[:])
        for j in range(4):
            se.op("tensor", lambda h, j=j: h.matmul(
                out=pB[:], lhsT=ctx.ones_row[:],
                rhs=b2row[:, 512 * j:512 * (j + 1)], start=True, stop=True))
            se.op("vector", lambda h, j=j: h.tensor_copy(
                out=b2r[:, 512 * j:512 * (j + 1)], in_=pB[:]))
        se.op("vector", lambda h: h.memset(runsum[:], 0.0))
        se.op("vector", lambda h: h.memset(runsq[:], 0.0))
        se.op("vector", lambda h: h.memset(amax2[:], 0.0))
        se.op("vector", lambda h: h.memset(asq[:], 0.0))

        for it in range(NT):
            xt = big_f1[:, 0:D]
            se.dma("sync", xt, xn0_in[it * 128:(it + 1) * 128, :])
            se.op("scalar", lambda h, xt=xt: h.activation(
                out=big_f2[:, 0:D], in_=xt, func=AF.Square, accum_out=c1[:]))
            se.op("vector", lambda h: h.tensor_scalar(
                out=c1[:], in0=c1[:], scalar1=1.0 / D, scalar2=EPS,
                op0=ALU.mult, op1=ALU.add))
            se.op("scalar", lambda h: h.activation(
                out=c1[:], in_=c1[:], func=AF.Ln))
            se.op("scalar", lambda h: h.activation(
                out=c1[:], in_=c1[:], func=AF.Exp, scale=ctx.const(-0.5)))
            se.op("vector", lambda h, xt=xt: h.reduce_max(
                out=c2[:], in_=xt, axis=AX.X, apply_absolute_value=True))
            se.op("vector", lambda h: h.tensor_tensor(
                out=c2[:], in0=c2[:], in1=c1[:], op=ALU.mult))
            se.op("vector", lambda h: h.tensor_scalar_max(
                out=c2[:], in0=c2[:], scalar1=EPS))
            se.op("vector", lambda h, it=it: h.tensor_scalar(
                out=invs[:, it:it + 1], in0=c2[:], scalar1=1.0 / 127.0,
                scalar2=None, op0=ALU.mult))
            se.op("vector", lambda h: h.reciprocal(out=c2[:], in_=c2[:]))
            se.op("vector", lambda h: h.tensor_scalar(
                out=c2[:], in0=c2[:], scalar1=127.0, scalar2=None,
                op0=ALU.mult))
            se.op("vector", lambda h: h.tensor_tensor(
                out=c3[:], in0=c1[:], in1=c2[:], op=ALU.mult))
            se.op("vector", lambda h, xt=xt: h.tensor_scalar(
                out=big_f2[:, 0:D], in0=xt, scalar1=c3[:, 0:1], scalar2=None,
                op0=ALU.mult))
            ctx.emit_round(big_f2[:, 0:D], big_b[:, 0:D], big_i[:],
                           big_f3[:, 0:D], big_f1[:, D:2 * D])
            for k in range(DK):
                se.op("tensor", lambda h, k=k: h.transpose(
                    out=pH[:, 0:128], in_=big_b[:, 128 * k:128 * (k + 1)],
                    identity=ident_b[:]))
                se.op("vector", lambda h, k=k, it=it: h.tensor_copy(
                    out=xq_fm[:, k * RPC + it * 128:k * RPC + (it + 1) * 128],
                    in_=pH[:, 0:128]))

        free_tok = None
        sq_tok = None
        dma_tok = None
        mm_last = None
        for wg in range(16):
            rdeps = [se.last] if wg == 0 else [mm_last]
            for k in range(DK):
                rhs_tok = se.dma("sync", big_b[:, k * 512:(k + 1) * 512],
                                 w1t_d[k * 128:(k + 1) * 128,
                                       wg * 512:(wg + 1) * 512], deps=rdeps)
            bt = se.dma("sync", rowb1[:], b1_in[0:1, wg * 512:(wg + 1) * 512],
                        deps=rdeps)
            bt = se.op("tensor", lambda h: h.matmul(
                out=pC[:], lhsT=ctx.ones_row[:], rhs=rowb1[:],
                start=True, stop=True), deps=[bt, free_tok] if free_tok
                else [bt])
            b1_tok = se.op("vector", lambda h: h.tensor_copy(
                out=b1w[:], in_=pC[:]), deps=[bt])
            for it in range(NT):
                pm = pds[it % 2]
                mdeps = [rhs_tok, b1_tok] + ([free_tok] if free_tok else [])
                for k in range(DK):
                    mm_last = se.op("tensor", lambda h, k=k, it=it, pm=pm: h.matmul(
                        out=pm[:], lhsT=xq_fm[:, k * RPC + it * 128:
                                              k * RPC + (it + 1) * 128],
                        rhs=big_b[:, k * 512:(k + 1) * 512],
                        start=(k == 0), stop=(k == DK - 1)), deps=mdeps)
                hb = big_f1[:, 0:512]
                cdeps = [mm_last] + ([sq_tok] if sq_tok else []) + \
                    ([dma_tok] if dma_tok else [])
                free_tok = t = se.op("vector", lambda h, pm=pm, it=it, hb=hb:
                                     h.tensor_scalar(
                    out=hb, in0=pm[:], scalar1=invs[:, it:it + 1],
                    scalar2=mclcol[:, 0:1], op0=ALU.mult, op1=ALU.mult),
                    deps=cdeps)
                t = se.op("vector", lambda h, hb=hb: h.tensor_tensor(
                    out=hb, in0=hb, in1=b1w[:], op=ALU.add), deps=[t])
                t = se.op("scalar", lambda h, hb=hb: h.activation(
                    out=big_f3[:, 0:512], in_=hb, func=AF.Tanh,
                    scale=ctx.const(0.5)), deps=[t])
                t = se.op("vector", lambda h, hb=hb: h.scalar_tensor_tensor(
                    out=hb, in0=big_f3[:, 0:512], scalar=1.0, in1=hb,
                    op0=ALU.add, op1=ALU.mult), deps=[t])
                t = se.op("vector", lambda h, hb=hb: h.tensor_scalar(
                    out=hb, in0=hb, scalar1=0.5, scalar2=None, op0=ALU.mult),
                    deps=[t])
                t = se.op("scalar", lambda h, hb=hb: h.activation(
                    out=hb, in_=hb, func=AF.Gelu, accum_out=c1[:]), deps=[t])
                t = se.op("vector", lambda h, it=it: h.tensor_tensor(
                    out=runsum[:, it:it + 1], in0=runsum[:, it:it + 1],
                    in1=c1[:], op=ALU.add), deps=[t])
                sq_tok = se.op("scalar", lambda h, hb=hb: h.activation(
                    out=big_f3[:, 0:512], in_=hb, func=AF.Square,
                    accum_out=c2[:]), deps=[t])
                t = se.op("vector", lambda h, it=it: h.tensor_tensor(
                    out=runsq[:, it:it + 1], in0=runsq[:, it:it + 1],
                    in1=c2[:], op=ALU.add), deps=[sq_tok])
                dma_tok = se.dma("sync", h_d[it * 128:(it + 1) * 128,
                                             wg * 512:(wg + 1) * 512], hb,
                                 deps=[sq_tok])
        se.op("vector", lambda h: h.engine_nop(), deps=se.fence(), close=True)

        for it in range(NT):
            se.op("vector", lambda h, it=it: h.tensor_scalar(
                out=mucol[:, it:it + 1], in0=runsum[:, it:it + 1],
                scalar1=1.0 / INNER, scalar2=None, op0=ALU.mult))
            se.op("scalar", lambda h, it=it: h.activation(
                out=c1[:], in_=mucol[:, it:it + 1], func=AF.Square))
            se.op("vector", lambda h, it=it: h.tensor_scalar(
                out=c2[:], in0=runsq[:, it:it + 1], scalar1=1.0 / INNER,
                scalar2=None, op0=ALU.mult))
            se.op("vector", lambda h: h.tensor_tensor(
                out=c2[:], in0=c2[:], in1=c1[:], op=ALU.subtract))
            se.op("scalar", lambda h: h.activation(
                out=c2[:], in_=c2[:], func=AF.Ln, bias=ctx.const(EPS)))
            se.op("scalar", lambda h, it=it: h.activation(
                out=rstd2[:, it:it + 1], in_=c2[:], func=AF.Exp,
                scale=ctx.const(-0.5)))

        for win in range(4):
            se.dma("sync", rowbuf2[:], lng_in[0:1, win * D:(win + 1) * D])
            for j in range(4):
                se.op("tensor", lambda h, j=j: h.matmul(
                    out=pC[:], lhsT=ctx.ones_row[:],
                    rhs=rowbuf2[:, 512 * j:512 * (j + 1)], start=True, stop=True))
                se.op("vector", lambda h, j=j: h.tensor_copy(
                    out=gw[:, 512 * j:512 * (j + 1)], in_=pC[:]))
            se.dma("sync", rowbuf2[:], lnb_in[0:1, win * D:(win + 1) * D])
            for j in range(4):
                se.op("tensor", lambda h, j=j: h.matmul(
                    out=pC[:], lhsT=ctx.ones_row[:],
                    rhs=rowbuf2[:, 512 * j:512 * (j + 1)], start=True, stop=True))
                se.op("vector", lambda h, j=j: h.tensor_copy(
                    out=bw[:, 512 * j:512 * (j + 1)], in_=pC[:]))
            for it in range(NT):
                se.dma("sync", big_f1[:, 0:D],
                       h_d[it * 128:(it + 1) * 128, win * D:(win + 1) * D])
                se.op("vector", lambda h, it=it: h.tensor_scalar(
                    out=big_f2[:, 0:D], in0=big_f1[:, 0:D],
                    scalar1=mucol[:, it:it + 1], scalar2=rstd2[:, it:it + 1],
                    op0=ALU.subtract, op1=ALU.mult))
                se.op("vector", lambda h: h.tensor_tensor(
                    out=big_f2[:, 0:D], in0=big_f2[:, 0:D], in1=gw[:, 0:D],
                    op=ALU.mult))
                se.op("vector", lambda h: h.tensor_tensor(
                    out=big_f2[:, 0:D], in0=big_f2[:, 0:D], in1=bw[:, 0:D],
                    op=ALU.add))
                se.op("vector", lambda h: h.reduce_max(
                    out=c1[:], in_=big_f2[:, 0:D], axis=AX.X,
                    apply_absolute_value=True))
                se.op("vector", lambda h, it=it: h.tensor_tensor(
                    out=amax2[:, it:it + 1], in0=amax2[:, it:it + 1],
                    in1=c1[:], op=ALU.max))
                se.op("scalar", lambda h: h.activation(
                    out=big_f3[:, 0:D], in_=big_f2[:, 0:D], func=AF.Square,
                    accum_out=c2[:]))
                se.op("vector", lambda h, it=it: h.tensor_tensor(
                    out=asq[:, it:it + 1], in0=asq[:, it:it + 1],
                    in1=c2[:], op=ALU.add))

        for it in range(NT):
            se.op("vector", lambda h, it=it: h.tensor_scalar(
                out=c3[:], in0=asq[:, it:it + 1], scalar1=1.0 / INNER,
                scalar2=EPS, op0=ALU.mult, op1=ALU.add))
            se.op("scalar", lambda h: h.activation(out=c3[:], in_=c3[:],
                                                   func=AF.Ln))
            se.op("scalar", lambda h: h.activation(
                out=c3[:], in_=c3[:], func=AF.Exp, scale=ctx.const(-0.5)))
            se.op("vector", lambda h, it=it: h.tensor_tensor(
                out=c1[:], in0=amax2[:, it:it + 1], in1=c3[:], op=ALU.mult))
            se.op("vector", lambda h: h.tensor_scalar_max(
                out=c1[:], in0=c1[:], scalar1=EPS))
            se.op("vector", lambda h, it=it: h.tensor_scalar(
                out=inv2c[:, it:it + 1], in0=c1[:], scalar1=1.0 / 127.0,
                scalar2=None, op0=ALU.mult))
            se.op("vector", lambda h: h.reciprocal(out=c1[:], in_=c1[:]))
            se.op("vector", lambda h, it=it: h.tensor_scalar(
                out=s2c[:, it:it + 1], in0=c1[:], scalar1=127.0,
                scalar2=None, op0=ALU.mult))
            se.op("vector", lambda h, it=it: h.tensor_tensor(
                out=comb2[:, it:it + 1], in0=s2c[:, it:it + 1], in1=c3[:],
                op=ALU.mult))

        for win in range(4):
            se.dma("sync", rowbuf2[:], lng_in[0:1, win * D:(win + 1) * D])
            for j in range(4):
                se.op("tensor", lambda h, j=j: h.matmul(
                    out=pC[:], lhsT=ctx.ones_row[:],
                    rhs=rowbuf2[:, 512 * j:512 * (j + 1)], start=True, stop=True))
                se.op("vector", lambda h, j=j: h.tensor_copy(
                    out=gw[:, 512 * j:512 * (j + 1)], in_=pC[:]))
            se.dma("sync", rowbuf2[:], lnb_in[0:1, win * D:(win + 1) * D])
            for j in range(4):
                se.op("tensor", lambda h, j=j: h.matmul(
                    out=pC[:], lhsT=ctx.ones_row[:],
                    rhs=rowbuf2[:, 512 * j:512 * (j + 1)], start=True, stop=True))
                se.op("vector", lambda h, j=j: h.tensor_copy(
                    out=bw[:, 512 * j:512 * (j + 1)], in_=pC[:]))
            for it in range(NT):
                se.dma("sync", big_f1[:, 0:D],
                       h_d[it * 128:(it + 1) * 128, win * D:(win + 1) * D])
                se.op("vector", lambda h, it=it: h.tensor_scalar(
                    out=big_f2[:, 0:D], in0=big_f1[:, 0:D],
                    scalar1=mucol[:, it:it + 1], scalar2=rstd2[:, it:it + 1],
                    op0=ALU.subtract, op1=ALU.mult))
                se.op("vector", lambda h: h.tensor_tensor(
                    out=big_f2[:, 0:D], in0=big_f2[:, 0:D], in1=gw[:, 0:D],
                    op=ALU.mult))
                se.op("vector", lambda h: h.tensor_tensor(
                    out=big_f2[:, 0:D], in0=big_f2[:, 0:D], in1=bw[:, 0:D],
                    op=ALU.add))
                se.op("vector", lambda h, it=it: h.tensor_scalar(
                    out=big_f2[:, 0:D], in0=big_f2[:, 0:D],
                    scalar1=comb2[:, it:it + 1], scalar2=None, op0=ALU.mult))
                ctx.emit_round(big_f2[:, 0:D], big_b[:, 0:D], big_i[:],
                               big_f3[:, 0:D], big_f1[:, D:2 * D])
                ev_prev = None
                base_t = se.last
                for half in range(2):
                    ph = pH if half == 0 else pH2
                    tt0 = None
                    for j in range(8):
                        k = half * 8 + j
                        tt0 = se.op("tensor", lambda h, j=j, k=k, ph=ph: h.matmul(
                            out=ph[:, j * 128:(j + 1) * 128],
                            lhsT=big_b[:, 128 * k:128 * (k + 1)],
                            rhs=ident_b[:], is_transpose=True,
                            start=(j == 0), stop=(j == 7)),
                            deps=[base_t] + ([ev_prev] if ev_prev else []))
                    ev_prev = se.op("vector", lambda h, half=half, ph=ph:
                                    h.tensor_copy(
                        out=hqT[:, half * 1024:(half + 1) * 1024],
                        in_=ph[:, 0:1024]), deps=[tt0])
                se.dma("sync",
                       hq_d[win * D:(win + 1) * D,
                            it * 128:(it + 1) * 128].rearrange(
                                "(k p) c -> p k c", p=128),
                       hqT[:].rearrange("p (k c) -> p k c", c=128),
                       deps=[ev_prev], close=True)

        for dch in range(4):
            for tp in range(2):
                nk = INNER // 128
                dtoks = {}
                mtoks = {}
                base = se.last
                for k in range(nk + 1):
                    if k < nk:
                        par = k % 2
                        deps = [base] if k < 2 else [mtoks[k - 2]]
                        d1 = se.dma("sync", big_b[:, par * 512:par * 512 + 512],
                                    w2t_d[k * 128:(k + 1) * 128,
                                          dch * 512:(dch + 1) * 512],
                                    deps=deps)
                        d2 = se.dma("sync",
                                    hqT[:, par * 512:par * 512 + 512],
                                    hq_d[k * 128:(k + 1) * 128,
                                         tp * 512:(tp + 1) * 512],
                                    deps=deps)
                        dtoks[k] = d2
                    if k >= 1:
                        kk = k - 1
                        par = kk % 2
                        for tt in range(4):
                            mtoks[kk] = se.op(
                                "tensor", lambda h, kk=kk, tt=tt, par=par: h.matmul(
                                    out=pds[tt][:],
                                    lhsT=hqT[:, par * 512 + tt * 128:
                                             par * 512 + (tt + 1) * 128],
                                    rhs=big_b[:, par * 512:par * 512 + 512],
                                    start=(kk == 0), stop=(kk == nk - 1)),
                                deps=[dtoks[kk]])
                se.op("vector", lambda h: h.engine_nop(), deps=[mtoks[nk - 1]],
                      close=True)
                for tt in range(4):
                    it = tp * 4 + tt
                    ob = big_f1[:, 0:512]
                    se.op("vector", lambda h, tt=tt, it=it, ob=ob: h.tensor_scalar(
                        out=ob, in0=pds[tt][:], scalar1=inv2c[:, it:it + 1],
                        scalar2=mclcol[:, 1:2], op0=ALU.mult, op1=ALU.mult))
                    se.op("vector", lambda h, dch=dch, ob=ob: h.tensor_tensor(
                        out=ob, in0=ob,
                        in1=b2r[:, dch * 512:(dch + 1) * 512], op=ALU.add))
                    if C_DEBUG or C_DEBUG_LITE:
                        se.dma("sync", f0dbg_out[it * 128:(it + 1) * 128,
                                                 dch * 512:(dch + 1) * 512], ob)
                    se.dma("sync", big_f2[:, 0:512],
                           x_in[it * 128:(it + 1) * 128,
                                dch * 512:(dch + 1) * 512])
                    se.op("vector", lambda h, ob=ob: h.tensor_tensor(
                        out=ob, in0=ob, in1=big_f2[:, 0:512], op=ALU.add))
                    se.dma("sync", big_f2[:, 0:512],
                           at_in[it * 128:(it + 1) * 128,
                                 dch * 512:(dch + 1) * 512])
                    se.op("vector", lambda h, ob=ob: h.tensor_tensor(
                        out=ob, in0=ob, in1=big_f2[:, 0:512], op=ALU.add))
                    se.dma("sync", x1_out[it * 128:(it + 1) * 128,
                                          dch * 512:(dch + 1) * 512], ob)

        if C_DEBUG:
            dbg = ctx.sbuf("dbg_sb", [128, 64], F32)
            se.op("vector", lambda h: h.tensor_copy(out=dbg[:, 0:2], in_=swcol[:]))
            se.op("vector", lambda h: h.tensor_copy(out=dbg[:, 2:4], in_=mclcol[:]))
            se.op("vector", lambda h: h.tensor_copy(out=dbg[:, 4:12], in_=mucol[:]))
            se.op("vector", lambda h: h.tensor_copy(out=dbg[:, 12:20], in_=rstd2[:]))
            se.op("vector", lambda h: h.tensor_copy(out=dbg[:, 20:28], in_=amax2[:]))
            se.op("vector", lambda h: h.tensor_copy(out=dbg[:, 28:36], in_=s2c[:]))
            se.op("vector", lambda h: h.tensor_copy(out=dbg[:, 36:44], in_=inv2c[:]))
            se.op("vector", lambda h: h.tensor_copy(out=dbg[:, 44:52], in_=invs[:]))
            se.op("vector", lambda h: h.tensor_copy(out=dbg[:, 52:60], in_=runsum[:]))
            se.dma("sync", dbg_out[:], dbg[:])
            for it in range(NT):
                se.dma("sync", hdbg_out[it * 128:(it + 1) * 128, :],
                       h_d[it * 128:(it + 1) * 128, :])
            for kk in range(INNER // 128):
                se.dma("sync", hqdbg_out[kk * 128:(kk + 1) * 128, :],
                       hq_d[kk * 128:(kk + 1) * 128, :])

        stats = big_f3[:, 0:24]
        mv = big_f3[:, 24:26]
        nmean = c1
        rstd = c2
        for it in range(NT):
            se.dma("sync", big_f1[:, 0:D], x1_out[it * 128:(it + 1) * 128, :])
            ctx.emit_lnstats(big_f1[:, 0:D], D, stats, mv, nmean, rstd)
            se.op("vector", lambda h: h.tensor_scalar(
                out=big_f2[:, 0:D], in0=big_f1[:, 0:D], scalar1=nmean[:, 0:1],
                scalar2=rstd[:, 0:1], op0=ALU.add, op1=ALU.mult))
            se.dma("sync", xn1_out[it * 128:(it + 1) * 128, :], big_f2[:, 0:D])

        se.finalize()
    return nc


def run_C(inputs, aux, prep):
    nc = build_C()
    xn0s = shard_rows(np.asarray(aux["xn0"]))
    xs = shard_rows(inputs["L0_attn_x"])
    ats = shard_rows(np.asarray(aux["attn0"]))
    in_maps = []
    for c in range(NCORES):
        in_maps.append({
            "xn0": xn0s[c], "x": xs[c], "attn0": ats[c],
            "w1t": prep["w1t"], "w2t": prep["w2t"],
            "b1": inputs["L0_ffn_b1"].reshape(1, -1),
            "lng": inputs["L0_ffn_ln_g"].reshape(1, -1),
            "lnb": inputs["L0_ffn_ln_b"].reshape(1, -1),
            "b2": inputs["L0_ffn_b2"].reshape(1, -1),
            "mcl2": np.asarray(aux["mcl"]).reshape(1, 2),
        })
    res = run_bass_kernel_spmd(nc, in_maps, core_ids=list(range(NCORES)))
    return {
        "x1": np.concatenate([res.results[c]["x1"] for c in range(NCORES)], 0),
        "xn1": np.concatenate([res.results[c]["xn1"] for c in range(NCORES)], 0),
    }



def build_D():
    nc = bass.Bass("TRN2", target_bir_lowering=False)
    xkv_in = nc.dram_tensor("xkv", [T, D], F32, kind="ExternalInput").ap()
    wq_in = nc.dram_tensor("Wq", [D, 256], F32, kind="ExternalInput").ap()
    wk_in = nc.dram_tensor("Wk", [D, 256], F32, kind="ExternalInput").ap()
    wv_in = nc.dram_tensor("Wv", [D, 256], F32, kind="ExternalInput").ap()
    wo_in = nc.dram_tensor("Wo", [256, D], F32, kind="ExternalInput").ap()
    wg_in = nc.dram_tensor("Wg", [D, D], F32, kind="ExternalInput").ap()
    ve0_in = nc.dram_tensor("ve0", [1, 256], F32, kind="ExternalInput").ap()
    ve1_in = nc.dram_tensor("ve1", [1, 256], F32, kind="ExternalInput").ap()
    km_in = nc.dram_tensor("kmask", [1, T], F32, kind="ExternalInput").ap()

    a_out = nc.dram_tensor("attn1", [RPC, D], F32, kind="ExternalOutput").ap()
    qp_d = nc.dram_tensor("qp_d", [16, 32 * RPC], BF16, kind="Internal").ap()
    kp_d = nc.dram_tensor("kp_d", [16, 32 * T], BF16, kind="Internal").ap()
    gate_d = nc.dram_tensor("gate_d", [RPC, D], F32, kind="Internal").ap()

    with ExitStack() as stack:
        ctx = Ctx(nc, stack)
        se = ctx.se
        idn_i = ctx.sbuf("idn_i", [128, 128], F32)
        ident_b = ctx.sbuf("ident_b", [128, 128], BF16)
        wq = ctx.sbuf("wq", [128, DK * 256], BF16)
        wk = ctx.sbuf("wk", [128, DK * 256], BF16)
        wv = ctx.sbuf("wv", [128, DK * 256], BF16)
        wo = ctx.sbuf("wo", [128, 2 * D], BF16)
        wgj = ctx.sbuf("wgj", [128, DK * 512], BF16)
        xt = ctx.sbuf("xt", [128, D], F32)
        xb = ctx.sbuf("xb", [128, D], BF16)
        fmt = ctx.sbuf("fmt", [128, DK * 128], BF16)
        fm_own = ctx.sbuf("fm_own", [128, NT * D], BF16)
        kq = ctx.sbuf("kq", [128, 256], F32)
        sa = ctx.sbuf("sa", [128, 256], BF16)
        sb_ = ctx.sbuf("sb_", [128, 256], BF16)
        trp = ctx.sbuf("trp", [128, 256], BF16)
        pv_tm = ctx.sbuf("pv_tm", [128, 16 * 256], BF16)
        kmr = ctx.sbuf("kmr", [128, T], F32)
        kmrow = ctx.sbuf("kmrow", [1, T], F32)
        verow = ctx.sbuf("verow", [1, 256], F32)
        ve0r = ctx.sbuf("ve0r", [128, 256], F32)
        dver = ctx.sbuf("dver", [128, 256], F32)
        qp_h = ctx.sbuf("qp_h", [16, RPC], BF16)
        kp_h = ctx.sbuf("kp_h", [16, T], BF16)
        t1b = ctx.sbuf("t1b", [128, 512], F32)
        getile = ctx.sbuf("getile", [128, 512], F32)
        srow = ctx.sbuf("srow", [128, T], F32)
        att_b = ctx.sbuf("att_b", [128, T], BF16)
        attT = ctx.sbuf("attT", [128, 1024], BF16)
        out1 = ctx.sbuf("out1", [128, NT * 256], F32)
        rs_r = ctx.sbuf("rs_r", [128, 1], F32)
        c1 = ctx.sbuf("c1", [128, 1], F32)
        ob = ctx.sbuf("ob", [128, 256], F32)
        obb = ctx.sbuf("obb", [128, 256], BF16)
        o_fm = ctx.sbuf("o_fm", [128, 256], BF16)
        gbuf = ctx.sbuf("gbuf", [128, 512], F32)
        att_f = ctx.sbuf("att_f", [128, 512], F32)

        pH = ctx.psum("pH", [128, 1024], BF16)
        pHd = ctx.psum("pHd", [128, 1024], BF16)
        pH2d = ctx.psum("pH2d", [128, 1024], BF16)
        pP = ctx.psum("pP", [128, 512], F32)
        pQ = ctx.psum("pQ", [128, 512], F32)
        pR = ctx.psum("pR", [128, 512], F32)

        ctx.emit_init()
        make_identity(ctx, idn_i, ident_b)

        se.dma("gpsimd", wq[:].rearrange("p (k f) -> p k f", f=256),
               wq_in.rearrange("(k p) f -> p k f", p=128))
        se.dma("gpsimd", wk[:].rearrange("p (k f) -> p k f", f=256),
               wk_in.rearrange("(k p) f -> p k f", p=128))
        se.dma("gpsimd", wv[:].rearrange("p (k f) -> p k f", f=256),
               wv_in.rearrange("(k p) f -> p k f", p=128))
        se.dma("gpsimd", wo[:].rearrange("p (k f) -> p k f", f=D),
               wo_in.rearrange("(k p) f -> p k f", p=128))
        se.dma("sync", kmrow[:], km_in[:])
        for j in range(4):
            se.op("tensor", lambda h, j=j: h.matmul(
                out=pP[:], lhsT=ctx.ones_row[:],
                rhs=kmrow[:, 512 * j:512 * (j + 1)], start=True, stop=True))
            se.op("vector", lambda h, j=j: h.tensor_copy(
                out=kmr[:, 512 * j:512 * (j + 1)], in_=pP[:]))
        se.dma("sync", verow[:], ve0_in[:])
        se.op("tensor", lambda h: h.matmul(
            out=pP[:, 0:256], lhsT=ctx.ones_row[:], rhs=verow[:],
            start=True, stop=True))
        se.op("vector", lambda h: h.tensor_copy(out=ve0r[:], in_=pP[:, 0:256]))
        se.dma("sync", verow[:], ve1_in[:])
        se.op("tensor", lambda h: h.matmul(
            out=pP[:, 0:256], lhsT=ctx.ones_row[:], rhs=verow[:],
            start=True, stop=True))
        se.op("vector", lambda h: h.tensor_copy(out=dver[:], in_=pP[:, 0:256]))
        se.op("vector", lambda h: h.tensor_tensor(
            out=dver[:], in0=dver[:], in1=ve0r[:], op=ALU.subtract))

        for st_ in range(16):
            se.dma("sync", xt[:], xkv_in[st_ * 128:(st_ + 1) * 128, :])
            cast_tok = se.op("vector", lambda h: h.tensor_copy(
                out=xb[:], in_=xt[:]), deps=[se.last], close=True)
            ev_prev = None
            for half in range(2):
                ph = pHd if half == 0 else pH2d
                tdeps = [cast_tok] + ([ev_prev] if ev_prev else [])
                tt0 = None
                for j in range(8):
                    k = half * 8 + j
                    tt0 = se.op("tensor", lambda h, j=j, k=k, ph=ph: h.matmul(
                        out=ph[:, j * 128:(j + 1) * 128],
                        lhsT=xb[:, 128 * k:128 * (k + 1)], rhs=ident_b[:],
                        is_transpose=True, start=(j == 0), stop=(j == 7)),
                        deps=tdeps)
                ev_prev = se.op("vector", lambda h, half=half, ph=ph:
                                h.tensor_copy(
                    out=fmt[:, half * 1024:(half + 1) * 1024],
                    in_=ph[:, 0:1024]), deps=[tt0])
            se.op("vector", lambda h: h.engine_nop(), deps=[ev_prev],
                  close=True)
            if st_ >= 8:
                se.op("vector", lambda h, st_=st_: h.tensor_copy(
                    out=fm_own[:, (st_ - 8) * D:(st_ - 7) * D], in_=fmt[:]))
            projs = [("k", wk, kp_d, T)]
            if st_ >= 8:
                projs.append(("q", wq, qp_d, RPC))
            for nm, wsb, dst, width in projs:
                for k in range(DK):
                    se.op("tensor", lambda h, k=k, wsb=wsb: h.matmul(
                        out=pP[:, 0:256], lhsT=fmt[:, 128 * k:128 * (k + 1)],
                        rhs=wsb[:, 256 * k:256 * (k + 1)],
                        start=(k == 0), stop=(k == DK - 1)))
                se.op("vector", lambda h: h.tensor_copy(out=kq[:], in_=pP[:, 0:256]))
                se.op("scalar", lambda h: h.activation(
                    out=sa[:], in_=kq[:], func=AF.Sigmoid))
                se.op("scalar", lambda h: h.activation(
                    out=sb_[:], in_=kq[:], func=AF.Sigmoid,
                    scale=ctx.const(-1.0)))
                soff = st_ * 128 if nm == "k" else (st_ - 8) * 128
                for sgn, src in ((0, sa), (8, sb_)):
                    for j in range(2):
                        se.op("tensor", lambda h, j=j, src=src: h.transpose(
                            out=pH[:, 0:128], in_=src[:, 128 * j:128 * (j + 1)],
                            identity=ident_b[:]))
                        se.op("vector", lambda h, j=j: h.tensor_copy(
                            out=trp[:, 128 * j:128 * (j + 1)], in_=pH[:, 0:128]))
                    for j in range(2):
                        dview = dst.rearrange("r (j2 g s) -> j2 g r s",
                                              j2=2, g=16)[
                            j, :, sgn:sgn + 8, soff:soff + 128]
                        se.dma("sync", dview,
                               trp[:, 128 * j:128 * (j + 1)])
            for k in range(DK):
                se.op("tensor", lambda h, k=k: h.matmul(
                    out=pP[:, 0:256], lhsT=fmt[:, 128 * k:128 * (k + 1)],
                    rhs=wv[:, 256 * k:256 * (k + 1)],
                    start=(k == 0), stop=(k == DK - 1)))
            se.op("scalar", lambda h, st_=st_: h.activation(
                out=pv_tm[:, 256 * st_:256 * (st_ + 1)], in_=pP[:, 0:256],
                func=AF.Sigmoid))

        for jc in range(4):
            se.dma("gpsimd", wgj[:].rearrange("p (k f) -> p k f", f=512),
                   wg_in[:, jc * 512:(jc + 1) * 512].rearrange(
                       "(k p) f -> p k f", p=128))
            for it in range(NT):
                for k in range(DK):
                    se.op("tensor", lambda h, k=k, it=it: h.matmul(
                        out=pQ[:], lhsT=fm_own[:, it * D + 128 * k:
                                               it * D + 128 * (k + 1)],
                        rhs=wgj[:, 512 * k:512 * (k + 1)],
                        start=(k == 0), stop=(k == DK - 1)))
                se.op("scalar", lambda h: h.activation(
                    out=gbuf[:], in_=pQ[:], func=AF.Sigmoid))
                se.dma("sync", gate_d[it * 128:(it + 1) * 128,
                                      jc * 512:(jc + 1) * 512], gbuf[:])

        se.op("gpsimd", lambda h: h.iota(
            t1b[:], pattern=[[-1, 512]], base=0, channel_multiplier=1,
            allow_small_or_imprecise_dtypes=True))

        for hh in range(32):
            se.dma("sync", qp_h[:], qp_d[:, hh * RPC:(hh + 1) * RPC])
            se.dma("sync", kp_h[:], kp_d[:, hh * T:(hh + 1) * T])
            amask_tok = None
            for qt in range(NT):
                t0g = 1024 + qt * 128
                chd = t0g // 512
                base0 = se.last
                d0g = float(t0g - chd * 512)
                a1 = se.op("vector", lambda h, d0g=d0g: h.tensor_scalar(
                    out=getile[:], in0=t1b[:], scalar1=d0g,
                    scalar2=0.0, op0=ALU.add, op1=ALU.is_ge), deps=[base0])
                a2 = se.op("vector", lambda h: h.tensor_scalar(
                    out=getile[:], in0=getile[:], scalar1=1e30,
                    scalar2=-1e30, op0=ALU.mult, op1=ALU.add), deps=[a1])
                amask_tok = se.op("vector", lambda h, s0=chd * 512:
                                  h.tensor_tensor(
                    out=getile[:], in0=getile[:],
                    in1=kmr[:, s0:s0 + 512], op=ALU.add), deps=[a2])
                ev_toks = []
                mm_tok = {}
                for ch in range(chd + 1):
                    s0 = ch * 512
                    pp = pP if ch % 2 == 0 else pQ
                    mdeps = [base0]
                    if ch >= 2:
                        mdeps = [ev_toks[ch - 2]]
                    mm_tok[ch] = se.op("tensor", lambda h, qt=qt, ch=ch,
                                       pp=pp: h.matmul(
                        out=pp[:], lhsT=qp_h[:, qt * 128:(qt + 1) * 128],
                        rhs=kp_h[:, ch * 512:(ch + 1) * 512],
                        start=True, stop=True), deps=mdeps)
                    if ch == chd:
                        ev = se.op("vector", lambda h, s0=s0, pp=pp:
                                   h.tensor_tensor(
                            out=srow[:, s0:s0 + 512], in0=pp[:],
                            in1=getile[:], op=ALU.add),
                            deps=[mm_tok[ch], amask_tok])
                    else:
                        ev = se.op("vector", lambda h, s0=s0, pp=pp:
                                   h.tensor_tensor(
                            out=srow[:, s0:s0 + 512], in0=pp[:],
                            in1=kmr[:, s0:s0 + 512], op=ALU.add),
                            deps=[mm_tok[ch]])
                    ev_toks.append(ev)
                se.op("vector", lambda h: h.engine_nop(), deps=[ev_toks[-1]],
                      close=True)
                used = (chd + 1) * 512
                se.op("scalar", lambda h, used=used: h.activation(
                    out=att_b[:, 0:used], in_=srow[:, 0:used], func=AF.Exp,
                    bias=ctx.const(-16.0), accum_out=rs_r[:]))
                se.op("vector", lambda h: h.reciprocal(out=rs_r[:], in_=rs_r[:]))
                nst = used // 128
                base = se.last
                ttoks = []
                etoks = {}
                mtoks = []
                for j in range(nst):
                    g = (j // 4) % 2
                    ph = pHd if g == 0 else pH2d
                    jj = j % 4
                    deps = [base]
                    if (j // 4) >= 2:
                        deps.append(etoks[(j // 4) - 2])
                    lastj = (jj == 3 or j == nst - 1)
                    t_tok = se.op("tensor", lambda h, j=j, jj=jj, ph=ph,
                                  lastj=lastj: h.matmul(
                        out=ph[:, jj * 128:(jj + 1) * 128],
                        lhsT=att_b[:, 128 * j:128 * (j + 1)],
                        rhs=ident_b[:], is_transpose=True,
                        start=(jj == 0), stop=lastj), deps=deps)
                    ttoks.append(t_tok)
                    if jj == 3 or j == nst - 1:
                        grp = j // 4
                        e_tok = se.op("vector", lambda h, g=g, ph=ph: h.tensor_copy(
                            out=attT[:, g * 512:(g + 1) * 512],
                            in_=ph[:, 0:512]),
                            deps=[t_tok])
                        etoks[grp] = e_tok
                        for j2 in range(grp * 4, min(grp * 4 + 4, nst)):
                            m_tok = se.op("tensor", lambda h, j2=j2, g=g,
                                          hh=hh, nst=nst: h.matmul(
                                out=pR[:, 0:8],
                                lhsT=attT[:, g * 512 + (j2 % 4) * 128:
                                          g * 512 + (j2 % 4 + 1) * 128],
                                rhs=pv_tm[:, j2 * 256 + hh * 8:
                                          j2 * 256 + hh * 8 + 8],
                                start=(j2 == 0), stop=(j2 == nst - 1)),
                                deps=[e_tok])
                            mtoks.append(m_tok)
                se.op("vector", lambda h, qt=qt, hh=hh: h.tensor_scalar(
                    out=out1[:, qt * 256 + hh * 8:qt * 256 + hh * 8 + 8],
                    in0=pR[:, 0:8], scalar1=rs_r[:, 0:1], scalar2=None,
                    op0=ALU.mult), deps=[mtoks[-1]], close=True)

        for qt in range(NT):
            se.op("vector", lambda h, qt=qt: h.tensor_tensor(
                out=ob[:], in0=out1[:, qt * 256:(qt + 1) * 256], in1=dver[:],
                op=ALU.mult))
            se.op("vector", lambda h: h.tensor_tensor(
                out=ob[:], in0=ob[:], in1=ve0r[:], op=ALU.add))
            se.op("vector", lambda h: h.tensor_copy(out=obb[:], in_=ob[:]))
            for j in range(2):
                se.op("tensor", lambda h, j=j: h.transpose(
                    out=pH[:, 0:128], in_=obb[:, 128 * j:128 * (j + 1)],
                    identity=ident_b[:]))
                se.op("vector", lambda h, j=j: h.tensor_copy(
                    out=o_fm[:, 128 * j:128 * (j + 1)], in_=pH[:, 0:128]))
            for jc in range(4):
                for k in range(2):
                    se.op("tensor", lambda h, jc=jc, k=k: h.matmul(
                        out=pQ[:], lhsT=o_fm[:, 128 * k:128 * (k + 1)],
                        rhs=wo[:, k * D + jc * 512:k * D + jc * 512 + 512],
                        start=(k == 0), stop=(k == 1)))
                se.dma("sync", gbuf[:], gate_d[qt * 128:(qt + 1) * 128,
                                               jc * 512:(jc + 1) * 512])
                se.op("vector", lambda h: h.tensor_tensor(
                    out=att_f[:], in0=pQ[:], in1=gbuf[:], op=ALU.mult))
                se.dma("sync", a_out[qt * 128:(qt + 1) * 128,
                                     jc * 512:(jc + 1) * 512], att_f[:])

        se.finalize()
    return nc


def run_D(inputs, aux):
    nc = build_D()
    xn1 = np.asarray(aux["xn1"]).reshape(B, T, D)
    in_maps = []
    for c in range(NCORES):
        b, half = c // 2, c % 2
        if half == 1:
            xkv = xn1[b]
            km = np.zeros((1, T), np.float32)
        else:
            xkv = np.concatenate([np.zeros_like(xn1[b][0:1024]),
                                  xn1[b][0:1024]], 0)
            km = np.zeros((1, T), np.float32)
            km[0, 0:1024] = -1e30
        in_maps.append({
            "xkv": np.ascontiguousarray(xkv), "kmask": km,
            "Wq": np.ascontiguousarray(inputs["L1_attn_Wq"]),
            "Wk": np.ascontiguousarray(inputs["L1_attn_Wk"]),
            "Wv": np.ascontiguousarray(inputs["L1_attn_Wv"]),
            "Wo": np.ascontiguousarray(inputs["L1_attn_Wo"]),
            "Wg": np.ascontiguousarray(inputs["L1_attn_Wg"]),
            "ve0": inputs["L1_attn_v_emb0"].reshape(1, -1),
            "ve1": inputs["L1_attn_v_emb1"].reshape(1, -1),
        })
    res = run_bass_kernel_spmd(nc, in_maps, core_ids=list(range(NCORES)))
    return np.concatenate([res.results[c]["attn1"] for c in range(NCORES)], 0)



def emit_mx_quant(ctx, t_ap, out_ap, W, m1, m2, m3, m4, m5, mi, bmax, sb, rs):
    se = ctx.se
    nb = W // 32
    se.op("vector", lambda h: h.reduce_max(
        out=bmax, in_=t_ap.rearrange("p (w k) -> p w k", k=32), axis=AX.X,
        apply_absolute_value=True))
    se.op("vector", lambda h: h.tensor_scalar(
        out=sb, in0=bmax, scalar1=1.0 / 448.0, scalar2=1e-30,
        op0=ALU.mult, op1=ALU.max))
    se.op("vector", lambda h: h.reciprocal(out=rs, in_=sb))
    t3 = t_ap.rearrange("p (w k) -> p w k", k=32)
    rs3 = rs.rearrange("p (w k) -> p w k", k=1)
    m13 = m1.rearrange("p (w k) -> p w k", k=32)
    a3b, b3b = bass.broadcast_tensor_aps(t3, rs3)
    se.op("vector", lambda h: h.tensor_tensor(out=m13, in0=a3b, in1=b3b,
                                              op=ALU.mult))
    se.op("scalar", lambda h: h.activation(out=m2, in_=m1, func=AF.Square))
    se.op("vector", lambda h: h.tensor_scalar_max(out=m2, in0=m2,
                                                  scalar1=1e-38))
    se.op("scalar", lambda h: h.activation(out=m2, in_=m2, func=AF.Ln))
    se.op("vector", lambda h: h.tensor_scalar(
        out=m2, in0=m2, scalar1=0.5 / LN2, scalar2=20.0,
        op0=ALU.mult, op1=ALU.add))
    se.op("vector", lambda h: h.tensor_copy(out=mi, in_=m2))
    se.op("vector", lambda h: h.tensor_tensor(out=m3, in0=m2, in1=mi,
                                              op=ALU.subtract))
    se.op("vector", lambda h: h.tensor_scalar(
        out=m3, in0=m3, scalar1=0.0, scalar2=None, op0=ALU.is_lt))
    se.op("vector", lambda h: h.scalar_tensor_tensor(
        out=m2, in0=m3, scalar=-1.0, in1=mi, op0=ALU.mult, op1=ALU.add))
    se.op("vector", lambda h: h.tensor_scalar_max(out=m2, in0=m2,
                                                  scalar1=14.0))
    se.op("scalar", lambda h: h.activation(
        out=m3, in_=m2, func=AF.Exp, scale=ctx.const(-LN2),
        bias=ctx.const(23.0 * LN2)))
    se.op("vector", lambda h: h.tensor_tensor(out=m1, in0=m1, in1=m3,
                                              op=ALU.mult))
    ctx.emit_round(m1, m5, mi, m4, m3)
    se.op("scalar", lambda h: h.activation(
        out=m3, in_=m2, func=AF.Exp, scale=ctx.const(LN2),
        bias=ctx.const(-23.0 * LN2)))
    se.op("vector", lambda h: h.tensor_tensor(out=m1, in0=m5, in1=m3,
                                              op=ALU.mult))
    m13b = m1.rearrange("p (w k) -> p w k", k=32)
    sb3 = sb.rearrange("p (w k) -> p w k", k=1)
    o3 = out_ap.rearrange("p (w k) -> p w k", k=32)
    a3c, b3c = bass.broadcast_tensor_aps(m13b, sb3)
    se.op("vector", lambda h: h.tensor_tensor(out=o3, in0=a3c, in1=b3c,
                                              op=ALU.mult))


def build_E():
    nc = bass.Bass("TRN2", target_bir_lowering=False)
    xn1_in = nc.dram_tensor("xn1", [RPC, D], F32, kind="ExternalInput").ap()
    x1_in = nc.dram_tensor("x1", [RPC, D], F32, kind="ExternalInput").ap()
    a1_in = nc.dram_tensor("attn1", [RPC, D], F32, kind="ExternalInput").ap()
    gq_t = nc.dram_tensor("gq_t", [D, INNER], BF16, kind="ExternalInput").ap()
    uq_t = nc.dram_tensor("uq_t", [D, INNER], BF16, kind="ExternalInput").ap()
    dq_t = nc.dram_tensor("dq_t", [INNER, D], BF16, kind="ExternalInput").ap()

    out_o = nc.dram_tensor("out", [RPC, D], F32, kind="ExternalOutput").ap()
    hq_d2 = nc.dram_tensor("hq_d2", [INNER, RPC], BF16, kind="Internal").ap()

    with ExitStack() as stack:
        ctx = Ctx(nc, stack)
        se = ctx.se
        idn_i = ctx.sbuf("idn_i", [128, 128], F32)
        ident_b = ctx.sbuf("ident_b", [128, 128], BF16)
        xq_fm = ctx.sbuf("xq_fm", [128, DK * RPC], BF16)
        wld = ctx.sbuf("wld", [128, INNER], F32)
        m1 = ctx.sbuf("m1", [128, D], F32)
        m2 = ctx.sbuf("m2", [128, D], F32)
        m3 = ctx.sbuf("m3", [128, D], F32)
        m4 = ctx.sbuf("m4", [128, D], F32)
        m5 = ctx.sbuf("m5", [128, D], F32)
        mi = ctx.sbuf("mi", [128, D], I32)
        bmax = ctx.sbuf("bmax", [128, 64], F32)
        sbs = ctx.sbuf("sbs", [128, 64], F32)
        rss = ctx.sbuf("rss", [128, 64], F32)
        qb = ctx.sbuf("qb", [128, D], BF16)
        grhs = ctx.sbuf("grhs", [128, INNER], BF16)
        urhs = ctx.sbuf("urhs", [128, INNER], BF16)
        hbuf = ctx.sbuf("hbuf", [128, 512], F32)
        hbuf4 = ctx.sbuf("hbuf4", [128, 2048], F32)
        qb4 = ctx.sbuf("qb4", [128, 2048], BF16)
        stg16 = ctx.sbuf("stg16", [128, 2048], BF16)
        hqv = ctx.sbuf("hqv", [128, 512], BF16)
        stg4 = ctx.sbuf("stg4", [128, 512], BF16)
        hql2 = ctx.sbuf("hql2", [128, 1024], BF16)
        resb = ctx.sbuf("resb", [128, 512], F32)
        ob = ctx.sbuf("ob", [128, 512], F32)

        pA = ctx.psum("pA", [128, 512], F32)
        pB = ctx.psum("pB", [128, 512], F32)
        pC = ctx.psum("pC", [128, 512], F32)
        pD = ctx.psum("pD", [128, 512], F32)
        pH = ctx.psum("pH", [128, 1024], BF16)
        pH2 = ctx.psum("pH2", [128, 1024], BF16)
        pds = [pA, pB, pC, pD]

        ctx.emit_init()
        make_identity(ctx, idn_i, ident_b)

        for it in range(NT):
            se.dma("sync", m1[:], xn1_in[it * 128:(it + 1) * 128, :])
            se.op("vector", lambda h: h.tensor_copy(out=m4[:], in_=m1[:]))
            emit_mx_quant(ctx, m4[:], qb[:], D, m1[:], m2[:], m3[:], m4[:],
                          m5[:], mi[:], bmax[:], sbs[:], rss[:])
            for k in range(DK):
                se.op("tensor", lambda h, k=k: h.transpose(
                    out=pH[:, 0:128], in_=qb[:, 128 * k:128 * (k + 1)],
                    identity=ident_b[:]))
                se.op("vector", lambda h, k=k, it=it: h.tensor_copy(
                    out=xq_fm[:, k * RPC + it * 128:k * RPC + (it + 1) * 128],
                    in_=pH[:, 0:128]))

        free_tok = None
        tail_tok = None
        store_tok = None
        mm_last = None
        W = 4 * 512
        nb = W // 32
        for wg in range(16):
            rdeps = [se.last] if wg == 0 else [mm_last]
            se.dma("sync", grhs[:].rearrange("p (k c) -> p k c", c=512),
                   gq_t.rearrange("(k p) c -> p k c", p=128)[
                       :, :, wg * 512:(wg + 1) * 512], deps=rdeps)
            rhs_tok = se.dma("sync", urhs[:].rearrange("p (k c) -> p k c", c=512),
                             uq_t.rearrange("(k p) c -> p k c", p=128)[
                                 :, :, wg * 512:(wg + 1) * 512], deps=rdeps)
            for g4 in range(2):
                hm_tok = None
                for i4 in range(4):
                    it = g4 * 4 + i4
                    mdeps = [rhs_tok] + ([free_tok] if free_tok else [])
                    for k in range(DK):
                        se.op("tensor", lambda h, k=k, it=it: h.matmul(
                            out=pA[:], lhsT=xq_fm[:, k * RPC + it * 128:
                                                  k * RPC + (it + 1) * 128],
                            rhs=grhs[:, 512 * k:512 * (k + 1)],
                            start=(k == 0), stop=(k == DK - 1)), deps=mdeps)
                    for k in range(DK):
                        mm_last = se.op("tensor", lambda h, k=k, it=it: h.matmul(
                            out=pB[:], lhsT=xq_fm[:, k * RPC + it * 128:
                                                  k * RPC + (it + 1) * 128],
                            rhs=urhs[:, 512 * k:512 * (k + 1)],
                            start=(k == 0), stop=(k == DK - 1)), deps=mdeps)
                    sl = slice(i4 * 512, (i4 + 1) * 512)
                    cdeps = [mm_last] + ([tail_tok] if tail_tok else [])
                    t = se.op("scalar", lambda h, sl=sl: h.activation(
                        out=hbuf4[:, sl], in_=pA[:], func=AF.Exp,
                        scale=ctx.const(-1.0)), deps=cdeps)
                    t = se.op("vector", lambda h, sl=sl: h.tensor_scalar(
                        out=hbuf4[:, sl], in0=hbuf4[:, sl], scalar1=1.0,
                        scalar2=None, op0=ALU.add), deps=[t])
                    t = se.op("vector", lambda h, sl=sl: h.reciprocal(
                        out=hbuf4[:, sl], in_=hbuf4[:, sl]), deps=[t])
                    t = se.op("vector", lambda h, sl=sl: h.tensor_tensor(
                        out=hbuf4[:, sl], in0=hbuf4[:, sl], in1=pA[:],
                        op=ALU.mult), deps=[t])
                    free_tok = t = se.op("vector", lambda h, sl=sl: h.tensor_tensor(
                        out=hbuf4[:, sl], in0=hbuf4[:, sl], in1=pB[:],
                        op=ALU.mult), deps=[t])
                    hm_tok = t
                t = hm_tok
                t = se.op("vector", lambda h: h.reduce_max(
                    out=bmax[:, 0:nb],
                    in_=hbuf4[:].rearrange("p (w k) -> p w k", k=32),
                    axis=AX.X, apply_absolute_value=True), deps=[t])
                t = se.op("vector", lambda h: h.tensor_scalar(
                    out=sbs[:, 0:nb], in0=bmax[:, 0:nb], scalar1=1.0 / 448.0,
                    scalar2=1e-30, op0=ALU.mult, op1=ALU.max), deps=[t])
                t = se.op("vector", lambda h: h.reciprocal(
                    out=rss[:, 0:nb], in_=sbs[:, 0:nb]), deps=[t])
                t3 = hbuf4[:].rearrange("p (w k) -> p w k", k=32)
                rs3 = rss[:, 0:nb].rearrange("p (w k) -> p w k", k=1)
                m13 = m1[:].rearrange("p (w k) -> p w k", k=32)
                a3b, b3b = bass.broadcast_tensor_aps(t3, rs3)
                t = se.op("vector", lambda h: h.tensor_tensor(
                    out=m13, in0=a3b, in1=b3b, op=ALU.mult), deps=[t])
                t = se.op("scalar", lambda h: h.activation(
                    out=m2[:], in_=m1[:], func=AF.Square), deps=[t])
                t = se.op("vector", lambda h: h.tensor_scalar_max(
                    out=m2[:], in0=m2[:], scalar1=1e-38), deps=[t])
                t = se.op("scalar", lambda h: h.activation(
                    out=m2[:], in_=m2[:], func=AF.Ln), deps=[t])
                t = se.op("vector", lambda h: h.tensor_scalar(
                    out=m2[:], in0=m2[:], scalar1=0.5 / LN2, scalar2=20.0,
                    op0=ALU.mult, op1=ALU.add), deps=[t])
                t = se.op("vector", lambda h: h.tensor_copy(
                    out=mi[:], in_=m2[:]), deps=[t])
                t = se.op("vector", lambda h: h.tensor_tensor(
                    out=m3[:], in0=m2[:], in1=mi[:], op=ALU.subtract), deps=[t])
                t = se.op("vector", lambda h: h.tensor_scalar(
                    out=m3[:], in0=m3[:], scalar1=0.0, scalar2=None,
                    op0=ALU.is_lt), deps=[t])
                t = se.op("vector", lambda h: h.scalar_tensor_tensor(
                    out=m2[:], in0=m3[:], scalar=-1.0, in1=mi[:],
                    op0=ALU.mult, op1=ALU.add), deps=[t])
                t = se.op("vector", lambda h: h.tensor_scalar_max(
                    out=m2[:], in0=m2[:], scalar1=14.0), deps=[t])
                t = se.op("scalar", lambda h: h.activation(
                    out=m3[:], in_=m2[:], func=AF.Exp, scale=ctx.const(-LN2),
                    bias=ctx.const(23.0 * LN2)), deps=[t])
                t = se.op("vector", lambda h: h.tensor_tensor(
                    out=m1[:], in0=m1[:], in1=m3[:], op=ALU.mult), deps=[t])
                t = se.op("vector", lambda h: h.tensor_copy(
                    out=mi[:], in_=m1[:]), deps=[t])
                t = se.op("vector", lambda h: h.tensor_tensor(
                    out=m4[:], in0=m1[:], in1=mi[:], op=ALU.subtract), deps=[t])
                t = se.op("vector", lambda h: h.tensor_scalar(
                    out=m3[:], in0=m4[:], scalar1=-0.5, scalar2=None,
                    op0=ALU.is_lt), deps=[t])
                t = se.op("vector", lambda h: h.scalar_tensor_tensor(
                    out=m4[:], in0=m4[:], scalar=0.5, in1=m3[:],
                    op0=ALU.is_gt, op1=ALU.subtract), deps=[t])
                t = se.op("vector", lambda h: h.tensor_tensor(
                    out=m5[:], in0=mi[:], in1=m4[:], op=ALU.add), deps=[t])
                t = se.op("scalar", lambda h: h.activation(
                    out=m3[:], in_=m2[:], func=AF.Exp, scale=ctx.const(LN2),
                    bias=ctx.const(-23.0 * LN2)), deps=[t])
                t = se.op("vector", lambda h: h.tensor_tensor(
                    out=m1[:], in0=m5[:], in1=m3[:], op=ALU.mult), deps=[t])
                m13b = m1[:].rearrange("p (w k) -> p w k", k=32)
                sb3 = sbs[:, 0:nb].rearrange("p (w k) -> p w k", k=1)
                o3 = qb4[:].rearrange("p (w k) -> p w k", k=32)
                a3c, b3c = bass.broadcast_tensor_aps(m13b, sb3)
                t = se.op("vector", lambda h: h.tensor_tensor(
                    out=o3, in0=a3c, in1=b3c, op=ALU.mult), deps=[t])
                tail_tok = t
                ttoks = []
                for half in range(2):
                    ph = pH if half == 0 else pH2
                    tt0 = None
                    for j in range(8):
                        idx = half * 8 + j
                        it4, kk4 = divmod(idx, 4)
                        src = qb4[:, it4 * 512 + kk4 * 128:
                                  it4 * 512 + (kk4 + 1) * 128]
                        prevev = [ttoks[-2]] if len(ttoks) >= 2 else []
                        tt0 = se.op("tensor", lambda h, j=j, src=src, ph=ph:
                                    h.matmul(
                            out=ph[:, j * 128:(j + 1) * 128], lhsT=src,
                            rhs=ident_b[:], is_transpose=True,
                            start=(j == 0), stop=(j == 7)),
                            deps=[t] + prevev)
                    evd = [tt0] + ([store_tok] if store_tok else [])
                    ev = se.op("vector", lambda h, half=half, ph=ph:
                               h.tensor_copy(
                        out=stg16[:, half * 1024:(half + 1) * 1024],
                        in_=ph[:, 0:1024]), deps=evd)
                    ttoks.append(ev)
                for i4 in range(4):
                    itg = g4 * 4 + i4
                    store_tok = se.dma(
                        "sync",
                        hq_d2[wg * 512:(wg + 1) * 512,
                              itg * 128:(itg + 1) * 128].rearrange(
                            "(k p) c -> p k c", p=128),
                        stg16[:, i4 * 512:(i4 + 1) * 512].rearrange(
                            "p (k c) -> p k c", c=128),
                        deps=ttoks)
        se.op("vector", lambda h: h.engine_nop(), deps=se.fence(), close=True)

        for dch in range(4):
            for tp in range(2):
                nk = INNER // 128
                dtoks = {}
                mtoks = {}
                base = se.last
                for k in range(nk + 1):
                    if k < nk:
                        par = k % 2
                        deps = [base] if k < 2 else [mtoks[k - 2]]
                        se.dma("sync", qb[:, par * 512:par * 512 + 512],
                               dq_t[k * 128:(k + 1) * 128,
                                    dch * 512:(dch + 1) * 512], deps=deps)
                        dtoks[k] = se.dma(
                            "sync", hql2[:, par * 512:par * 512 + 512],
                            hq_d2[k * 128:(k + 1) * 128,
                                  tp * 512:(tp + 1) * 512], deps=deps)
                    if k >= 1:
                        kk = k - 1
                        par = kk % 2
                        for tt in range(4):
                            mtoks[kk] = se.op(
                                "tensor", lambda h, kk=kk, tt=tt, par=par: h.matmul(
                                    out=pds[tt][:],
                                    lhsT=hql2[:, par * 512 + tt * 128:
                                              par * 512 + (tt + 1) * 128],
                                    rhs=qb[:, par * 512:par * 512 + 512],
                                    start=(kk == 0), stop=(kk == nk - 1)),
                                deps=[dtoks[kk]])
                se.op("vector", lambda h: h.engine_nop(), deps=[mtoks[nk - 1]],
                      close=True)
                for tt in range(4):
                    it = tp * 4 + tt
                    se.dma("sync", resb[:],
                           x1_in[it * 128:(it + 1) * 128,
                                 dch * 512:(dch + 1) * 512])
                    se.op("vector", lambda h, tt=tt: h.tensor_tensor(
                        out=ob[:], in0=pds[tt][:], in1=resb[:], op=ALU.add))
                    se.dma("sync", resb[:],
                           a1_in[it * 128:(it + 1) * 128,
                                 dch * 512:(dch + 1) * 512])
                    se.op("vector", lambda h: h.tensor_tensor(
                        out=ob[:], in0=ob[:], in1=resb[:], op=ALU.add))
                    se.dma("sync", out_o[it * 128:(it + 1) * 128,
                                         dch * 512:(dch + 1) * 512], ob[:])

        se.finalize()
    return nc


def run_E(inputs, aux, prep):
    nc = build_E()
    xn1s = shard_rows(np.asarray(aux["xn1"]))
    x1s = shard_rows(np.asarray(aux["x1"]))
    a1s = shard_rows(np.asarray(aux["attn1"]))
    in_maps = []
    for c in range(NCORES):
        in_maps.append({
            "xn1": xn1s[c], "x1": x1s[c], "attn1": a1s[c],
            "gq_t": prep["gq"], "uq_t": prep["uq"], "dq_t": prep["dq"],
        })
    res = run_bass_kernel_spmd(nc, in_maps, core_ids=list(range(NCORES)))
    return np.concatenate([res.results[c]["out"] for c in range(NCORES)], 0)


def kernel(**inputs):
    aux = run_A(inputs)
    prep = run_P(inputs, aux)
    aux["attn0"] = run_B(inputs, aux)
    auxC = run_C(inputs, aux, prep)
    aux.update(auxC)
    aux["attn1"] = run_D(inputs, aux)
    out = run_E(inputs, aux, prep)
    return out.reshape(B, T, D)
